# version 08
# speedup vs baseline: 2.1878x; 1.1538x over previous
import numpy as np
import ml_dtypes
from contextlib import ExitStack

import concourse.bass as bass
import concourse.mybir as mybir
from concourse.bass_utils import run_bass_kernel_spmd
from concourse.alu_op_type import AluOpType as ALU

F32 = mybir.dt.float32
BF16 = mybir.dt.bfloat16
I32 = mybir.dt.int32
AF = mybir.ActivationFunctionType
AX = mybir.AxisListType
bfdt = ml_dtypes.bfloat16

D = 1024
T = 2048
DI = 2048
NH = 32
DS = 64
HD = 64
KC = 8
Q = 128
PROJ_PAD = 4608
N_CORES = 8

ENGS = ("sp", "act", "dve", "pool", "pe")
_ENG_ATTR = {"sp": "sync", "act": "scalar", "dve": "vector", "pool": "gpsimd", "pe": "tensor"}


class _RecInstr:
    def then_inc(self, sem, n):
        return None


class _Rec:

    def __init__(self):
        self.w = []
        self.r = []
        self.calls = []

    def __getattr__(self, nm):
        rec = self

        def call(*a, **kw):
            rec.calls.append((nm, a))
            aps = [v for v in list(a) + list(kw.values()) if _apish(v)]
            if aps:
                rec.w.append(aps[0])
                rec.r.extend(aps[1:])
            return _RecInstr()

        return call


def _apish(v):
    cn = type(v).__name__
    return cn == "AP" or cn.endswith("TensorHandle")


_TSET = {"Exp": "nle", "Ln": "nle", "Softplus": "nle",
         "Silu": "silu", "Gelu": "gelu", "Sigmoid": "sigmoid"}


def _ap_rect(x):
    cn = type(x).__name__
    if cn.endswith("TensorHandle"):
        return (x.name, 0, 1 << 30, 0, 1 << 30)
    t = x.tensor
    esz = mybir.dt.size(x.dtype)
    tsz = mybir.dt.size(t.dtype)
    wb = tsz
    for s in list(t.shape)[1:]:
        wb *= int(s)
    off_b = x.offset * esz
    plo, flo = off_b // wb, off_b % wb
    pspan = fspan = 0
    for stride, n in x.ap:
        ext = (int(n) - 1) * int(stride) * esz
        if ext == 0:
            continue
        st_b = int(stride) * esz
        if ext < 0:
            return (x.name, 0, 1 << 30, 0, 1 << 30)
        if st_b % wb == 0:
            pspan += ext // wb
        elif st_b < wb:
            fspan += ext
        else:
            return (x.name, 0, 1 << 30, 0, 1 << 30)
    if flo + fspan + esz > wb:
        return (x.name, plo, plo + pspan + (flo + fspan) // wb + 1, 0, wb)
    return (x.name, plo, plo + pspan + 1, flo, flo + fspan + esz)


class PB:

    def __init__(self, sim_safe=False, serial=False, relax=()):
        self.nc = bass.Bass("TRN2", target_bir_lowering=False)
        self.sim_safe = sim_safe
        self.serial = serial
        self.relax = frozenset(relax)
        self.st = ExitStack()
        self.scopes = []
        self.ops = {e: [] for e in ENGS}
        self.sems = {}
        self.cnt = {}
        self._acc = {e: {} for e in ("act", "dve", "pool")}
        self._bar = {}
        self._tbl = None
        self.strict_rng = {}
        for s in ("pe", "act", "dve", "pool"):
            self.mksem(s)

    def mksem(self, s):
        if s not in self.sems:
            self.sems[s] = self.st.enter_context(self.nc.semaphore(s))
            self.cnt[s] = 0

    def push_scope(self):
        self.scopes.append(ExitStack())

    def pop_scope(self):
        self.scopes.pop().close()

    def _stack(self):
        return self.scopes[-1] if getattr(self, "scopes", None) else self.st

    def sbuf(self, name, shape, dtype):
        name = getattr(self, "name_prefix", "") + name
        return self._stack().enter_context(self.nc.sbuf_tensor(name, shape, dtype))

    def psum(self, name, shape, dtype):
        name = getattr(self, "name_prefix", "") + name
        return self._stack().enter_context(self.nc.psum_tensor(name, shape, dtype))

    def fence(self, sp=True):
        w = [(s, self.cnt[s]) for s in list(self.cnt) if self.cnt[s] > 0]
        self.op("pe", lambda e: e.nop(), w)
        self.op("act", lambda e: e.nop(), w)
        self.op("dve", lambda e: e.engine_nop(), w)
        self.op("pool", lambda e: e.engine_nop(), w)
        if sp:
            self.op("sp", lambda e: e.nop(), w)
        self._bar = dict(self.cnt)
        for acc in self._acc.values():
            for nm in list(acc):
                kept = [e for e in acc[nm] if e[1] > self._bar.get(e[0], 0)]
                if kept:
                    acc[nm] = kept
                else:
                    del acc[nm]

    def dram(self, name, shape, dtype, kind=None):
        if kind is None:
            return self.nc.dram_tensor(name, shape, dtype).ap()
        return self.nc.dram_tensor(name, shape, dtype, kind=kind).ap()

    def op(self, eng, fn, waits=(), inc=None, n=1, track=True):
        waits = tuple(waits)
        if self.serial and eng in ("pe", "act", "dve", "pool"):
            if self.serial is True:
                tgt = ("pe", "act", "dve", "pool")
            else:
                tgt = self.serial if eng == "pe" else ()
            waits = waits + tuple((s, self.cnt[s]) for s in tgt if s != eng)
            if inc is None and (self.serial is True or eng in ("act", "dve", "pool")):
                inc = eng
        rects = None
        if self.sim_safe and eng in ("pe", "act", "dve", "pool"):
            rng = self.strict_rng.get(eng)
            if eng not in self.relax or (
                    rng is not None and rng[0] <= self.cnt[eng] < rng[1]):
                waits = waits + ((eng, self.cnt[eng]),)
            elif eng != "pe":
                try:
                    rec = _Rec()
                    fn(rec)
                    rects = ([(True, _ap_rect(a)) for a in rec.w]
                             + [(False, _ap_rect(a)) for a in rec.r])
                except Exception:
                    rects = None
                    waits = waits + ((eng, self.cnt[eng]),)
                else:
                    need = {}
                    acc = self._acc[eng]
                    for isw, (nm, pl, ph, fl, fh) in rects:
                        for (sem, c, oisw, opl, oph, ofl, ofh) in acc.get(nm, ()):
                            if ((isw or oisw) and pl < oph and opl < ph
                                    and fl < ofh and ofl < fh):
                                if c > need.get(sem, -1):
                                    need[sem] = c
                    if eng == "act":
                        for nm_c, a_c in rec.calls:
                            if nm_c != "activation" or len(a_c) < 3:
                                continue
                            f = str(a_c[2]).rsplit(".", 1)[-1]
                            ts = _TSET.get(f)
                            if ts is not None and ts != self._tbl:
                                self._tbl = ts
                                if self.cnt[eng] > need.get(eng, -1):
                                    need[eng] = self.cnt[eng]
                    for sem, c in sorted(need.items()):
                        if c > self._bar.get(sem, 0):
                            waits = waits + ((sem, c),)
            if inc is None:
                inc = eng
        v = None
        if inc is not None:
            self.cnt[inc] += n
            v = self.cnt[inc]
        if rects is not None:
            al = self._acc[eng]
            for isw, rect in rects:
                al.setdefault(rect[0], []).append((inc, v, isw) + rect[1:])
        self.ops[eng].append((waits, fn, inc, n))
        return v

    def dma(self, eng, dst, src, waits=(), sem=None, **kw):
        if sem is None:
            sem = "d_" + eng
        self.mksem(sem)
        return self.op(eng, lambda e: e.dma_start(dst, src, **kw), waits, sem, 16)

    def build(self):
        blk = self.st.enter_context(self.nc.Block())
        for e in ENGS:
            ops = self.ops[e]

            def body(eng, ops=ops):
                lastw = {}
                for waits, fn, inc, n in ops:
                    for s, v in waits:
                        if v > 0 and lastw.get(s, 0) < v:
                            eng.wait_ge(self.sems[s], v)
                            lastw[s] = v
                    instr = fn(eng)
                    if inc is not None:
                        assert instr is not None
                        instr.then_inc(self.sems[inc], n)

            getattr(blk, _ENG_ATTR[e])(body)
        self.st.close()
        return self.nc


class RB:

    def __init__(self, n):
        self.rel = [("pe", 0)] * n
        self.i = 0

    def acquire(self):
        k = self.i % len(self.rel)
        self.i += 1
        sem, v = self.rel[k]
        return k, [(sem, v)]

    def release(self, k, sem, v):
        self.rel[k] = (sem, v)



def emit_ln_stats(P, xr, t0, tile, xqb, rb_xq, statr, statb, onesb, ones1b,
                  pk_sum, pk_sq, pk_m, pk_r, bank_rel, pk_s=None, extra_waits=(),
                  rep_extra=()):
    mmS = mmQ = None
    for c in range(KC):
        src = xr[:, c * T + t0: c * T + t0 + tile]
        kb, w = rb_xq.acquire()
        a1 = P.op("act", lambda e, s=src, d=xqb[kb]: e.activation(d[:, 0:tile], s, AF.Copy),
                  list(w) + list(extra_waits), "act")
        mmS = P.op("pe", lambda e, d=xqb[kb], c=c: e.matmul(
            pk_sum[0:1, 0:tile], onesb[:], d[:, 0:tile],
            start=(c == 0), stop=(c == KC - 1)),
            [("act", a1)] + ([bank_rel["sum"]] if c == 0 else []), "pe")
        kb2, w2 = rb_xq.acquire()
        a2 = P.op("act", lambda e, s=src, d=xqb[kb2]: e.activation(d[:, 0:tile], s, AF.Square),
                  list(w2), "act")
        mmQ = P.op("pe", lambda e, d=xqb[kb2], c=c: e.matmul(
            pk_sq[0:1, 0:tile], onesb[:], d[:, 0:tile],
            start=(c == 0), stop=(c == KC - 1)),
            [("act", a2)] + ([bank_rel["sq"]] if c == 0 else []), "pe")
        rb_xq.release(kb, "pe", mmS)
        rb_xq.release(kb2, "pe", mmQ)

    def sl(i):
        return statr[0:1, i * tile:(i + 1) * tile]

    inv = 1.0 / D
    d1 = P.op("dve", lambda e: e.tensor_scalar(sl(0), pk_sum[0:1, 0:tile], inv, None,
                                               ALU.mult, ALU.bypass), [("pe", mmS)], "dve")
    d2 = P.op("dve", lambda e: e.tensor_scalar(sl(1), pk_sq[0:1, 0:tile], inv, None,
                                               ALU.mult, ALU.bypass), [("pe", mmQ)], "dve")
    bank_rel["sum"] = ("dve", d1)
    bank_rel["sq"] = ("dve", d2)
    P.op("dve", lambda e: e.tensor_tensor(sl(2), sl(0), sl(0), ALU.mult))
    d4 = P.op("dve", lambda e: e.tensor_tensor(sl(1), sl(1), sl(2), ALU.subtract),
              inc="dve")
    d4b = P.op("dve", lambda e: e.tensor_scalar(sl(2), sl(1), 1e-5, None,
                                                ALU.add, ALU.bypass), [("dve", d4)], "dve")
    a3 = P.op("act", lambda e: e.activation(sl(2), sl(2), AF.Ln),
              [("dve", d4b)], "act")
    a4 = P.op("act", lambda e: e.activation(sl(3), sl(2), AF.Exp, scale=-0.5),
              [("act", a3)], "act")
    have_s = pk_s is not None
    if have_s:
        d5 = P.op("dve", lambda e: e.tensor_scalar(sl(4), sl(1), 1e-5, None,
                                                   ALU.add, ALU.bypass), [("dve", d4)], "dve")
        P.op("dve", lambda e: e.reciprocal(sl(4), sl(4)))
        P.op("dve", lambda e: e.tensor_tensor(sl(4), sl(1), sl(4), ALU.mult))
        d6 = P.op("dve", lambda e: e.tensor_scalar(sl(4), sl(4), 1e-6, None,
                                                   ALU.add, ALU.bypass), inc="dve")
        a5 = P.op("act", lambda e: e.activation(sl(4), sl(4), AF.Ln), [("dve", d6)], "act")
        a6 = P.op("act", lambda e: e.activation(sl(4), sl(4), AF.Exp, scale=-0.5),
                  inc="act")
        d7 = P.op("dve", lambda e: e.tensor_tensor(sl(5), sl(3), sl(4), ALU.mult),
                  [("act", a6)], "dve")

    def sb(i):
        return statb[0:1, i * tile:(i + 1) * tile]

    reps = [("m", sl(0), 0, pk_m, d1, 6), ("r", sl(3), 2, pk_r, a4, 7)]
    if have_s:
        reps.append(("s", sl(5), 4, pk_s, d7, 2))
    out = {}
    for name, row, bi, pk, wv, tmpi in reps:
        wsem = "dve" if name in ("m", "s") else "act"
        ah = P.op("act", lambda e, r=row, o=sb(bi): e.activation(o, r, AF.Copy),
                  [(wsem, wv)], "act")
        dl = P.op("dve", lambda e, r=row, h=sb(bi), o=sl(tmpi): e.tensor_tensor(o, r, h, ALU.subtract),
                  [("act", ah)], "dve")
        al = P.op("act", lambda e, o=sb(bi + 1), r=sl(tmpi): e.activation(o, r, AF.Copy),
                  [("dve", dl)], "act")
        m1 = P.op("pe", lambda e, pk=pk, h=sb(bi): e.matmul(
            pk, ones1b[:], h, start=True, stop=False),
            [("act", ah), bank_rel[name]] + list(rep_extra), None)
        m2 = P.op("pe", lambda e, pk=pk, l=sb(bi + 1): e.matmul(
            pk, ones1b[:], l, start=False, stop=True),
            [("act", al)], "pe")
        out[name] = m2
    out["dtail"] = ("dve", P.cnt["dve"])
    return out



def emit_phase_A(P, sh, tile=256):
    ntiles = T // tile
    qpt = tile // Q
    upto = 'full'; max_tiles = None; scan_heads = NH; no_ops = ()
    xr = sh["xr"]; sid = sh["sid"]; sidb = sh["sidb"]
    onesb = sh["onesb"]; ones1b = sh["ones1b"]

    wA = P.dram("A_wA", [D, PROJ_PAD], F32, kind="ExternalInput")
    convT = P.dram("A_convT", [DI, 4], F32, kind="ExternalInput")
    cbT = P.dram("A_cbT", [128, 16], F32, kind="ExternalInput")
    nscT = P.dram("A_nscT", [128, 8], F32, kind="ExternalInput")
    alog = P.dram("A_alog", [NH, 1], F32, kind="ExternalInput")
    dtbias = P.dram("A_dtbias", [NH, 1], F32, kind="ExternalInput")
    wo_in = P.dram("A_wo_in", [DI, D], F32, kind="ExternalInput")
    caus = P.dram("A_caus", [128, 128], F32, kind="ExternalInput")
    seli = P.dram("A_seli", [128, NH * 128], F32, kind="ExternalInput")
    wab = P.dram("A_wab_scr", [D, PROJ_PAD], BF16)
    wob = P.dram("A_wob_scr", [DI, D], BF16)
    dbg_hooks = set()

    xnb = P.sbuf("xnb", [128, KC * tile], BF16)
    wbuf = [P.sbuf(f"wbuf{i}", [128, KC * 256], BF16) for i in range(2)]
    wobuf = [P.sbuf(f"wobuf{i}", [128, 16 * 128], BF16) for i in range(2)]
    sgate = P.sbuf("sgate", [128, qpt * DI], BF16)
    sbm = P.sbuf("sbm", [128, 16 * (tile + 3)], BF16)
    hbuf = P.sbuf("hbuf", [128, 16 * tile], BF16)
    hT = [P.sbuf(f"hT{i}", [128, DI], BF16) for i in range(2)]
    ybuf = [P.sbuf(f"ybuf{i}", [128, DI], BF16) for i in range(2)]
    ygT = P.sbuf("ygT", [128, 16 * tile], BF16)
    state = P.sbuf("state", [64, NH * HD], F32)
    stateb = P.sbuf("stateb", [128, NH * HD], BF16)
    Bq = P.sbuf("Bq", [128, tile], BF16)
    Cq = P.sbuf("Cq", [128, tile], BF16)
    dtraw = P.sbuf("dtraw", [32, tile], F32)
    lnb = [P.sbuf(f"lnb{i}", [128, tile], F32) for i in range(2)]
    t1b = [P.sbuf(f"t1b{i}", [128, tile], F32) for i in range(2)]
    cvt = [P.sbuf(f"cvt{i}", [128, tile], BF16) for i in range(2)]
    xqb = [P.sbuf(f"xqb{i}", [128, tile], BF16) for i in range(4)]
    statr = P.sbuf("statr", [1, 8 * tile], F32)
    statb = P.sbuf("statb", [1, 6 * tile], BF16)
    dtv = P.sbuf("dtv", [32, 128], F32)
    Lbuf = P.sbuf("Lbuf", [32, 128], F32)
    edec = P.sbuf("edec", [32, 128], F32)
    dtrow = P.sbuf("dtrow", [32, 128], F32)
    eLend = P.sbuf("eLend", [32, 1], F32)
    Lhi = P.sbuf("Lhi", [128, 128], BF16)
    Llo = P.sbuf("Llo", [128, 128], BF16)
    Ltmp = P.sbuf("Ltmp", [32, 128], F32)
    tev = P.sbuf("tev", [128, 128], F32)
    negLT = P.sbuf("negLT", [128, 32], F32)
    expLT = P.sbuf("expLT", [128, 32], F32)
    Gs = P.sbuf("Gs", [128, 128], BF16)
    Gm = [P.sbuf(f"Gm{i}", [128, 128], BF16) for i in range(2)]
    Dc = [P.sbuf(f"Dc{i}", [128, 128], F32) for i in range(4)]
    Eb = [P.sbuf(f"Eb{i}", [128, 128], F32) for i in range(4)]
    Mb = [P.sbuf(f"Mb{i}", [128, 128], BF16) for i in range(4)]
    XdTb = [P.sbuf(f"XdTb{i}", [128, 64], BF16) for i in range(4)]
    XdT2b = [P.sbuf(f"XdT2b{i}", [128, 64], BF16) for i in range(4)]
    BTs = P.sbuf("BTs", [128, 128], BF16)
    scaus = P.sbuf("scaus", [128, 128], BF16)
    sselb = P.sbuf("sselb", [128, NH * 128], BF16)
    scw = P.sbuf("scw", [128, 16 * 4], F32)
    scb = P.sbuf("scb", [128, 16], F32)
    snsc = P.sbuf("snsc", [128, 8], F32)
    salog = P.sbuf("salog", [32, 1], F32)
    sdtb = P.sbuf("sdtb", [32, 1], F32)
    negeA = P.sbuf("negeA", [32, 1], F32)
    ones32 = P.sbuf("ones32", [32, 128], F32)
    zero32 = P.sbuf("zero32", [32, 1], F32)

    pk0 = P.psum("pk0", [128, 512], F32)
    pk1 = P.psum("pk1", [128, 512], F32)
    pk2 = P.psum("pk2", [128, 512], F32)
    pk3 = P.psum("pk3", [128, 512], F32)
    pk4 = P.psum("pk4", [128, 512], F32)
    pk5 = P.psum("pk5", [128, 1024], BF16)
    pk6 = P.psum("pk6", [128, 1024], BF16)
    pk7 = P.psum("pk7", [128, 512], F32)

    P.dma("sp", scw[:].rearrange("p (c k) -> p c k", c=16),
          convT.rearrange("(c p) k -> p c k", p=128), sem="initA")
    P.dma("sp", scb[:], cbT, sem="initA")
    P.dma("sp", snsc[:], nscT, sem="initA")
    P.dma("sp", salog[:], alog, sem="initA")
    P.dma("sp", sdtb[:], dtbias, sem="initA")
    w_init = P.cnt["initA"]
    w_wab = P.dma("pool", wab, wA, max_dma_last_dim=8176, sem="cast_a")
    w_wob = P.dma("pool", wob, wo_in, max_dma_last_dim=8176, sem="cast_b")
    w_caus = P.dma("pool", scaus[:], caus, sem="cast_c")
    w_sel = P.dma("pool", sselb[:], seli, sem="cast_d")

    P.op("dve", lambda e: e.memset(ones32[:], 1.0))
    P.op("dve", lambda e: e.memset(zero32[:], 0.0))
    P.op("dve", lambda e: e.memset(state[:], 0.0))
    P.op("dve", lambda e: e.memset(BTs[:], 0.0))
    P.op("dve", lambda e: e.memset(Bq[:], 0.0))
    P.op("dve", lambda e: e.memset(Cq[:], 0.0))
    P.op("dve", lambda e: e.memset(Lhi[:], 0.0))
    P.op("dve", lambda e: e.memset(Llo[:], 0.0))
    d_init = P.op("dve", lambda e: e.memset(stateb[:], 0.0), inc="dve")
    p_halo = P.op("pool", lambda e: e.memset(sbm[:], 0.0), inc="pool")
    a_eA = P.op("act", lambda e: e.activation(negeA[:], salog[:], AF.Exp),
                [("initA", w_init)], "act")
    d_eA = P.op("dve", lambda e: e.tensor_scalar(negeA[:], negeA[:], -1.0, None,
                                                 ALU.mult, ALU.bypass),
                [("act", a_eA)], "dve")

    fence_w = [("initA", w_init), ("dve", d_init), ("pool", p_halo),
               ("cast_c", w_caus), ("cast_d", w_sel), ("act", a_eA),
               ("dve", d_eA)]
    P.op("pe", lambda e: e.nop(), fence_w)
    P.op("act", lambda e: e.nop(), fence_w)
    P.op("dve", lambda e: e.engine_nop(), fence_w)
    P.op("pool", lambda e: e.engine_nop(), fence_w)

    rb_xq = RB(4)
    rb_t1 = RB(2)
    rb_ln = RB(2)
    rb_cvt = RB(2)
    rb_w = RB(2)
    rb_wo = RB(2)
    yrel = [("act", 0)] * 4
    yrel2 = [("dve", 0)] * 4
    srel = [("dve", 0)] * 4
    erel = [("dve", 0)] * 2
    trel = ("act", 0)
    bf16rel = [("act", 0)] * 2
    bf16turn = [0]
    Grel = ("act", 0)
    hTrel = [("pool", 0)] * 2
    ybrel = [("act", 0)] * 2
    gmrel = ("pool", 0)
    gsrel = ("pool", 0)
    tev_dve_rel = 0
    tev_pool_rel = 0
    aE_prev = [("act", 0)] * 4
    pm_prev = [("pool", 0)] * 4
    y2_prev = [("pe", 0)] * 4
    xd2_prev = [("dve", 0)] * 4
    s_prev = [("pe", 0)] * 4
    bank_rel = {"sum": ("dve", 0), "sq": ("dve", 0), "m": ("dve", 0),
                "r": ("dve", 0), "s": ("dve", 0)}
    sbm_cons = [("dve", 0), ("pool", 0)]
    prev_state_b = [("dve", d_init)] * NH
    last_wout_dve = 0
    statefin = []

    for i in range(ntiles if max_tiles is None else max_tiles):
        t0 = i * tile

        repv = emit_ln_stats(P, xr, t0, tile, xqb, rb_xq, statr, statb, onesb,
                             ones1b, pk0, pk1, pk2[:, 0:tile], pk3[:, 0:tile],
                             bank_rel, pk_s=pk2[:, 256:256 + tile],
                             extra_waits=[("initA", w_init),
                                          ("dve", last_wout_dve)],
                             rep_extra=[trel, Grel, erel[0], erel[1]])

        p_x_last = 0
        for c in range(KC):
            xc = xr[:, c * T + t0: c * T + t0 + tile]
            kt, wt = rb_t1.acquire()
            d_t1 = P.op("dve", lambda e, x=xc, kt=kt: e.tensor_tensor(
                t1b[kt][:], x, pk2[:, 0:tile], ALU.subtract),
                wt + [("pe", repv["m"]), ("pe", repv["s"]),
                      ("dve", last_wout_dve), repv["dtail"]], "dve")
            kl, wl = rb_ln.acquire()
            d_ln = P.op("dve", lambda e, kl=kl, kt=kt: e.tensor_tensor(
                lnb[kl][:], t1b[kt][:], pk3[:, 0:tile], ALU.mult),
                wl + [("pe", repv["r"])], "dve")
            p_x = P.op("pool", lambda e, x=xc, kl=kl: e.tensor_tensor(
                x, x, lnb[kl][:], ALU.add), [("dve", d_ln)], "pool")
            rb_ln.release(kl, "pool", p_x)
            d_xn = P.op("dve", lambda e, kt=kt, c=c: e.scalar_tensor_tensor(
                xnb[:, c * tile:(c + 1) * tile], t1b[kt][:], snsc[:, c:c + 1],
                pk2[:, 256:256 + tile], ALU.mult, ALU.mult),
                [("pe", repv["s"])], "dve")
            rb_t1.release(kt, "dve", d_xn)
            p_x_last = p_x
        d_xn_last = P.cnt["dve"]
        bank_rel["m"] = ("dve", d_xn_last)
        bank_rel["r"] = ("dve", d_xn_last)
        bank_rel["s"] = ("dve", d_xn_last)


        if upto == 'ln':
            continue
        if i > 0:
            for c in range(16):
                off = c * (tile + 3)
                p_halo = P.op("pool", lambda e, off=off: e.tensor_copy(
                    sbm[:, off:off + 3], sbm[:, off + tile:off + tile + 3]),
                    sbm_cons if c == 0 else [], "pool")

        ssm_evac = {}
        dt_evac = B_evac = C_evac = None
        for j in range(8, 18):
            kw, ww = rb_w.acquire()
            w_sl = P.dma("sp", wbuf[kw][:].rearrange("p (k f) -> p k f", k=KC),
                         wA_slab_ap(wab, j), ww + [("cast_a", w_wab)],
                         sem="wd%d" % kw)
            mmv = None
            for sub in range(2):
                fc = j * 2 + sub
                if fc >= 35:
                    continue
                pk, bk = (pk0, "sum") if (fc % 2 == 0) else (pk1, "sq")
                for k in range(KC):
                    mmv = P.op("pe", lambda e, kw=kw, k=k, sub=sub, pk=pk: e.matmul(
                        pk[:, 0:tile],
                        wbuf[kw][:, k * 256 + sub * 128: k * 256 + sub * 128 + 128],
                        xnb[:, k * tile:(k + 1) * tile],
                        start=(k == 0), stop=(k == KC - 1)),
                        [("wd%d" % kw, w_sl), ("dve", d_xn_last)] +
                        ([bank_rel[bk]] if k == 0 else []),
                        "pe" if k == KC - 1 else None)
                if fc <= 31:
                    c = fc - 16
                    off = c * (tile + 3)
                    av = P.op("act", lambda e, pk=pk, off=off: e.activation(
                        sbm[:, off + 3:off + 3 + tile], pk[:, 0:tile], AF.Copy),
                        [("pe", mmv), ("pool", p_halo)], "act")
                    ssm_evac[c] = av
                elif fc == 32:
                    av = dt_evac = P.op("act", lambda e, pk=pk: e.activation(
                        dtraw[:, 0:tile], pk[0:32, 0:tile], AF.Copy),
                        [("pe", mmv)], "act")
                elif fc == 33:
                    av = B_evac = P.op("act", lambda e, pk=pk: e.activation(
                        Bq[0:64, 0:tile], pk[0:64, 0:tile], AF.Copy),
                        [("pe", mmv)], "act")
                else:
                    av = C_evac = P.op("act", lambda e, pk=pk: e.activation(
                        Cq[0:64, 0:tile], pk[0:64, 0:tile], AF.Copy),
                        [("pe", mmv)], "act")
                bank_rel[bk] = ("act", av)
            rb_w.release(kw, "pe", mmv)

        gate_evac = {}
        gbank = {0: (pk0, "sum"), 1: (pk1, "sq")}
        for ns in range(4):
            mm_tc = {0: None, 1: None}
            for jj in range(2):
                j = ns * 2 + jj
                kw, ww = rb_w.acquire()
                w_sl = P.dma("sp", wbuf[kw][:].rearrange("p (k f) -> p k f", k=KC),
                             wA_slab_ap(wab, j), ww + [("cast_a", w_wab)],
                             sem="wd%d" % kw)
                for tc in range(qpt):
                    pk, bk = gbank[tc]
                    for k in range(KC):
                        mm_tc[tc] = P.op("pe", lambda e, kw=kw, k=k, jj=jj, tc=tc, pk=pk: e.matmul(
                            pk[:, jj * 256:(jj + 1) * 256],
                            xnb[:, k * tile + tc * 128: k * tile + tc * 128 + 128],
                            wbuf[kw][:, k * 256:(k + 1) * 256],
                            start=(k == 0), stop=(k == KC - 1)),
                            [("wd%d" % kw, w_sl), ("dve", d_xn_last)] +
                            ([bank_rel[bk]] if (k == 0 and jj == 0) else []),
                            "pe" if k == KC - 1 else None)
                rb_w.release(kw, "pe", mm_tc[qpt - 1])
            for tc in range(qpt):
                pk, bk = gbank[tc]
                av = P.op("act", lambda e, pk=pk, tc=tc, ns=ns: e.activation(
                    sgate[:, tc * DI + ns * 512: tc * DI + (ns + 1) * 512],
                    pk[:, 0:512], AF.Silu), [("pe", mm_tc[tc])], "act")
                gate_evac[(tc, ns)] = av
                bank_rel[bk] = ("act", av)

        if upto == 'proj':
            continue
        h_evac = {}
        for c in range(16):
            eng = "dve"
            kc_, wc = rb_cvt.acquire()
            off = c * (tile + 3)
            waits = wc + [("act", ssm_evac[c]), ("pool", p_halo),
                          ("initA", w_init)]
            P.op(eng, lambda e, kc_=kc_, c=c, off=off: e.tensor_scalar(
                cvt[kc_][:], sbm[:, off:off + tile], scw[:, c * 4 + 0:c * 4 + 1],
                scb[:, c:c + 1], ALU.mult, ALU.add), waits)
            civ = None
            for k in (1, 2, 3):
                civ = P.op(eng, lambda e, kc_=kc_, c=c, k=k, off=off: e.scalar_tensor_tensor(
                    cvt[kc_][:], sbm[:, off + k:off + k + tile],
                    scw[:, c * 4 + k:c * 4 + k + 1], cvt[kc_][:],
                    ALU.mult, ALU.add), inc=(eng if k == 3 else None))
            av = P.op("act", lambda e, kc_=kc_, c=c: e.activation(
                hbuf[:, c * tile:(c + 1) * tile], cvt[kc_][:], AF.Silu),
                [(eng, civ)], "act")
            h_evac[c] = av
            rb_cvt.release(kc_, "act", av)
        h_all = P.cnt["act"]


        if upto == 'conv':
            continue
        yrel[2] = yrel2[2] = srel[2] = bank_rel["sum"]
        yrel[3] = yrel2[3] = srel[3] = bank_rel["sq"]
        for q in range(qpt):
            g = i * qpt + q
            dtq = dtraw[:, q * Q:(q + 1) * Q]
            a_abs = P.op("act", lambda e, s=dtq: e.activation(
                Ltmp[:], s, AF.Abs, bias=sdtb[:, 0:1]),
                [("act", dt_evac), ("initA", w_init), ("dve", P.cnt["dve"])], "act")
            a_e = P.op("act", lambda e: e.activation(Ltmp[:], Ltmp[:], AF.Exp,
                                                     scale=-1.0), inc="act")
            a_l = P.op("act", lambda e: e.activation(Ltmp[:], Ltmp[:], AF.Ln,
                                                     bias=1.0), inc="act")
            d_r = P.op("dve", lambda e, s=dtq: e.tensor_scalar(
                dtv[:], s, sdtb[:, 0:1], 0.0, ALU.add, ALU.max),
                [("act", dt_evac), ("pe", P.cnt["pe"])], "dve")
            d_dt = P.op("dve", lambda e: e.tensor_tensor(
                dtv[:], dtv[:], Ltmp[:], ALU.add),
                [("act", a_l)], "dve")
            d_a = P.op("dve", lambda e: e.tensor_scalar(
                Ltmp[:], dtv[:], negeA[:, 0:1], None, ALU.mult, ALU.bypass),
                [("dve", d_eA)], "dve")
            d_L = P.op("dve", lambda e: e.tensor_tensor_scan(
                Lbuf[:], ones32[:], Ltmp[:], zero32[:, 0:1],
                ALU.mult, ALU.add), [("dve", d_a), ("pe", P.cnt["pe"])], "dve")
            a_ed = P.op("act", lambda e: e.activation(
                edec[:], Lbuf[:], AF.Exp, scale=-1.0,
                bias=Lbuf[:, 127:128]), [("dve", d_L)], "act")
            a_dt0 = P.op("act", lambda e: e.activation(
                eLend[:], Lbuf[:, 127:128], AF.Exp), [("dve", d_L)], "act")
            d_dtr = P.op("dve", lambda e: e.tensor_scalar(
                dtrow[:], ones32[:], eLend[:, 0:1], None, ALU.mult, ALU.bypass),
                [("act", a_dt0), ("pe", P.cnt["pe"])], "dve")
            a_hi = P.op("act", lambda e: e.activation(Lhi[0:32, :], Lbuf[:], AF.Copy),
                        [("dve", d_L), ("pe", P.cnt["pe"])], "act")
            d_lo = P.op("dve", lambda e: e.tensor_tensor(
                Ltmp[:], Lbuf[:], Lhi[0:32, :], ALU.subtract), [("act", a_hi)], "dve")
            a_lo = P.op("act", lambda e: e.activation(Llo[0:32, :], Ltmp[:], AF.Copy),
                        [("dve", d_lo)], "act")
            id32 = sid[0:32, 0:32]
            pe_t1 = P.op("pe", lambda e: e.transpose(pk3[:, 256:288], dtv[:], id32),
                         [("dve", d_dt), ("dve", d_a), trel, Grel, erel[1],
                          ("initA", w_init)], None)
            pe_t1 = P.op("pe", lambda e: e.transpose(pk3[:, 288:320], Lbuf[:], id32),
                         [("dve", d_L), ("act", a_hi), ("dve", d_lo)], None)
            pe_t1 = P.op("pe", lambda e: e.transpose(pk3[:, 320:352], edec[:], id32),
                         [("act", a_ed)], None)
            pe_t1 = P.op("pe", lambda e: e.transpose(pk3[:, 352:384], dtrow[:], id32),
                         [("dve", d_dtr)], "pe")
            a_tev = P.op("act", lambda e: e.activation(tev[:], pk3[:, 256:384], AF.Copy),
                         [("pe", pe_t1), ("dve", tev_dve_rel), ("pool", tev_pool_rel)],
                         "act")
            trel = ("act", a_tev)
            a_dtotT = a_tev
            d_nL = P.op("dve", lambda e: e.tensor_scalar(
                negLT[:], tev[:, 32:64], -1.0, None, ALU.mult, ALU.bypass),
                [("act", a_tev)], "dve")
            a_eL = P.op("act", lambda e: e.activation(expLT[:], tev[:, 32:64], AF.Exp),
                        [("act", a_tev)], "act")
            bt = bf16turn[0] % 2
            bf16turn[0] += 1
            pkb = pk5 if bt == 0 else pk6
            pe_t3 = P.op("pe", lambda e, q=q, pkb=pkb: e.transpose(
                pkb[:, 0:64], Bq[0:64, q * Q:(q + 1) * Q], sidb[0:64, 0:64]),
                [("act", B_evac), bf16rel[bt]], "pe")
            a_BT = P.op("act", lambda e, pkb=pkb: e.activation(
                BTs[:, 0:64], pkb[:, 0:64], AF.Copy), [("pe", pe_t3)], "act")
            bf16rel[bt] = ("act", a_BT)
            pe_G = P.op("pe", lambda e, q=q: e.matmul(
                pk3[:, 384:512], Bq[:, q * Q:(q + 1) * Q], Cq[:, q * Q:(q + 1) * Q],
                start=True, stop=True),
                [("act", B_evac), ("act", C_evac), Grel, ("act", a_tev),
                 erel[1]], "pe")
            a_G = P.op("act", lambda e: e.activation(Gs[:], pk3[:, 384:512], AF.Copy),
                       [("pe", pe_G), gsrel], "act")
            Grel = ("act", a_G)
            kh = g % 2
            hT_ev = 0
            for c in range(16):
                bt = bf16turn[0] % 2
                bf16turn[0] += 1
                pkb = pk5 if bt == 0 else pk6
                pe_th = P.op("pe", lambda e, c=c, q=q, pkb=pkb: e.transpose(
                    pkb[:, 128:256],
                    hbuf[:, c * tile + q * Q: c * tile + q * Q + 128], sidb[:]),
                    [("act", h_evac[c]), bf16rel[bt]] +
                    ([hTrel[kh]] if c == 0 else []), "pe")
                av = P.op("act", lambda e, c=c, kh=kh, pkb=pkb: e.activation(
                    hT[kh][:, c * 128:(c + 1) * 128],
                    pkb[:, 128:256], AF.Copy),
                    [("pe", pe_th)], "act")
                bf16rel[bt] = ("act", av)
                hT_ev = av
            if scan_heads == 0:
                continue
            p_gmc = P.op("pool", lambda e: e.tensor_tensor(
                Gm[0][:], Gs[:], scaus[:], ALU.mult),
                [("act", a_G), ("cast_c", w_caus), gmrel], "pool")
            gsrel = ("pool", p_gmc)
            d_yg_last = 0
            G4 = 4
            for hg in range(0, scan_heads, G4):
                hs = list(range(hg, min(hg + G4, scan_heads)))
                st = {h: {} for h in hs}
                for h in hs:
                    ks = h % 4
                    pke = pk2 if h % 2 == 0 else pk3
                    P.op("pe", lambda e, h=h, pke=pke: e.matmul(
                        pke[:, 0:128],
                        sselb[:, h * 128:(h + 1) * 128], Lhi[:],
                        start=True, stop=False),
                        [("act", a_hi), ("cast_d", w_sel), erel[h % 2],
                         ("dve", d_xn_last)] +
                        ([("act", a_tev), ("act", a_G)] if h % 2 == 1 else []),
                        None)
                    pe_rep = P.op("pe", lambda e, h=h, pke=pke: e.matmul(
                        pke[:, 0:128],
                        sselb[:, h * 128:(h + 1) * 128], Llo[:],
                        start=False, stop=True), [("act", a_lo)], "pe")
                    d_dc = P.op("dve", lambda e, ks=ks, h=h, pke=pke: e.tensor_scalar(
                        Dc[ks][:], pke[:, 0:128],
                        negLT[:, h:h + 1], 0.0, ALU.add, ALU.min),
                        [("pe", pe_rep), ("dve", d_nL), aE_prev[ks]], "dve")
                    erel[h % 2] = ("dve", d_dc)
                    a_E = P.op("act", lambda e, ks=ks: e.activation(
                        Eb[ks][:], Dc[ks][:], AF.Exp),
                        [("dve", d_dc), pm_prev[ks]], "act")
                    aE_prev[ks] = ("act", a_E)
                for h in hs:
                    ks = h % 4
                    p_m = P.op("pool", lambda e, ks=ks: e.tensor_tensor(
                        Mb[ks][:], Eb[ks][:], Gm[0][:], ALU.mult),
                        [aE_prev[ks], ("pool", p_gmc), y2_prev[ks]], "pool")
                    pm_prev[ks] = ("pool", p_m)
                    p_xd = P.op("act", lambda e, ks=ks, h=h, kh=kh: e.activation(
                        XdTb[ks][:], hT[kh][:, h * 64:(h + 1) * 64], AF.Copy,
                        scale=tev[:, h:h + 1]),
                        [("act", hT_ev), ("act", a_tev), y2_prev[ks],
                         xd2_prev[ks]], "act")
                    p_xd2 = P.op("dve", lambda e, ks=ks, h=h: e.tensor_scalar(
                        XdT2b[ks][:], XdTb[ks][:], tev[:, 64 + h:64 + h + 1],
                        None, ALU.mult, ALU.bypass),
                        [("act", p_xd), s_prev[ks]], "dve")
                    xd2_prev[ks] = ("dve", p_xd2)
                    st[h].update(p_m=p_m, p_xd=p_xd, p_xd2=p_xd2)
                for h in hs:
                    ks = h % 4
                    par = h % 4
                    pkp = (pk4, pk7, pk0, pk1)[par]
                    pe_yi = P.op("pe", lambda e, h=h, q=q, pkp=pkp: e.matmul(
                        pkp[:, 0:64], Cq[:, q * Q:(q + 1) * Q],
                        stateb[:, h * 64:(h + 1) * 64], start=True, stop=True),
                        [("act", C_evac), prev_state_b[h], yrel[par],
                         yrel2[par], srel[par]], "pe")
                    pe_y2 = P.op("pe", lambda e, ks=ks, pkp=pkp: e.matmul(
                        pkp[:, 64:128], Mb[ks][:],
                        XdTb[ks][:], start=True, stop=True),
                        [("pool", st[h]["p_m"]), ("act", st[h]["p_xd"])], "pe")
                    y2_prev[ks] = ("pe", pe_y2)
                    pe_s = P.op("pe", lambda e, ks=ks, pkp=pkp: e.matmul(
                        pkp[:, 128:192], BTs[:],
                        XdT2b[ks][:], start=True, stop=True),
                        [("dve", st[h]["p_xd2"]), ("act", a_BT)], "pe")
                    st[h]["pe_s"] = pe_s
                for h in hs:
                    par = h % 4
                    pkp = (pk4, pk7, pk0, pk1)[par]
                    pe_s = st[h]["pe_s"]
                    a_yi = P.op("act", lambda e, h=h, kh=kh, pkp=pkp: e.activation(
                        ybuf[kh][:, h * 64:(h + 1) * 64],
                        pkp[:, 0:64], AF.Copy,
                        scale=expLT[:, h:h + 1]),
                        [("pe", pe_s), ("act", a_eL)] +
                        ([ybrel[kh]] if h == 0 else []), "act")
                    yrel[par] = ("act", a_yi)
                    d_yg = P.op("dve", lambda e, h=h, kh=kh, pkp=pkp: e.tensor_tensor(
                        ybuf[kh][:, h * 64:(h + 1) * 64],
                        ybuf[kh][:, h * 64:(h + 1) * 64],
                        pkp[:, 64:128], ALU.add),
                        [("pe", pe_s), ("act", a_yi)], "dve")
                    yrel2[par] = ("dve", d_yg)
                    d_yg_last = d_yg
                    d_st = P.op("dve", lambda e, h=h, pkp=pkp: e.scalar_tensor_tensor(
                        state[:, h * 64:(h + 1) * 64],
                        state[:, h * 64:(h + 1) * 64],
                        tev[0:64, 96 + h:97 + h],
                        pkp[0:64, 128:192],
                        ALU.mult, ALU.add), [("pe", pe_s), ("act", a_dtotT)], "dve")
                    srel[par] = ("dve", d_st)
                    p_stb = P.op("pool", lambda e, h=h: e.tensor_copy(
                        stateb[0:64, h * 64:(h + 1) * 64],
                        state[:, h * 64:(h + 1) * 64]),
                        [("dve", d_st)], "pool")
                    prev_state_b[h] = ("pool", p_stb)
            hTrel[kh] = ("act", P.cnt["act"])
            gmrel = ("pool", P.cnt["pool"])
            tev_dve_rel = P.cnt["dve"]
            tev_pool_rel = P.cnt["pool"]

            d_gate = P.op("dve", lambda e, kh=kh, q=q: e.tensor_tensor(
                ybuf[kh][:], ybuf[kh][:], sgate[:, q * DI:(q + 1) * DI], ALU.mult),
                [("dve", d_yg_last)] +
                [("act", gate_evac[(q, ns)]) for ns in range(4)], "dve")
            d_yg_last = d_gate

            ygT_ev = 0
            for c in range(16):
                bt = bf16turn[0] % 2
                bf16turn[0] += 1
                pkb = pk5 if bt == 0 else pk6
                pe_ty = P.op("pe", lambda e, c=c, kh=kh, pkb=pkb: e.transpose(
                    pkb[:, 128:256],
                    ybuf[kh][:, c * 128:(c + 1) * 128], sidb[:]),
                    [("dve", d_yg_last), bf16rel[bt]], "pe")
                av = P.op("act", lambda e, c=c, q=q, pkb=pkb: e.activation(
                    ygT[:, c * tile + q * Q: c * tile + q * Q + 128],
                    pkb[:, 128:256], AF.Copy),
                    [("pe", pe_ty)], "act")
                bf16rel[bt] = ("act", av)
                ygT_ev = av
            ybrel[kh] = ("act", ygT_ev)

        sbm_cons = [("dve", P.cnt["dve"]), ("pool", P.cnt["pool"])]
        if upto == 'scan':
            continue
        bank_rel["sum"] = srel[2]
        bank_rel["sq"] = srel[3]

        for dm in range(KC):
            kw, ww = rb_wo.acquire()
            w_sl = P.dma("sp", wobuf[kw][:].rearrange("p (k d) -> p k d", k=16),
                         wo_slab_ap(wob, dm), ww + [("cast_b", w_wob)],
                         sem="wod%d" % kw)
            pk, bk = (pk0, "sum") if (dm % 2 == 0) else (pk1, "sq")
            mmv = None
            for fc in range(16):
                mmv = P.op("pe", lambda e, kw=kw, fc=fc, pk=pk: e.matmul(
                    pk[:, 0:tile], wobuf[kw][:, fc * 128:(fc + 1) * 128],
                    ygT[:, fc * tile:(fc + 1) * tile],
                    start=(fc == 0), stop=(fc == 15)),
                    [("wod%d" % kw, w_sl), ("act", ygT_ev)] +
                    ([bank_rel[bk]] if fc == 0 else []),
                    "pe" if fc == 15 else None)
            rb_wo.release(kw, "pe", mmv)
            xc = xr[:, dm * T + t0: dm * T + t0 + tile]
            dv = P.op("dve", lambda e, x=xc, pk=pk: e.tensor_tensor(
                x, x, pk[:, 0:tile], ALU.add),
                [("pe", mmv), ("pool", p_x_last)], "dve")
            bank_rel[bk] = ("dve", dv)
            last_wout_dve = dv

    return




def emit_phase_B(P, sh, tile=256):
    ntiles = T // tile
    FF = 4096
    NFC = FF // 128
    xr = sh["xr"]; sid = sh["sid"]; sidb = sh["sidb"]
    onesb = sh["onesb"]; ones1b = sh["ones1b"]

    W1 = P.dram("B_W1", [FF, D], F32, kind="ExternalInput")
    W2 = P.dram("B_W2", [D, FF], F32, kind="ExternalInput")
    b1T = P.dram("B_b1T", [128, NFC], F32, kind="ExternalInput")
    b2T = P.dram("B_b2T", [128, KC], F32, kind="ExternalInput")

    w2ts = P.dram("B_w2t_scr", [NFC, 128, 1024], BF16)
    w1t = P.sbuf("w1t", [128, KC * FF], BF16)
    w2sl = [P.sbuf(f"w2sl{i}", [128, 8 * 1024], BF16) for i in range(2)]
    w2st = P.sbuf("w2st", [128, 1024], BF16)
    xn2 = P.sbuf("xn2", [128, KC * tile], BF16)
    ws = [P.sbuf(f"ws{i}", [128, 1024], F32) for i in range(2)]
    wtt = [P.sbuf(f"wtt{i}", [128, 1024], BF16) for i in range(2)]
    tmpb = P.sbuf("tmpb", [128, 1024], BF16)
    hball = P.sbuf("hball", [128, NFC * tile], BF16)
    xqb = [P.sbuf(f"xqb{i}", [128, tile], BF16) for i in range(4)]
    statr = P.sbuf("statr", [1, 6 * tile], F32)
    statb = P.sbuf("statb", [1, 4 * tile], BF16)
    acc1 = P.sbuf("acc1", [128, 2], F32)
    accb = P.sbuf("accb", [128, 4], BF16)
    thr = P.sbuf("thr", [128, 2], F32)
    nthr = P.sbuf("nthr", [128, 2], F32)
    rowt = P.sbuf("rowt", [1, 8], F32)
    rowb = P.sbuf("rowb", [1, 8], BF16)
    sb1 = P.sbuf("sb1", [128, NFC], F32)
    sb2 = P.sbuf("sb2", [128, KC], F32)
    red = P.sbuf("red", [128, 1], F32)

    pk0 = P.psum("pk0", [128, 512], F32)
    pk1 = P.psum("pk1", [128, 512], F32)
    pk2 = P.psum("pk2", [128, 512], F32)
    pk3 = P.psum("pk3", [128, 512], F32)
    pk4 = P.psum("pk4", [128, 512], F32)
    pk7 = P.psum("pk7", [128, 512], F32)
    pk5 = P.psum("pk5", [128, 1024], BF16)
    pk6 = P.psum("pk6", [128, 1024], BF16)

    P.dma("sp", sb1[:], b1T, sem="initB")
    P.dma("sp", sb2[:], b2T, sem="initB")
    w_init = P.cnt["initB"]
    P.op("dve", lambda e: e.memset(acc1[:], 0.0), [("initB", w_init)])
    d_init = P.op("dve", lambda e: e.memset(red[:], 0.0), inc="dve")
    fence_w = [("initB", w_init), ("dve", d_init)]
    P.op("pe", lambda e: e.nop(), fence_w)
    P.op("act", lambda e: e.nop(), fence_w)
    P.op("pool", lambda e: e.engine_nop(), fence_w)

    def w1_slab_ap(fc):
        return W1[fc * 128:(fc + 1) * 128, :]

    def w2_slab_ap(fc):
        r = W2.rearrange("(k p) f -> p k f", p=128)
        return r[:, :, fc * 128:(fc + 1) * 128]

    rb_ws = RB(2)
    d_red = 0
    for wi in (0, 1):
        for c in range(NFC if wi == 0 else NFC):
            kw, ww = rb_ws.acquire()
            ap = w1_slab_ap(c) if wi == 0 else w2_slab_ap(c)
            w_sl = P.dma("sp", ws[kw][:], ap, ww, sem="wd%d" % kw)
            P.op("dve", lambda e, kw=kw: e.reduce_sum(
                red[:], ws[kw][:], axis=AX.X, apply_absolute_value=True),
                [("wd%d" % kw, w_sl), ("dve", d_init)], "dve")
            d_red = P.op("dve", lambda e, wi=wi: e.tensor_tensor(
                acc1[:, wi:wi + 1], acc1[:, wi:wi + 1], red[:], ALU.add), inc="dve")
            rb_ws.release(kw, "dve", d_red)
    a_h = P.op("act", lambda e: e.activation(accb[:, 0:2], acc1[:, 0:2], AF.Copy),
               [("dve", d_red)], "act")
    d_l = P.op("dve", lambda e: e.tensor_tensor(thr[:, 0:2], acc1[:, 0:2],
                                                accb[:, 0:2], ALU.subtract),
               [("act", a_h)], "dve")
    a_l = P.op("act", lambda e: e.activation(accb[:, 2:4], thr[:, 0:2], AF.Copy),
               [("dve", d_l)], "act")
    m1 = P.op("pe", lambda e: e.matmul(pk0[0:1, 0:2], onesb[:], accb[:, 0:2],
                                       start=True, stop=False), [("act", a_h)], None)
    m2 = P.op("pe", lambda e: e.matmul(pk0[0:1, 0:2], onesb[:], accb[:, 2:4],
                                       start=False, stop=True), [("act", a_l)], "pe")
    d_t = P.op("dve", lambda e: e.tensor_scalar(rowt[0:1, 0:2], pk0[0:1, 0:2],
                                                0.5 / (FF * D), None, ALU.mult,
                                                ALU.bypass), [("pe", m2)], "dve")
    d_t2 = P.op("dve", lambda e: e.tensor_scalar(rowt[0:1, 2:4], rowt[0:1, 0:2], -1.0,
                                                 None, ALU.mult, ALU.bypass), inc="dve")
    a_th = P.op("act", lambda e: e.activation(rowb[0:1, 0:4], rowt[0:1, 0:4],
                                              AF.Copy), [("dve", d_t2)], "act")
    d_tl = P.op("dve", lambda e: e.tensor_tensor(rowt[0:1, 4:8], rowt[0:1, 0:4],
                                                 rowb[0:1, 0:4], ALU.subtract),
                [("act", a_th)], "dve")
    a_tl = P.op("act", lambda e: e.activation(rowb[0:1, 4:8], rowt[0:1, 4:8],
                                              AF.Copy), [("dve", d_tl)], "act")
    m3 = P.op("pe", lambda e: e.matmul(pk1[:, 0:4], ones1b[:], rowb[0:1, 0:4],
                                       start=True, stop=False), [("act", a_th)], None)
    m4 = P.op("pe", lambda e: e.matmul(pk1[:, 0:4], ones1b[:], rowb[0:1, 4:8],
                                       start=False, stop=True), [("act", a_tl)], "pe")
    a_thr = P.op("act", lambda e: e.activation(thr[:, 0:2], pk1[:, 0:2], AF.Copy),
                 [("pe", m4)], "act")
    a_nthr = P.op("act", lambda e: e.activation(nthr[:, 0:2], pk1[:, 2:4], AF.Copy),
                  [("pe", m4)], "act")

    rb_wt = RB(2)
    bf16rel = [("act", 0)] * 2
    bfturn = [0]
    w2st_rel = ("act", 0)
    for wi in (0, 1):
        for c in range(NFC if wi == 0 else NFC):
            kw, ww = rb_ws.acquire()
            ap = w1_slab_ap(c) if wi == 0 else w2_slab_ap(c)
            w_sl = P.dma("sp", ws[kw][:], ap, ww, sem="wd%d" % kw)
            kt, wt_ = rb_wt.acquire()
            P.op("dve", lambda e, kw=kw, wi=wi: e.tensor_scalar(
                tmpb[:], ws[kw][:], nthr[:, wi:wi + 1], None, ALU.is_lt, ALU.bypass),
                [("wd%d" % kw, w_sl), ("act", a_nthr), ("act", a_thr)] + wt_, "dve")
            d_tn = P.op("dve", lambda e, kw=kw, kt=kt, wi=wi: e.scalar_tensor_tensor(
                wtt[kt][:], ws[kw][:], thr[:, wi:wi + 1], tmpb[:],
                ALU.is_gt, ALU.subtract), inc="dve")
            rb_ws.release(kw, "dve", d_tn)
            lastpe = None
            for k in range(8):
                bt = bfturn[0] % 2
                bfturn[0] += 1
                pkb = pk5 if bt == 0 else pk6
                pe_t = P.op("pe", lambda e, kt=kt, k=k, pkb=pkb: e.transpose(
                    pkb[:, 0:128], wtt[kt][:, k * 128:(k + 1) * 128], sidb[:]),
                    [("dve", d_tn), bf16rel[bt]], "pe")
                lastpe = pe_t
                if wi == 0:
                    fn = lambda e, c=c, k=k, pkb=pkb: e.activation(
                        w1t[:, k * FF + c * 128: k * FF + (c + 1) * 128],
                        pkb[:, 0:128], AF.Copy)
                else:
                    fn = lambda e, c=c, k=k, pkb=pkb: e.activation(
                        w2st[:, k * 128:(k + 1) * 128],
                        pkb[:, 0:128], AF.Copy)
                av = P.op("act", fn, [("pe", pe_t)] +
                          ([w2st_rel] if (wi == 1 and k == 0) else []), "act")
                bf16rel[bt] = ("act", av)
            if wi == 1:
                w2st_rel = ("d_w2s", P.dma("sp", w2ts[c], w2st[:],
                                           [("act", av)], sem="d_w2s"))
            rb_wt.release(kt, "pe", lastpe)
    wt_done = P.cnt["act"]

    rb_xq = RB(4)
    rb_w2 = RB(2)
    bank_rel = {"sum": ("dve", 0), "sq": ("dve", 0), "m": ("dve", 0),
                "r": ("dve", 0)}
    orel = [("dve", 0)] * 2
    hrel = [("act", 0)] * 2
    h_mm_prev = 0

    for i in range(ntiles):
        t0 = i * tile
        repv = emit_ln_stats_x(P, xr, tile, xqb, rb_xq, statr, statb, onesb,
                               ones1b, pk0, pk1, pk2[:, 0:tile], pk3[:, 0:tile],
                               bank_rel, extra_waits=[orel[0], orel[1]],
                               xoff=t0, xstride=T)
        d_xn_last = 0
        for c in range(KC):
            xc = xr[:, c * T + t0: c * T + t0 + tile]
            P.op("dve", lambda e, x=xc, c=c: e.tensor_tensor(
                xn2[:, c * tile:(c + 1) * tile], x, pk2[:, 0:tile], ALU.subtract),
                [("pe", repv["m"]), ("pe", h_mm_prev), repv["dtail"]], "dve")
            d_xn_last = P.op("dve", lambda e, c=c: e.tensor_tensor(
                xn2[:, c * tile:(c + 1) * tile], xn2[:, c * tile:(c + 1) * tile],
                pk3[:, 0:tile], ALU.mult),
                [("pe", repv["r"])], "dve")
        bank_rel["m"] = ("dve", d_xn_last)
        bank_rel["r"] = ("dve", d_xn_last)

        a_h_last = 0
        for fc in range(NFC):
            par = fc % 2
            pkh = pk4 if par == 0 else pk7
            mmh = None
            for k in range(KC):
                mmh = P.op("pe", lambda e, k=k, fc=fc, pkh=pkh: e.matmul(
                    pkh[:, 0:tile],
                    w1t[:, k * FF + fc * 128:k * FF + (fc + 1) * 128],
                    xn2[:, k * tile:(k + 1) * tile],
                    start=(k == 0), stop=(k == KC - 1)),
                    [("act", wt_done), ("dve", d_xn_last)] +
                    ([hrel[par]] if k == 0 else []),
                    "pe" if k == KC - 1 else None)
            a_h2 = P.op("act", lambda e, fc=fc, pkh=pkh: e.activation(
                hball[:, fc * tile:(fc + 1) * tile], pkh[:, 0:tile], AF.Gelu,
                bias=sb1[:, fc:fc + 1]),
                [("pe", mmh)] + ([orel[0], orel[1]] if fc < 2 else []), "act")
            hrel[par] = ("act", a_h2)
            a_h_last = a_h2
        for fcg in range(4):
            kw2, ww2 = rb_w2.acquire()
            w_sl2 = P.dma("sp", w2sl[kw2][:].rearrange("p (g f) -> p g f", g=8),
                          w2ts[fcg * 8:(fcg + 1) * 8].rearrange("g p f -> p g f"),
                          list(ww2) + [("d_w2s", P.cnt.get("d_w2s", 0))],
                          sem="w2d%d" % kw2)
            for dm in range(KC):
                pko = pk0 if dm % 2 == 0 else pk1
                mmo = None
                for fci in range(8):
                    fc = fcg * 8 + fci
                    mmo = P.op("pe", lambda e, fci=fci, fc=fc, dm=dm, pko=pko,
                               kw2=kw2: e.matmul(
                        pko[:, 0:tile],
                        w2sl[kw2][:, fci * 1024 + dm * 128: fci * 1024 + (dm + 1) * 128],
                        hball[:, fc * tile:(fc + 1) * tile],
                        start=(fci == 0), stop=(fci == 7)),
                        [("act", a_h_last), ("w2d%d" % kw2, w_sl2)] +
                        ([orel[dm % 2]] if fci == 0 else []),
                        "pe" if fci == 7 else None)
                if fcg == 0:
                    dv = P.op("dve", lambda e, dm=dm, pko=pko, t0=t0: e.scalar_tensor_tensor(
                        xr[:, dm * T + t0: dm * T + t0 + tile],
                        pko[:, 0:tile], sb2[:, dm:dm + 1],
                        xr[:, dm * T + t0: dm * T + t0 + tile], ALU.add, ALU.add),
                        [("pe", mmo)], "dve")
                else:
                    dv = P.op("dve", lambda e, dm=dm, pko=pko, t0=t0: e.tensor_tensor(
                        xr[:, dm * T + t0: dm * T + t0 + tile],
                        xr[:, dm * T + t0: dm * T + t0 + tile],
                        pko[:, 0:tile], ALU.add),
                        [("pe", mmo)], "dve")
                orel[dm % 2] = ("dve", dv)
            rb_w2.release(kw2, "pe", mmo)
        h_mm_prev = P.cnt["pe"]

    return


def emit_ln_stats_x(P, xbuf, tile, xqb, rb_xq, statr, statb, onesb, ones1b,
                    pk_sum, pk_sq, pk_m, pk_r, bank_rel, extra_waits=(),
                    rep_extra=(), xoff=0, xstride=None):
    if xstride is None:
        xstride = tile
    mmS = mmQ = None
    for c in range(KC):
        src = xbuf[:, c * xstride + xoff: c * xstride + xoff + tile]
        kb, w = rb_xq.acquire()
        a1 = P.op("act", lambda e, s=src, d=xqb[kb]: e.activation(d[:, 0:tile], s, AF.Copy),
                  list(w) + list(extra_waits), "act")
        mmS = P.op("pe", lambda e, d=xqb[kb], c=c: e.matmul(
            pk_sum[0:1, 0:tile], onesb[:], d[:, 0:tile],
            start=(c == 0), stop=(c == KC - 1)),
            [("act", a1)] + ([bank_rel["sum"]] if c == 0 else []), "pe")
        kb2, w2 = rb_xq.acquire()
        a2 = P.op("act", lambda e, s=src, d=xqb[kb2]: e.activation(d[:, 0:tile], s, AF.Square),
                  list(w2) + list(extra_waits), "act")
        mmQ = P.op("pe", lambda e, d=xqb[kb2], c=c: e.matmul(
            pk_sq[0:1, 0:tile], onesb[:], d[:, 0:tile],
            start=(c == 0), stop=(c == KC - 1)),
            [("act", a2)] + ([bank_rel["sq"]] if c == 0 else []), "pe")
        rb_xq.release(kb, "pe", mmS)
        rb_xq.release(kb2, "pe", mmQ)

    def sl(i):
        return statr[0:1, i * tile:(i + 1) * tile]

    inv = 1.0 / D
    d1 = P.op("dve", lambda e: e.tensor_scalar(sl(0), pk_sum[0:1, 0:tile], inv, None,
                                               ALU.mult, ALU.bypass), [("pe", mmS)], "dve")
    d2 = P.op("dve", lambda e: e.tensor_scalar(sl(1), pk_sq[0:1, 0:tile], inv, None,
                                               ALU.mult, ALU.bypass), [("pe", mmQ)], "dve")
    bank_rel["sum"] = ("dve", d1)
    bank_rel["sq"] = ("dve", d2)
    P.op("dve", lambda e: e.tensor_tensor(sl(2), sl(0), sl(0), ALU.mult))
    d4 = P.op("dve", lambda e: e.tensor_tensor(sl(1), sl(1), sl(2), ALU.subtract),
              inc="dve")
    d4b = P.op("dve", lambda e: e.tensor_scalar(sl(2), sl(1), 1e-5, None,
                                                ALU.add, ALU.bypass), [("dve", d4)], "dve")
    a3 = P.op("act", lambda e: e.activation(sl(2), sl(2), AF.Ln), [("dve", d4b)], "act")
    a4 = P.op("act", lambda e: e.activation(sl(3), sl(2), AF.Exp, scale=-0.5),
              [("act", a3)], "act")
    reps = [("m", sl(0), 0, pk_m, ("dve", d1), 4), ("r", sl(3), 2, pk_r, ("act", a4), 5)]
    out = {}
    for name, row, bi, pk, wv, tmpi in reps:
        def sb(i):
            return statb[0:1, i * tile:(i + 1) * tile]
        ah = P.op("act", lambda e, r=row, o=sb(bi): e.activation(o, r, AF.Copy),
                  [wv], "act")
        dl = P.op("dve", lambda e, r=row, h=sb(bi), o=sl(tmpi): e.tensor_tensor(
            o, r, h, ALU.subtract), [("act", ah)], "dve")
        al = P.op("act", lambda e, o=sb(bi + 1), r=sl(tmpi): e.activation(o, r, AF.Copy),
                  [("dve", dl)], "act")
        m1 = P.op("pe", lambda e, pk=pk, h=sb(bi): e.matmul(
            pk, ones1b[:], h, start=True, stop=False),
            [("act", ah), bank_rel[name]] + list(rep_extra), None)
        m2 = P.op("pe", lambda e, pk=pk, l=sb(bi + 1): e.matmul(
            pk, ones1b[:], l, start=False, stop=True),
            [("act", al)], "pe")
        out[name] = m2
    out["dtail"] = ("dve", P.cnt["dve"])
    return out


def wA_slab_ap(wab, j):
    r = wab.rearrange("(k p) f -> p k f", p=128)
    return r[:, :, j * 256:(j + 1) * 256]


def wo_slab_ap(wob, dm):
    r = wob.rearrange("(k p) d -> p k d", p=128)
    return r[:, :, dm * 128:(dm + 1) * 128]



def emit_phase_C(P, sh, tile=256):
    ntiles = T // tile
    NHD = 16
    TC = tile // 128
    xr = sh["xr"]; sid = sh["sid"]; sidb = sh["sidb"]
    onesb = sh["onesb"]; ones1b = sh["ones1b"]

    Wq = P.dram("C_Wq", [D, D], F32, kind="ExternalInput")
    Wk = P.dram("C_Wk", [D, D], F32, kind="ExternalInput")
    Wv = P.dram("C_Wv", [D, D], F32, kind="ExternalInput")
    Wg = P.dram("C_Wg", [D, D], F32, kind="ExternalInput")
    Wo = P.dram("C_Wo", [D, D], F32, kind="ExternalInput")
    bg = P.dram("C_bg", [1, D], F32, kind="ExternalInput")

    xn3 = P.sbuf("xn3", [128, KC * T], BF16)
    WS = [P.sbuf(f"WS{i}", [128, KC * D], BF16) for i in range(3)]
    ktm = P.sbuf("ktm", [128, TC * D], BF16)
    vA = P.sbuf("vA", [128, TC * NHD * 65], BF16)
    qb = P.sbuf("qb", [128, KC * tile], BF16)
    gb = P.sbuf("gb", [128, TC * D], BF16)
    o2 = P.sbuf("o2", [128, TC * D], BF16)
    o2T = P.sbuf("o2T", [128, KC * tile], BF16)
    KVA = P.sbuf("KVA", [128, NHD * 65], BF16)
    bgrep = P.sbuf("bgrep", [128, D], F32)
    bgror = P.sbuf("bgror", [1, D], F32)
    bgrow = P.sbuf("bgrow", [1, 2 * D], BF16)
    et1 = P.sbuf("et1", [128, 512], F32)
    et2 = P.sbuf("et2", [128, 512], F32)
    zrec = P.sbuf("zrec", [128, 2], F32)
    xqb = [P.sbuf(f"xqb{i}", [128, tile], BF16) for i in range(4)]
    statr = P.sbuf("statr", [1, 6 * tile], F32)
    statb = P.sbuf("statb", [1, 4 * tile], BF16)

    pk0 = P.psum("pk0", [128, 512], F32)
    pk1 = P.psum("pk1", [128, 512], F32)
    pk2 = P.psum("pk2", [128, 512], F32)
    pk3 = P.psum("pk3", [128, 512], F32)
    pk4 = P.psum("pk4", [128, 512], F32)
    pk7 = P.psum("pk7", [128, 512], F32)
    pk5 = P.psum("pk5", [128, 1024], BF16)
    pk6 = P.psum("pk6", [128, 1024], BF16)

    P.dma("sp", bgror[:], bg, sem="initC")
    w_init = P.cnt["initC"]

    def wrearr(w):
        return w.rearrange("(k p) f -> p k f", p=128)

    w_k = P.dma("pool", WS[0][:].rearrange("p (k f) -> p k f", k=KC),
                wrearr(Wk), sem="cast_k")
    w_v = P.dma("pool", WS[1][:].rearrange("p (k f) -> p k f", k=KC),
                wrearr(Wv), sem="cast_v")
    w_o = P.dma("pool", WS[2][:].rearrange("p (k f) -> p k f", k=KC),
                wrearr(Wo), sem="cast_o")
    d_init = P.op("dve", lambda e: e.memset(vA[:], 1.0), inc="dve")
    fence_w = [("initC", w_init), ("dve", d_init)]
    P.op("pe", lambda e: e.nop(), fence_w)
    P.op("act", lambda e: e.nop(), fence_w)
    P.op("pool", lambda e: e.engine_nop(), fence_w)

    a_bh = P.op("act", lambda e: e.activation(bgrow[0:1, 0:D], bgror[:], AF.Copy),
                [("initC", w_init)], "act")
    d_bl = P.op("dve", lambda e: e.tensor_tensor(bgrep[0:1, 0:D], bgror[:],
                                                 bgrow[0:1, 0:D], ALU.subtract),
                [("act", a_bh)], "dve")
    a_bl = P.op("act", lambda e: e.activation(bgrow[0:1, D:2 * D], bgrep[0:1, 0:D],
                                              AF.Copy), [("dve", d_bl)], "act")
    a_bgrep = 0
    for half in range(2):
        pkh = pk0 if half == 0 else pk1
        P.op("pe", lambda e, pkh=pkh, half=half: e.matmul(
            pkh[:, 0:512], ones1b[:], bgrow[0:1, half * 512:half * 512 + 512],
            start=True, stop=False), [("act", a_bh), ("dve", d_bl)], None)
        m_bg = P.op("pe", lambda e, pkh=pkh, half=half: e.matmul(
            pkh[:, 0:512], ones1b[:], bgrow[0:1, D + half * 512:D + half * 512 + 512],
            start=False, stop=True), [("act", a_bl)], "pe")
        a_bgrep = P.op("act", lambda e, pkh=pkh, half=half: e.activation(
            bgrep[:, half * 512:(half + 1) * 512], pkh[:, 0:512], AF.Copy),
            [("pe", m_bg)], "act")

    rb_xq = RB(4)
    bank_rel = {"sum": ("act", a_bgrep), "sq": ("act", a_bgrep),
                "m": ("dve", 0), "r": ("dve", 0)}
    d_xn_last = 0
    for i in range(ntiles):
        t0 = i * tile
        repv = emit_ln_stats_x(P, xr, tile, xqb, rb_xq, statr, statb, onesb,
                               ones1b, pk0, pk1, pk2[:, 0:tile], pk3[:, 0:tile],
                               bank_rel, xoff=t0, xstride=T)
        for c in range(KC):
            xc = xr[:, c * T + t0: c * T + t0 + tile]
            P.op("dve", lambda e, x=xc, c=c, t0=t0: e.tensor_tensor(
                xn3[:, c * T + t0: c * T + t0 + tile], x, pk2[:, 0:tile],
                ALU.subtract), [("pe", repv["m"]), repv["dtail"]], "dve")
            d_xn_last = P.op("dve", lambda e, c=c, t0=t0: e.tensor_tensor(
                xn3[:, c * T + t0: c * T + t0 + tile],
                xn3[:, c * T + t0: c * T + t0 + tile], pk3[:, 0:tile], ALU.mult),
                [("pe", repv["r"])], "dve")
        bank_rel["m"] = ("dve", d_xn_last)
        bank_rel["r"] = ("dve", d_xn_last)

    prel = [("act", 0), ("act", 0)]
    kv_first = [True] * 4
    pproj = [0]
    kv_mm_last = None
    ktm_rel = ("pe", 0)
    vA_rel = ("pe", 0)
    for i in range(ntiles):
        t0 = i * tile
        k_ev = v_ev = None
        for wi, wbuf, wdone in ((0, WS[0], w_k), (1, WS[1], w_v)):
            for half in range(2):
                for tc in range(TC):
                    par = pproj[0] % 2
                    pproj[0] += 1
                    pkh = pk4 if par == 0 else pk7
                    mmv = None
                    for k in range(KC):
                        mmv = P.op("pe", lambda e, k=k, half=half, tc=tc, t0=t0,
                                   pkh=pkh, wbuf=wbuf: e.matmul(
                            pkh[:, 0:512],
                            xn3[:, k * T + t0 + tc * 128: k * T + t0 + tc * 128 + 128],
                            wbuf[:, k * D + half * 512: k * D + half * 512 + 512],
                            start=(k == 0), stop=(k == KC - 1)),
                            [("cast_k" if wi == 0 else "cast_v", wdone),
                             ("dve", d_xn_last)] + ([prel[par]] if k == 0 else []) +
                            ([ktm_rel] if (wi == 0 and half == 0 and tc == 0) else []) +
                            ([vA_rel] if (wi == 1 and half == 0 and tc == 0) else []),
                            "pe" if k == KC - 1 else None)
                    if wi == 0:
                        d_e1 = P.op("dve", lambda e, pkh=pkh: e.tensor_scalar(
                            et1[:], pkh[:, 0:512], 0.0, None, ALU.min, ALU.bypass),
                            [("pe", mmv)], "dve")
                        a_e = P.op("act", lambda e: e.activation(et2[:], et1[:], AF.Exp),
                                   [("dve", d_e1)], "act")
                        k_ev = P.op("dve", lambda e, pkh=pkh, half=half, tc=tc: e.scalar_tensor_tensor(
                            ktm[:, tc * D + half * 512: tc * D + (half + 1) * 512],
                            pkh[:, 0:512], 0.0, et2[:], ALU.max, ALU.add),
                            [("act", a_e)], "dve")
                        prel[par] = ("dve", k_ev)
                    else:
                        dst = vA[:, tc * NHD * 65:(tc + 1) * NHD * 65].rearrange(
                            "p (h c) -> p h c", c=65)[:, half * 8:(half + 1) * 8, 0:64]
                        v_ev = P.op("act", lambda e, pkh=pkh, dst=dst: e.activation(
                            dst, pkh[:, 0:512].rearrange("p (h c) -> p h c", c=64),
                            AF.Copy), [("pe", mmv)], "act")
                        prel[par] = ("act", v_ev)
        for h in range(NHD):
            bank = (pk0, pk1, pk2, pk3)[h // 4]
            reg = (h % 4) * 65
            for tc in range(TC):
                first = kv_first[h // 4]
                kv_first[h // 4] = False
                last = (i == ntiles - 1) and (tc == TC - 1)
                kv_mm_last = P.op("pe", lambda e, h=h, tc=tc, bank=bank, reg=reg,
                                  first=first, last=last: e.matmul(
                    bank[0:64, reg:reg + 65],
                    ktm[:, tc * D + h * 64: tc * D + h * 64 + 64],
                    vA[:, tc * NHD * 65 + h * 65: tc * NHD * 65 + (h + 1) * 65],
                    start=first, stop=last, skip_group_check=True),
                    [("dve", k_ev), ("act", v_ev)] +
                    ([bank_rel["m"], bank_rel["r"], bank_rel["sum"], bank_rel["sq"]]
                     if first else []),
                    "pe")
        ktm_rel = ("pe", kv_mm_last)
        vA_rel = ("pe", kv_mm_last)

    a_kv = 0
    for b in range(4):
        bank = (pk0, pk1, pk2, pk3)[b]
        for rw in (0, 64):
            a_kv = P.op("act", lambda e, b=b, bank=bank, rw=rw: e.activation(
                KVA[rw:rw + 64, b * 260:(b + 1) * 260], bank[0:64, 0:260], AF.Copy),
                [("pe", kv_mm_last)], "act")

    w_q = P.dma("pool", WS[0][:].rearrange("p (k f) -> p k f", k=KC),
                wrearr(Wq), [ktm_rel, ("dve", P.cnt["dve"])], sem="cast_k")
    w_g = P.dma("pool", WS[1][:].rearrange("p (k f) -> p k f", k=KC),
                wrearr(Wg), [vA_rel, ("act", P.cnt["act"])], sem="cast_v")
    orel = [("dve", 0), ("dve", 0)]
    grel = ("act", 0)
    hrel = [("dve", 0), ("dve", 0)]
    qrel = [("dve", 0), ("dve", 0)]
    bfrel = [("act", 0), ("act", 0)]
    bft = [0]
    for i in range(ntiles):
        t0 = i * tile
        q_ev = None
        for fc in range(KC):
            pkq = pk2 if fc % 2 == 0 else pk3
            mmq = None
            for k in range(KC):
                mmq = P.op("pe", lambda e, k=k, fc=fc, t0=t0, pkq=pkq: e.matmul(
                    pkq[:, 0:tile],
                    WS[0][:, k * D + fc * 128: k * D + (fc + 1) * 128],
                    xn3[:, k * T + t0: k * T + t0 + tile],
                    start=(k == 0), stop=(k == KC - 1)),
                    [("cast_k", w_q), ("dve", d_xn_last)] +
                    ([qrel[fc % 2], ("act", a_kv)] if k == 0 else []),
                    "pe" if k == KC - 1 else None)
            d_e1 = P.op("dve", lambda e, pkq=pkq: e.tensor_scalar(
                et1[:, 0:tile], pkq[:, 0:tile], 0.0, None, ALU.min, ALU.bypass),
                [("pe", mmq)], "dve")
            a_e = P.op("act", lambda e: e.activation(et2[:, 0:tile], et1[:, 0:tile],
                                                     AF.Exp), [("dve", d_e1)], "act")
            q_ev = P.op("dve", lambda e, pkq=pkq, fc=fc: e.scalar_tensor_tensor(
                qb[:, fc * tile:(fc + 1) * tile], pkq[:, 0:tile], 0.0, et2[:, 0:tile],
                ALU.max, ALU.add), [("act", a_e)], "dve")
            qrel[fc % 2] = ("dve", q_ev)
        g_ev = None
        for half in range(2):
            for tc in range(TC):
                mmg = None
                for k in range(KC):
                    mmg = P.op("pe", lambda e, k=k, half=half, tc=tc, t0=t0: e.matmul(
                        pk4[:, 0:512],
                        xn3[:, k * T + t0 + tc * 128: k * T + t0 + tc * 128 + 128],
                        WS[1][:, k * D + half * 512: k * D + half * 512 + 512],
                        start=(k == 0), stop=(k == KC - 1)),
                        [("cast_v", w_g), ("dve", d_xn_last)] +
                        ([grel, hrel[0]] if k == 0 else []),
                        "pe" if k == KC - 1 else None)
                d_g1 = P.op("dve", lambda e, half=half: e.tensor_tensor(
                    et1[:], pk4[:, 0:512], bgrep[:, half * 512:(half + 1) * 512],
                    ALU.add), [("pe", mmg), ("act", a_bgrep)], "dve")
                g_ev = P.op("act", lambda e, half=half, tc=tc: e.activation(
                    gb[:, tc * D + half * 512: tc * D + (half + 1) * 512], et1[:],
                    AF.Sigmoid), [("dve", d_g1)], "act")
                grel = ("act", g_ev)
        d_o_last = 0
        o2_pe_rel = P.cnt["pe"] if i == 0 else o2_pe_rel
        for h in range(NHD):
            par = h % 2
            pkh = pk4 if par == 0 else pk7
            fc2 = h // 2
            r0 = (h % 2) * 64
            for tc in range(TC):
                mmo = P.op("pe", lambda e, h=h, tc=tc, pkh=pkh, fc2=fc2, r0=r0,
                           t0=t0: e.matmul(
                    pkh[:, 0:65],
                    qb[r0:r0 + 64, fc2 * tile + tc * 128: fc2 * tile + tc * 128 + 128],
                    KVA[r0:r0 + 64, h * 65:(h + 1) * 65],
                    start=True, stop=True),
                    [("dve", q_ev), ("act", a_kv), hrel[par],
                     (grel if (h == 0 and tc == 0) else ("act", 0))], "pe")
                d_z = P.op("dve", lambda e, pkh=pkh: e.tensor_scalar(
                    zrec[:, 0:1], pkh[:, 64:65], 8.0, 8e-6, ALU.mult, ALU.max),
                    [("pe", mmo)], "dve")
                d_zr = P.op("dve", lambda e: e.reciprocal(zrec[:, 1:2], zrec[:, 0:1]),
                            [("dve", d_z)], "dve")
                d_o = P.op("dve", lambda e, pkh=pkh, h=h, tc=tc: e.scalar_tensor_tensor(
                    o2[:, tc * D + h * 64: tc * D + (h + 1) * 64],
                    pkh[:, 0:64], zrec[:, 1:2],
                    gb[:, tc * D + h * 64: tc * D + (h + 1) * 64],
                    ALU.mult, ALU.mult),
                    [("dve", d_zr), ("act", g_ev), ("pe", o2_pe_rel)], "dve")
                hrel[par] = ("dve", d_o)
                d_o_last = d_o
        to_ev = 0
        for tc in range(TC):
            for fc2 in range(KC):
                bt = bft[0] % 2
                bft[0] += 1
                pkb = pk5 if bt == 0 else pk6
                pe_t = P.op("pe", lambda e, tc=tc, fc2=fc2, pkb=pkb: e.transpose(
                    pkb[:, 0:128], o2[:, tc * D + fc2 * 128: tc * D + (fc2 + 1) * 128],
                    sidb[:]), [("dve", d_o_last), bfrel[bt]], "pe")
                to_ev = P.op("act", lambda e, tc=tc, fc2=fc2, pkb=pkb: e.activation(
                    o2T[:, fc2 * tile + tc * 128: fc2 * tile + tc * 128 + 128],
                    pkb[:, 0:128], AF.Copy), [("pe", pe_t)], "act")
                bfrel[bt] = ("act", to_ev)
        o2_pe_rel = 0
        for dm in range(KC):
            pko = pk0 if dm % 2 == 0 else pk1
            mmw = None
            for fc2 in range(KC):
                mmw = P.op("pe", lambda e, dm=dm, fc2=fc2, pko=pko: e.matmul(
                    pko[:, 0:tile],
                    WS[2][:, fc2 * D + dm * 128: fc2 * D + (dm + 1) * 128],
                    o2T[:, fc2 * tile:(fc2 + 1) * tile],
                    start=(fc2 == 0), stop=(fc2 == KC - 1)),
                    [("cast_o", w_o), ("act", to_ev)] +
                    ([orel[dm % 2], ("act", a_kv)] if fc2 == 0 else []),
                    "pe" if fc2 == KC - 1 else None)
            dv = P.op("dve", lambda e, dm=dm, pko=pko, t0=t0: e.tensor_tensor(
                xr[:, dm * T + t0: dm * T + t0 + tile],
                xr[:, dm * T + t0: dm * T + t0 + tile],
                pko[:, 0:tile], ALU.add), [("pe", mmw)], "dve")
            orel[dm % 2] = ("dve", dv)
            o2_pe_rel = mmw

    return



def emit_phase_D(P, sh, tile=512):
    ntiles = T // tile
    HID = 3072
    NFC = HID // 128
    xr = sh["xr"]; sid = sh["sid"]; sidb = sh["sidb"]
    onesb = sh["onesb"]; ones1b = sh["ones1b"]

    wg = P.dram("D_wg", [HID, D], I32, kind="ExternalInput")
    wu = P.dram("D_wu", [HID, D], I32, kind="ExternalInput")
    wd = P.dram("D_wd", [D, HID], I32, kind="ExternalInput")

    wgts = P.dram("D_wgt_scr", [NFC, 128, 1024], BF16)
    wuts = P.dram("D_wut_scr", [NFC, 128, 1024], BF16)
    wdt = P.sbuf("wdt", [128, NFC * D], BF16)
    wst = P.sbuf("wst", [128, 1024], BF16)
    wgl = [P.sbuf(f"wgl{i}", [128, 1024], BF16) for i in range(2)]
    wul = [P.sbuf(f"wul{i}", [128, 1024], BF16) for i in range(2)]
    xn4 = P.sbuf("xn4", [128, KC * tile], BF16)
    wsl = [P.sbuf(f"wsl{i}", [128, KC * 128], BF16) for i in range(2)]
    gball = P.sbuf("gball", [128, tile], BF16)
    uball = P.sbuf("uball", [128, tile], BF16)
    hball = P.sbuf("hball", [128, NFC * tile], BF16)
    xqb = [P.sbuf(f"xqb{i}", [128, tile], BF16) for i in range(4)]
    statr = P.sbuf("statr", [1, 6 * tile], F32)
    statb = P.sbuf("statb", [1, 4 * tile], BF16)

    pk0 = P.psum("pk0", [128, 512], F32)
    pk1 = P.psum("pk1", [128, 512], F32)
    pk2 = P.psum("pk2", [128, 512], F32)
    pk3 = P.psum("pk3", [128, 512], F32)
    pk4 = P.psum("pk4", [128, 512], F32)
    pk7 = P.psum("pk7", [128, 512], F32)
    pk5 = P.psum("pk5", [128, 1024], BF16)
    pk6 = P.psum("pk6", [128, 1024], BF16)

    rb_w = RB(2)
    bfrel = [("act", 0), ("act", 0)]
    bft = [0]
    wst_rel = ("act", 0)
    for wi, wsrc in ((0, wg), (1, wu), (2, wd)):
        for fc in range(NFC):
            kw, ww = rb_w.acquire()
            if wi < 2:
                ap = wsrc[fc * 128:(fc + 1) * 128, :].rearrange("p (k f) -> p k f", k=KC)
            else:
                ap = wsrc.rearrange("(k p) f -> p k f", p=128)[:, :, fc * 128:(fc + 1) * 128]
            w_sl = P.dma("pool", wsl[kw][:].rearrange("p (k f) -> p k f", k=KC), ap,
                         ww, sem="dwc%d" % kw)
            lastpe = None
            for k in range(KC):
                bt = bft[0] % 2
                bft[0] += 1
                pkb = pk5 if bt == 0 else pk6
                pe_t = P.op("pe", lambda e, kw=kw, k=k, pkb=pkb: e.transpose(
                    pkb[:, 0:128], wsl[kw][:, k * 128:(k + 1) * 128], sidb[:]),
                    [("dwc%d" % kw, w_sl), bfrel[bt]], "pe")
                lastpe = pe_t
                if wi < 2:
                    fn = lambda e, k=k, pkb=pkb: e.activation(
                        wst[:, k * 128:(k + 1) * 128], pkb[:, 0:128], AF.Copy)
                else:
                    fn = lambda e, fc=fc, k=k, pkb=pkb: e.activation(
                        wdt[:, fc * D + k * 128: fc * D + (k + 1) * 128],
                        pkb[:, 0:128], AF.Copy)
                av = P.op("act", fn, [("pe", pe_t)] +
                          ([wst_rel] if (wi < 2 and k == 0) else []), "act")
                bfrel[bt] = ("act", av)
            if wi < 2:
                scr = wgts if wi == 0 else wuts
                wst_rel = ("d_wst", P.dma("sp", scr[fc], wst[:],
                                          [("act", av)], sem="d_wst"))
            rb_w.release(kw, "pe", lastpe)
    wt_done = P.cnt["act"]

    rb_xq = RB(4)
    rb_wg = RB(2)
    rb_wu = RB(2)
    wst_done = P.cnt["d_wst"]
    bank_rel = {"sum": ("dve", 0), "sq": ("dve", 0), "m": ("dve", 0),
                "r": ("dve", 0)}
    orel = [("dve", 0)] * 2
    grel = [("act", 0)] * 2
    urel = [("act", 0)] * 2
    h_mm_prev = 0
    s2 = 1.0 / (127.0 * 127.0)

    for i in range(ntiles):
        t0 = i * tile
        repv = emit_ln_stats_x(P, xr, tile, xqb, rb_xq, statr, statb, onesb,
                               ones1b, pk0, pk1, pk2[:, 0:tile], pk3[:, 0:tile],
                               bank_rel,
                               extra_waits=[orel[0], orel[1], urel[0], urel[1]],
                               xoff=t0, xstride=T)
        d_xn_last = 0
        for c in range(KC):
            xc = xr[:, c * T + t0: c * T + t0 + tile]
            P.op("dve", lambda e, x=xc, c=c: e.tensor_tensor(
                xn4[:, c * tile:(c + 1) * tile], x, pk2[:, 0:tile], ALU.subtract),
                [("pe", repv["m"]), ("pe", h_mm_prev), repv["dtail"]], "dve")
            d_xn_last = P.op("dve", lambda e, c=c: e.tensor_tensor(
                xn4[:, c * tile:(c + 1) * tile], xn4[:, c * tile:(c + 1) * tile],
                pk3[:, 0:tile], ALU.mult), [("pe", repv["r"])], "dve")
        bank_rel["m"] = ("dve", d_xn_last)
        bank_rel["r"] = ("dve", d_xn_last)

        d_h_last = 0
        for fc in range(NFC):
            pkg = pk4 if fc % 2 == 0 else pk7
            pku = pk2 if fc % 2 == 0 else pk3
            kwg, wwg = rb_wg.acquire()
            w_g = P.dma("sp", wgl[kwg][:], wgts[fc],
                        wwg + [("d_wst", P.cnt["d_wst"])] if False else
                        wwg + [("d_wst", wst_done)], sem="wgd%d" % kwg)
            kwu, wwu = rb_wu.acquire()
            w_u = P.dma("sp", wul[kwu][:], wuts[fc],
                        wwu + [("d_wst", wst_done)], sem="wud%d" % kwu)
            mmg = mmu = None
            for k in range(KC):
                mmg = P.op("pe", lambda e, k=k, fc=fc, pkg=pkg, kwg=kwg: e.matmul(
                    pkg[:, 0:tile], wgl[kwg][:, k * 128:(k + 1) * 128],
                    xn4[:, k * tile:(k + 1) * tile],
                    start=(k == 0), stop=(k == KC - 1)),
                    [("act", wt_done), ("dve", d_xn_last), ("wgd%d" % kwg, w_g)] +
                    ([grel[fc % 2]] if k == 0 else []),
                    "pe" if k == KC - 1 else None)
            rb_wg.release(kwg, "pe", mmg)
            for k in range(KC):
                mmu = P.op("pe", lambda e, k=k, fc=fc, pku=pku, kwu=kwu: e.matmul(
                    pku[:, 0:tile], wul[kwu][:, k * 128:(k + 1) * 128],
                    xn4[:, k * tile:(k + 1) * tile],
                    start=(k == 0), stop=(k == KC - 1)),
                    [("act", wt_done), ("dve", d_xn_last), ("wud%d" % kwu, w_u)] +
                    ([urel[fc % 2], bank_rel["m"], bank_rel["r"]] if k == 0 else []),
                    "pe" if k == KC - 1 else None)
            rb_wu.release(kwu, "pe", mmu)
            a_g = P.op("act", lambda e, pkg=pkg: e.activation(
                gball[:], pkg[:, 0:tile], AF.Silu, scale=1.0 / 127.0),
                [("pe", mmg), ("dve", d_h_last)], "act")
            grel[fc % 2] = ("act", a_g)
            a_u = P.op("act", lambda e, pku=pku: e.activation(
                uball[:], pku[:, 0:tile], AF.Copy),
                [("pe", mmu), ("dve", d_h_last)], "act")
            urel[fc % 2] = ("act", a_u)
            d_h_last = P.op("dve", lambda e, fc=fc: e.tensor_tensor(
                hball[:, fc * tile:(fc + 1) * tile], gball[:], uball[:], ALU.mult),
                [("act", a_g), ("act", a_u)], "dve")
        h_mm_prev = P.cnt["pe"]

        dv = None
        for dm in range(KC):
            pko = pk0 if dm % 2 == 0 else pk1
            mmo = None
            for fc in range(NFC):
                mmo = P.op("pe", lambda e, fc=fc, dm=dm, pko=pko: e.matmul(
                    pko[:, 0:tile], wdt[:, fc * D + dm * 128: fc * D + (dm + 1) * 128],
                    hball[:, fc * tile:(fc + 1) * tile],
                    start=(fc == 0), stop=(fc == NFC - 1)),
                    [("dve", d_h_last)] + ([orel[dm % 2]] if fc == 0 else []),
                    "pe" if fc == NFC - 1 else None)
            dv = P.op("dve", lambda e, dm=dm, pko=pko, t0=t0: e.scalar_tensor_tensor(
                xr[:, dm * T + t0: dm * T + t0 + tile], pko[:, 0:tile], s2,
                xr[:, dm * T + t0: dm * T + t0 + tile], ALU.mult, ALU.add),
                [("pe", mmo)], "dve")
            orel[dm % 2] = ("dve", dv)

    return



def build_fused(sim_safe=True, relax=("pe", "act", "dve", "pool"), phases="ABCD"):
    P = PB(sim_safe=sim_safe, relax=relax)
    sr = globals().get("_STRICT_RNG")
    if sr:
        P.strict_rng[sr[0]] = (sr[1], sr[2])
    x_in = P.dram("x_in", [D, T], F32, kind="ExternalInput")
    identf = P.dram("identf", [128, 128], F32, kind="ExternalInput")
    x_out = P.dram("x_out", [D, T], F32, kind="ExternalOutput")

    xr = P.sbuf("xr", [128, KC * T], F32)
    sid = P.sbuf("sid", [128, 128], F32)
    sidb = P.sbuf("sidb", [128, 128], BF16)
    onesb = P.sbuf("onesb", [128, 1], BF16)
    ones1b = P.sbuf("ones1b", [1, 128], BF16)

    P.dma("sp", xr[:].rearrange("p (c t) -> p c t", c=KC),
          x_in.rearrange("(c p) t -> p c t", p=128), sem="init0")
    P.dma("sp", sid[:], identf, sem="init0")
    w0 = P.cnt["init0"]
    P.op("dve", lambda e: e.tensor_copy(sidb[:], sid[:]), [("init0", w0)])
    P.op("dve", lambda e: e.memset(onesb[:], 1.0))
    d0 = P.op("dve", lambda e: e.memset(ones1b[:], 1.0), inc="dve")
    fw = [("init0", w0), ("dve", d0)]
    P.op("pe", lambda e: e.nop(), fw)
    P.op("act", lambda e: e.nop(), fw)
    P.op("dve", lambda e: e.engine_nop(), fw)
    P.op("pool", lambda e: e.engine_nop(), fw)
    P.op("sp", lambda e: e.nop(), fw)

    sh = dict(xr=xr, sid=sid, sidb=sidb, onesb=onesb, ones1b=ones1b)
    all_phases = (("pA_", emit_phase_A), ("pB_", emit_phase_B),
                  ("pC_", emit_phase_C), ("pD_", emit_phase_D))
    sel = phases
    for pfx, emit in all_phases:
        if pfx[1] not in sel:
            continue
        P.push_scope()
        P.name_prefix = pfx
        emit(P, sh)
        P.name_prefix = ""
        P.fence()
        P.pop_scope()

    fin = P.dma("sp", x_out.rearrange("(c p) t -> p c t", p=128),
                xr[:].rearrange("p (c t) -> p c t", c=KC), sem="fin")
    P.op("sp", lambda e: e.nop(), [("fin", fin)])
    return P


def _prep_inputs(inputs):
    f32 = np.float32
    w_in = np.asarray(inputs["L0_attn_w_in"], f32)
    pad = np.zeros((D, 96), f32)
    pad64 = np.zeros((D, 64), f32)
    pad128 = np.zeros((D, 128), f32)
    wA = np.ascontiguousarray(np.concatenate(
        [w_in[:, 0:4096], w_in[:, 4096:4128], pad,
         w_in[:, 4128:4192], pad64, w_in[:, 4192:4256], pad64, pad128], axis=1))
    assert wA.shape == (D, PROJ_PAD)
    s = np.arange(128)[:, None]
    t = np.arange(128)[None, :]
    seli = np.zeros((128, NH * 128), f32)
    for h in range(NH):
        seli[h, h * 128:(h + 1) * 128] = 1.0
    com = dict(
        A_wA=wA,
        A_convT=np.ascontiguousarray(np.asarray(inputs["L0_attn_conv_weight"], f32).T),
        A_cbT=np.ascontiguousarray(np.asarray(inputs["L0_attn_conv_bias"], f32).reshape(16, 128).T),
        A_nscT=np.ascontiguousarray(np.asarray(inputs["L0_attn_norm_scale"], f32).reshape(8, 128).T),
        A_alog=np.ascontiguousarray(np.asarray(inputs["L0_attn_A_log"], f32).reshape(NH, 1)),
        A_dtbias=np.ascontiguousarray(np.asarray(inputs["L0_attn_dt_bias"], f32).reshape(NH, 1)),
        A_wo_in=np.ascontiguousarray(np.asarray(inputs["L0_attn_w_out"], f32)),
        A_caus=(s <= t).astype(f32),
        A_seli=seli,
        B_W1=np.asarray(inputs["L0_ffn_W1"], f32),
        B_W2=np.asarray(inputs["L0_ffn_W2"], f32),
        B_b1T=np.ascontiguousarray(np.asarray(inputs["L0_ffn_b1"], f32).reshape(32, 128).T),
        B_b2T=np.ascontiguousarray(np.asarray(inputs["L0_ffn_b2"], f32).reshape(8, 128).T),
        C_Wq=np.asarray(inputs["L1_attn_Wq"], f32),
        C_Wk=np.asarray(inputs["L1_attn_Wk"], f32),
        C_Wv=np.asarray(inputs["L1_attn_Wv"], f32),
        C_Wg=np.asarray(inputs["L1_attn_Wg"], f32),
        C_Wo=np.asarray(inputs["L1_attn_Wo"], f32),
        C_bg=np.ascontiguousarray(np.asarray(inputs["L1_attn_bg"], f32).reshape(1, D)),
        D_wg=np.asarray(inputs["L1_ffn_w_gate"], np.int32),
        D_wu=np.asarray(inputs["L1_ffn_w_up"], np.int32),
        D_wd=np.asarray(inputs["L1_ffn_w_down"], np.int32),
        identf=np.eye(128, dtype=f32),
    )
    xs = np.asarray(inputs["L0_attn_x"], np.float32)
    maps = []
    for b in range(xs.shape[0]):
        m = dict(com)
        m["x_in"] = np.ascontiguousarray(xs[b].T)
        maps.append(m)
    return maps


def kernel(**inputs):
    maps = _prep_inputs(inputs)
    nc = build_fused().build()
    res = run_bass_kernel_spmd(nc, maps, core_ids=list(range(len(maps))))
    out = np.stack([np.ascontiguousarray(r["x_out"].T) for r in res.results])
    return out.astype(np.float32)


# version 09
# speedup vs baseline: 2.2337x; 1.0210x over previous
import numpy as np
import ml_dtypes
from contextlib import ExitStack

import concourse.bass as bass
import concourse.mybir as mybir
from concourse.bass_utils import run_bass_kernel_spmd
from concourse.alu_op_type import AluOpType as ALU

F32 = mybir.dt.float32
BF16 = mybir.dt.bfloat16
I32 = mybir.dt.int32
AF = mybir.ActivationFunctionType
AX = mybir.AxisListType
bfdt = ml_dtypes.bfloat16

D = 1024
T = 2048
DI = 2048
NH = 32
DS = 64
HD = 64
KC = 8
Q = 128
PROJ_PAD = 4608
N_CORES = 8

ENGS = ("sp", "act", "dve", "pool", "pe")
_ENG_ATTR = {"sp": "sync", "act": "scalar", "dve": "vector", "pool": "gpsimd", "pe": "tensor"}


class _RecInstr:
    def then_inc(self, sem, n):
        return None


class _Rec:

    def __init__(self):
        self.w = []
        self.r = []
        self.calls = []

    def __getattr__(self, nm):
        rec = self

        def call(*a, **kw):
            rec.calls.append((nm, a))
            aps = [v for v in list(a) + list(kw.values()) if _apish(v)]
            if aps:
                rec.w.append(aps[0])
                rec.r.extend(aps[1:])
            return _RecInstr()

        return call


def _apish(v):
    cn = type(v).__name__
    return cn == "AP" or cn.endswith("TensorHandle")


_TSET = {"Exp": "nle", "Ln": "nle", "Softplus": "nle",
         "Silu": "silu", "Gelu": "gelu", "Sigmoid": "sigmoid"}


def _ap_rect(x):
    cn = type(x).__name__
    if cn.endswith("TensorHandle"):
        return (x.name, 0, 1 << 30, 0, 1 << 30)
    t = x.tensor
    esz = mybir.dt.size(x.dtype)
    tsz = mybir.dt.size(t.dtype)
    wb = tsz
    for s in list(t.shape)[1:]:
        wb *= int(s)
    off_b = x.offset * esz
    plo, flo = off_b // wb, off_b % wb
    pspan = fspan = 0
    for stride, n in x.ap:
        ext = (int(n) - 1) * int(stride) * esz
        if ext == 0:
            continue
        st_b = int(stride) * esz
        if ext < 0:
            return (x.name, 0, 1 << 30, 0, 1 << 30)
        if st_b % wb == 0:
            pspan += ext // wb
        elif st_b < wb:
            fspan += ext
        else:
            return (x.name, 0, 1 << 30, 0, 1 << 30)
    if flo + fspan + esz > wb:
        return (x.name, plo, plo + pspan + (flo + fspan) // wb + 1, 0, wb)
    return (x.name, plo, plo + pspan + 1, flo, flo + fspan + esz)


class PB:

    def __init__(self, sim_safe=False, serial=False, relax=()):
        self.nc = bass.Bass("TRN2", target_bir_lowering=False)
        self.sim_safe = sim_safe
        self.serial = serial
        self.relax = frozenset(relax)
        self.st = ExitStack()
        self.scopes = []
        self.ops = {e: [] for e in ENGS}
        self.sems = {}
        self.cnt = {}
        self._acc = {e: {} for e in ("act", "dve", "pool")}
        self._bar = {}
        self._tbl = None
        self.strict_rng = {}
        for s in ("pe", "act", "dve", "pool"):
            self.mksem(s)

    def mksem(self, s):
        if s not in self.sems:
            self.sems[s] = self.st.enter_context(self.nc.semaphore(s))
            self.cnt[s] = 0

    def push_scope(self):
        self.scopes.append(ExitStack())

    def pop_scope(self):
        self.scopes.pop().close()

    def _stack(self):
        return self.scopes[-1] if getattr(self, "scopes", None) else self.st

    def sbuf(self, name, shape, dtype):
        name = getattr(self, "name_prefix", "") + name
        return self._stack().enter_context(self.nc.sbuf_tensor(name, shape, dtype))

    def psum(self, name, shape, dtype):
        name = getattr(self, "name_prefix", "") + name
        return self._stack().enter_context(self.nc.psum_tensor(name, shape, dtype))

    def fence(self, sp=True):
        w = [(s, self.cnt[s]) for s in list(self.cnt) if self.cnt[s] > 0]
        self.op("pe", lambda e: e.nop(), w)
        self.op("act", lambda e: e.nop(), w)
        self.op("dve", lambda e: e.engine_nop(), w)
        self.op("pool", lambda e: e.engine_nop(), w)
        if sp:
            self.op("sp", lambda e: e.nop(), w)
        self._bar = dict(self.cnt)
        for acc in self._acc.values():
            for nm in list(acc):
                kept = [e for e in acc[nm] if e[1] > self._bar.get(e[0], 0)]
                if kept:
                    acc[nm] = kept
                else:
                    del acc[nm]

    def dram(self, name, shape, dtype, kind=None):
        if kind is None:
            return self.nc.dram_tensor(name, shape, dtype).ap()
        return self.nc.dram_tensor(name, shape, dtype, kind=kind).ap()

    def op(self, eng, fn, waits=(), inc=None, n=1, track=True):
        waits = tuple(waits)
        if self.serial and eng in ("pe", "act", "dve", "pool"):
            if self.serial is True:
                tgt = ("pe", "act", "dve", "pool")
            else:
                tgt = self.serial if eng == "pe" else ()
            waits = waits + tuple((s, self.cnt[s]) for s in tgt if s != eng)
            if inc is None and (self.serial is True or eng in ("act", "dve", "pool")):
                inc = eng
        rects = None
        if self.sim_safe and eng in ("pe", "act", "dve", "pool"):
            rng = self.strict_rng.get(eng)
            if eng not in self.relax or (
                    rng is not None and rng[0] <= self.cnt[eng] < rng[1]):
                waits = waits + ((eng, self.cnt[eng]),)
            elif eng != "pe":
                try:
                    rec = _Rec()
                    fn(rec)
                    rects = ([(True, _ap_rect(a)) for a in rec.w]
                             + [(False, _ap_rect(a)) for a in rec.r])
                except Exception:
                    rects = None
                    waits = waits + ((eng, self.cnt[eng]),)
                else:
                    need = {}
                    acc = self._acc[eng]
                    for isw, (nm, pl, ph, fl, fh) in rects:
                        for (sem, c, oisw, opl, oph, ofl, ofh) in acc.get(nm, ()):
                            if ((isw or oisw) and pl < oph and opl < ph
                                    and fl < ofh and ofl < fh):
                                if c > need.get(sem, -1):
                                    need[sem] = c
                    if eng == "act":
                        for nm_c, a_c in rec.calls:
                            if nm_c != "activation" or len(a_c) < 3:
                                continue
                            f = str(a_c[2]).rsplit(".", 1)[-1]
                            ts = _TSET.get(f)
                            if ts is not None and ts != self._tbl:
                                self._tbl = ts
                                if self.cnt[eng] > need.get(eng, -1):
                                    need[eng] = self.cnt[eng]
                    for sem, c in sorted(need.items()):
                        if c > self._bar.get(sem, 0):
                            waits = waits + ((sem, c),)
            if inc is None:
                inc = eng
        v = None
        if inc is not None:
            self.cnt[inc] += n
            v = self.cnt[inc]
        if rects is not None:
            al = self._acc[eng]
            for isw, rect in rects:
                al.setdefault(rect[0], []).append((inc, v, isw) + rect[1:])
        self.ops[eng].append((waits, fn, inc, n))
        return v

    def dma(self, eng, dst, src, waits=(), sem=None, **kw):
        if sem is None:
            sem = "d_" + eng
        self.mksem(sem)
        return self.op(eng, lambda e: e.dma_start(dst, src, **kw), waits, sem, 16)

    def build(self):
        blk = self.st.enter_context(self.nc.Block())
        for e in ENGS:
            ops = self.ops[e]

            def body(eng, ops=ops):
                lastw = {}
                for waits, fn, inc, n in ops:
                    for s, v in waits:
                        if v > 0 and lastw.get(s, 0) < v:
                            eng.wait_ge(self.sems[s], v)
                            lastw[s] = v
                    instr = fn(eng)
                    if inc is not None:
                        assert instr is not None
                        instr.then_inc(self.sems[inc], n)

            getattr(blk, _ENG_ATTR[e])(body)
        self.st.close()
        return self.nc


class RB:

    def __init__(self, n):
        self.rel = [("pe", 0)] * n
        self.i = 0

    def acquire(self):
        k = self.i % len(self.rel)
        self.i += 1
        sem, v = self.rel[k]
        return k, [(sem, v)]

    def release(self, k, sem, v):
        self.rel[k] = (sem, v)



def emit_ln_stats(P, xr, t0, tile, xqb, rb_xq, statr, statb, onesb, ones1b,
                  pk_sum, pk_sq, pk_m, pk_r, bank_rel, pk_s=None, extra_waits=(),
                  rep_extra=()):
    mmS = mmQ = None
    for c in range(KC):
        src = xr[:, c * T + t0: c * T + t0 + tile]
        kb, w = rb_xq.acquire()
        a1 = P.op("act", lambda e, s=src, d=xqb[kb]: e.activation(d[:, 0:tile], s, AF.Copy),
                  list(w) + list(extra_waits), "act")
        mmS = P.op("pe", lambda e, d=xqb[kb], c=c: e.matmul(
            pk_sum[0:1, 0:tile], onesb[:], d[:, 0:tile],
            start=(c == 0), stop=(c == KC - 1)),
            [("act", a1)] + ([bank_rel["sum"]] if c == 0 else []), "pe")
        kb2, w2 = rb_xq.acquire()
        a2 = P.op("act", lambda e, s=src, d=xqb[kb2]: e.activation(d[:, 0:tile], s, AF.Square),
                  list(w2), "act")
        mmQ = P.op("pe", lambda e, d=xqb[kb2], c=c: e.matmul(
            pk_sq[0:1, 0:tile], onesb[:], d[:, 0:tile],
            start=(c == 0), stop=(c == KC - 1)),
            [("act", a2)] + ([bank_rel["sq"]] if c == 0 else []), "pe")
        rb_xq.release(kb, "pe", mmS)
        rb_xq.release(kb2, "pe", mmQ)

    def sl(i):
        return statr[0:1, i * tile:(i + 1) * tile]

    inv = 1.0 / D
    d1 = P.op("dve", lambda e: e.tensor_scalar(sl(0), pk_sum[0:1, 0:tile], inv, None,
                                               ALU.mult, ALU.bypass), [("pe", mmS)], "dve")
    d2 = P.op("dve", lambda e: e.tensor_scalar(sl(1), pk_sq[0:1, 0:tile], inv, None,
                                               ALU.mult, ALU.bypass), [("pe", mmQ)], "dve")
    bank_rel["sum"] = ("dve", d1)
    bank_rel["sq"] = ("dve", d2)
    P.op("dve", lambda e: e.tensor_tensor(sl(2), sl(0), sl(0), ALU.mult))
    d4 = P.op("dve", lambda e: e.tensor_tensor(sl(1), sl(1), sl(2), ALU.subtract),
              inc="dve")
    d4b = P.op("dve", lambda e: e.tensor_scalar(sl(2), sl(1), 1e-5, None,
                                                ALU.add, ALU.bypass), [("dve", d4)], "dve")
    a3 = P.op("act", lambda e: e.activation(sl(2), sl(2), AF.Ln),
              [("dve", d4b)], "act")
    a4 = P.op("act", lambda e: e.activation(sl(3), sl(2), AF.Exp, scale=-0.5),
              [("act", a3)], "act")
    have_s = pk_s is not None
    if have_s:
        d5 = P.op("dve", lambda e: e.tensor_scalar(sl(4), sl(1), 1e-5, None,
                                                   ALU.add, ALU.bypass), [("dve", d4)], "dve")
        P.op("dve", lambda e: e.reciprocal(sl(4), sl(4)))
        P.op("dve", lambda e: e.tensor_tensor(sl(4), sl(1), sl(4), ALU.mult))
        d6 = P.op("dve", lambda e: e.tensor_scalar(sl(4), sl(4), 1e-6, None,
                                                   ALU.add, ALU.bypass), inc="dve")
        a5 = P.op("act", lambda e: e.activation(sl(4), sl(4), AF.Ln), [("dve", d6)], "act")
        a6 = P.op("act", lambda e: e.activation(sl(4), sl(4), AF.Exp, scale=-0.5),
                  inc="act")
        d7 = P.op("dve", lambda e: e.tensor_tensor(sl(5), sl(3), sl(4), ALU.mult),
                  [("act", a6)], "dve")

    def sb(i):
        return statb[0:1, i * tile:(i + 1) * tile]

    reps = [("m", sl(0), 0, pk_m, d1, 6), ("r", sl(3), 2, pk_r, a4, 7)]
    if have_s:
        reps.append(("s", sl(5), 4, pk_s, d7, 2))
    out = {}
    for name, row, bi, pk, wv, tmpi in reps:
        wsem = "dve" if name in ("m", "s") else "act"
        ah = P.op("act", lambda e, r=row, o=sb(bi): e.activation(o, r, AF.Copy),
                  [(wsem, wv)], "act")
        dl = P.op("dve", lambda e, r=row, h=sb(bi), o=sl(tmpi): e.tensor_tensor(o, r, h, ALU.subtract),
                  [("act", ah)], "dve")
        al = P.op("act", lambda e, o=sb(bi + 1), r=sl(tmpi): e.activation(o, r, AF.Copy),
                  [("dve", dl)], "act")
        m1 = P.op("pe", lambda e, pk=pk, h=sb(bi): e.matmul(
            pk, ones1b[:], h, start=True, stop=False),
            [("act", ah), bank_rel[name]] + list(rep_extra), None)
        m2 = P.op("pe", lambda e, pk=pk, l=sb(bi + 1): e.matmul(
            pk, ones1b[:], l, start=False, stop=True),
            [("act", al)], "pe")
        out[name] = m2
    out["dtail"] = ("dve", P.cnt["dve"])
    return out



def emit_phase_A(P, sh, tile=256):
    ntiles = T // tile
    qpt = tile // Q
    upto = 'full'; max_tiles = None; scan_heads = NH; no_ops = ()
    xr = sh["xr"]; sid = sh["sid"]; sidb = sh["sidb"]
    onesb = sh["onesb"]; ones1b = sh["ones1b"]

    wA = P.dram("A_wA", [D, PROJ_PAD], F32, kind="ExternalInput")
    convT = P.dram("A_convT", [DI, 4], F32, kind="ExternalInput")
    cbT = P.dram("A_cbT", [128, 16], F32, kind="ExternalInput")
    nscT = P.dram("A_nscT", [128, 8], F32, kind="ExternalInput")
    alog = P.dram("A_alog", [NH, 1], F32, kind="ExternalInput")
    dtbias = P.dram("A_dtbias", [NH, 1], F32, kind="ExternalInput")
    wo_in = P.dram("A_wo_in", [DI, D], F32, kind="ExternalInput")
    caus = P.dram("A_caus", [128, 128], F32, kind="ExternalInput")
    seli = P.dram("A_seli", [128, NH * 128], F32, kind="ExternalInput")
    wab = P.dram("A_wab_scr", [D, PROJ_PAD], BF16)
    wob = P.dram("A_wob_scr", [DI, D], BF16)
    dbg_hooks = set()

    xnb = P.sbuf("xnb", [128, KC * tile], BF16)
    wbuf = [P.sbuf(f"wbuf{i}", [128, KC * 256], BF16) for i in range(2)]
    wobuf = [P.sbuf(f"wobuf{i}", [128, 16 * 128], BF16) for i in range(2)]
    sgate = P.sbuf("sgate", [128, qpt * DI], BF16)
    sbm = P.sbuf("sbm", [128, 16 * (tile + 3)], BF16)
    hbuf = P.sbuf("hbuf", [128, 16 * tile], BF16)
    hT = [P.sbuf(f"hT{i}", [128, DI], BF16) for i in range(2)]
    ybuf = [P.sbuf(f"ybuf{i}", [128, DI], BF16) for i in range(2)]
    ygT = P.sbuf("ygT", [128, 16 * tile], BF16)
    state = P.sbuf("state", [64, NH * HD], F32)
    stateb = P.sbuf("stateb", [128, NH * HD], BF16)
    Bq = P.sbuf("Bq", [128, tile], BF16)
    Cq = P.sbuf("Cq", [128, tile], BF16)
    dtraw = P.sbuf("dtraw", [32, tile], F32)
    lnb = [P.sbuf(f"lnb{i}", [128, tile], F32) for i in range(2)]
    t1b = [P.sbuf(f"t1b{i}", [128, tile], F32) for i in range(2)]
    cvt = [P.sbuf(f"cvt{i}", [128, tile], BF16) for i in range(2)]
    xqb = [P.sbuf(f"xqb{i}", [128, tile], BF16) for i in range(4)]
    statr = P.sbuf("statr", [1, 8 * tile], F32)
    statb = P.sbuf("statb", [1, 6 * tile], BF16)
    dtv = P.sbuf("dtv", [32, 128], F32)
    Lbuf = P.sbuf("Lbuf", [32, 128], F32)
    edec = P.sbuf("edec", [32, 128], F32)
    dtrow = P.sbuf("dtrow", [32, 128], F32)
    eLend = P.sbuf("eLend", [32, 1], F32)
    Lhi = P.sbuf("Lhi", [128, 128], BF16)
    Llo = P.sbuf("Llo", [128, 128], BF16)
    Ltmp = P.sbuf("Ltmp", [32, 128], F32)
    tev = P.sbuf("tev", [128, 128], F32)
    negLT = P.sbuf("negLT", [128, 32], F32)
    expLT = P.sbuf("expLT", [128, 32], F32)
    Gs = P.sbuf("Gs", [128, 128], BF16)
    Gm = [P.sbuf(f"Gm{i}", [128, 128], BF16) for i in range(2)]
    Dc = [P.sbuf(f"Dc{i}", [128, 128], F32) for i in range(4)]
    Eb = [P.sbuf(f"Eb{i}", [128, 128], F32) for i in range(4)]
    Mb = [P.sbuf(f"Mb{i}", [128, 128], BF16) for i in range(4)]
    XdTb = [P.sbuf(f"XdTb{i}", [128, 64], BF16) for i in range(4)]
    XdT2b = [P.sbuf(f"XdT2b{i}", [128, 64], BF16) for i in range(4)]
    BTs = P.sbuf("BTs", [128, 128], BF16)
    scaus = P.sbuf("scaus", [128, 128], BF16)
    sselb = P.sbuf("sselb", [128, NH * 128], BF16)
    scw = P.sbuf("scw", [128, 16 * 4], F32)
    scb = P.sbuf("scb", [128, 16], F32)
    snsc = P.sbuf("snsc", [128, 8], F32)
    salog = P.sbuf("salog", [32, 1], F32)
    sdtb = P.sbuf("sdtb", [32, 1], F32)
    negeA = P.sbuf("negeA", [32, 1], F32)
    ones32 = P.sbuf("ones32", [32, 128], F32)
    zero32 = P.sbuf("zero32", [32, 1], F32)

    pk0 = P.psum("pk0", [128, 512], F32)
    pk1 = P.psum("pk1", [128, 512], F32)
    pk2 = P.psum("pk2", [128, 512], F32)
    pk3 = P.psum("pk3", [128, 512], F32)
    pk4 = P.psum("pk4", [128, 512], F32)
    pk5 = P.psum("pk5", [128, 1024], BF16)
    pk6 = P.psum("pk6", [128, 1024], BF16)
    pk7 = P.psum("pk7", [128, 512], F32)

    P.dma("sp", scw[:].rearrange("p (c k) -> p c k", c=16),
          convT.rearrange("(c p) k -> p c k", p=128), sem="initA")
    P.dma("sp", scb[:], cbT, sem="initA")
    P.dma("sp", snsc[:], nscT, sem="initA")
    P.dma("sp", salog[:], alog, sem="initA")
    P.dma("sp", sdtb[:], dtbias, sem="initA")
    w_init = P.cnt["initA"]
    w_wab = P.dma("pool", wab, wA, max_dma_last_dim=8176, sem="cast_a")
    w_wob = P.dma("pool", wob, wo_in, max_dma_last_dim=8176, sem="cast_b")
    w_caus = P.dma("pool", scaus[:], caus, sem="cast_c")
    w_sel = P.dma("pool", sselb[:], seli, sem="cast_d")

    P.op("dve", lambda e: e.memset(ones32[:], 1.0))
    P.op("dve", lambda e: e.memset(zero32[:], 0.0))
    P.op("dve", lambda e: e.memset(state[:], 0.0))
    P.op("dve", lambda e: e.memset(BTs[:], 0.0))
    P.op("dve", lambda e: e.memset(Bq[:], 0.0))
    P.op("dve", lambda e: e.memset(Cq[:], 0.0))
    P.op("dve", lambda e: e.memset(Lhi[:], 0.0))
    P.op("dve", lambda e: e.memset(Llo[:], 0.0))
    d_init = P.op("dve", lambda e: e.memset(stateb[:], 0.0), inc="dve")
    p_halo = P.op("pool", lambda e: e.memset(sbm[:], 0.0), inc="pool")
    a_eA = P.op("act", lambda e: e.activation(negeA[:], salog[:], AF.Exp),
                [("initA", w_init)], "act")
    d_eA = P.op("dve", lambda e: e.tensor_scalar(negeA[:], negeA[:], -1.0, None,
                                                 ALU.mult, ALU.bypass),
                [("act", a_eA)], "dve")

    fence_w = [("initA", w_init), ("dve", d_init), ("pool", p_halo),
               ("cast_c", w_caus), ("cast_d", w_sel), ("act", a_eA),
               ("dve", d_eA)]
    P.op("pe", lambda e: e.nop(), fence_w)
    P.op("act", lambda e: e.nop(), fence_w)
    P.op("dve", lambda e: e.engine_nop(), fence_w)
    P.op("pool", lambda e: e.engine_nop(), fence_w)

    rb_xq = RB(4)
    rb_t1 = RB(2)
    rb_ln = RB(2)
    rb_cvt = RB(2)
    rb_w = RB(2)
    rb_wo = RB(2)
    yrel = [("act", 0)] * 4
    yrel2 = [("dve", 0)] * 4
    srel = [("dve", 0)] * 4
    erel = [("dve", 0)] * 2
    trel = ("act", 0)
    bf16rel = [("act", 0)] * 2
    bf16turn = [0]
    Grel = ("act", 0)
    hTrel = [("pool", 0)] * 2
    ybrel = [("act", 0)] * 2
    gmrel = ("pool", 0)
    gsrel = ("pool", 0)
    tev_dve_rel = 0
    tev_pool_rel = 0
    aE_prev = [("act", 0)] * 4
    pm_prev = [("pool", 0)] * 4
    y2_prev = [("pe", 0)] * 4
    xd2_prev = [("dve", 0)] * 4
    s_prev = [("pe", 0)] * 4
    bank_rel = {"sum": ("dve", 0), "sq": ("dve", 0), "m": ("dve", 0),
                "r": ("dve", 0), "s": ("dve", 0)}
    sbm_cons = [("dve", 0), ("pool", 0)]
    prev_state_b = [("dve", d_init)] * NH
    last_wout_dve = 0
    statefin = []

    for i in range(ntiles if max_tiles is None else max_tiles):
        t0 = i * tile

        repv = emit_ln_stats(P, xr, t0, tile, xqb, rb_xq, statr, statb, onesb,
                             ones1b, pk0, pk1, pk2[:, 0:tile], pk3[:, 0:tile],
                             bank_rel, pk_s=pk2[:, 256:256 + tile],
                             extra_waits=[("initA", w_init),
                                          ("dve", last_wout_dve)],
                             rep_extra=[trel, Grel, erel[0], erel[1]])

        p_x_last = 0
        for c in range(KC):
            xc = xr[:, c * T + t0: c * T + t0 + tile]
            kt, wt = rb_t1.acquire()
            d_t1 = P.op("dve", lambda e, x=xc, kt=kt: e.tensor_tensor(
                t1b[kt][:], x, pk2[:, 0:tile], ALU.subtract),
                wt + [("pe", repv["m"]), ("pe", repv["s"]),
                      ("dve", last_wout_dve), repv["dtail"]], "dve")
            kl, wl = rb_ln.acquire()
            d_ln = P.op("dve", lambda e, kl=kl, kt=kt: e.tensor_tensor(
                lnb[kl][:], t1b[kt][:], pk3[:, 0:tile], ALU.mult),
                wl + [("pe", repv["r"])], "dve")
            p_x = P.op("pool", lambda e, x=xc, kl=kl: e.tensor_tensor(
                x, x, lnb[kl][:], ALU.add), [("dve", d_ln)], "pool")
            rb_ln.release(kl, "pool", p_x)
            d_xn = P.op("dve", lambda e, kt=kt, c=c: e.scalar_tensor_tensor(
                xnb[:, c * tile:(c + 1) * tile], t1b[kt][:], snsc[:, c:c + 1],
                pk2[:, 256:256 + tile], ALU.mult, ALU.mult),
                [("pe", repv["s"])], "dve")
            rb_t1.release(kt, "dve", d_xn)
            p_x_last = p_x
        d_xn_last = P.cnt["dve"]
        bank_rel["m"] = ("dve", d_xn_last)
        bank_rel["r"] = ("dve", d_xn_last)
        bank_rel["s"] = ("dve", d_xn_last)


        if upto == 'ln':
            continue
        if i > 0:
            for c in range(16):
                off = c * (tile + 3)
                p_halo = P.op("pool", lambda e, off=off: e.tensor_copy(
                    sbm[:, off:off + 3], sbm[:, off + tile:off + tile + 3]),
                    sbm_cons if c == 0 else [], "pool")

        ssm_evac = {}
        dt_evac = B_evac = C_evac = None
        for j in range(8, 18):
            kw, ww = rb_w.acquire()
            w_sl = P.dma("sp", wbuf[kw][:].rearrange("p (k f) -> p k f", k=KC),
                         wA_slab_ap(wab, j), ww + [("cast_a", w_wab)],
                         sem="wd%d" % kw)
            mmv = None
            for sub in range(2):
                fc = j * 2 + sub
                if fc >= 35:
                    continue
                pk, bk = (pk0, "sum") if (fc % 2 == 0) else (pk1, "sq")
                for k in range(KC):
                    mmv = P.op("pe", lambda e, kw=kw, k=k, sub=sub, pk=pk: e.matmul(
                        pk[:, 0:tile],
                        wbuf[kw][:, k * 256 + sub * 128: k * 256 + sub * 128 + 128],
                        xnb[:, k * tile:(k + 1) * tile],
                        start=(k == 0), stop=(k == KC - 1)),
                        [("wd%d" % kw, w_sl), ("dve", d_xn_last)] +
                        ([bank_rel[bk]] if k == 0 else []),
                        "pe" if k == KC - 1 else None)
                if fc <= 31:
                    c = fc - 16
                    off = c * (tile + 3)
                    av = P.op("act", lambda e, pk=pk, off=off: e.activation(
                        sbm[:, off + 3:off + 3 + tile], pk[:, 0:tile], AF.Copy),
                        [("pe", mmv), ("pool", p_halo)], "act")
                    ssm_evac[c] = av
                elif fc == 32:
                    av = dt_evac = P.op("act", lambda e, pk=pk: e.activation(
                        dtraw[:, 0:tile], pk[0:32, 0:tile], AF.Copy),
                        [("pe", mmv)], "act")
                elif fc == 33:
                    av = B_evac = P.op("act", lambda e, pk=pk: e.activation(
                        Bq[0:64, 0:tile], pk[0:64, 0:tile], AF.Copy),
                        [("pe", mmv)], "act")
                else:
                    av = C_evac = P.op("act", lambda e, pk=pk: e.activation(
                        Cq[0:64, 0:tile], pk[0:64, 0:tile], AF.Copy),
                        [("pe", mmv)], "act")
                bank_rel[bk] = ("act", av)
            rb_w.release(kw, "pe", mmv)

        gate_evac = {}
        gbank = {0: (pk0, "sum"), 1: (pk1, "sq")}
        for ns in range(4):
            mm_tc = {0: None, 1: None}
            for jj in range(2):
                j = ns * 2 + jj
                kw, ww = rb_w.acquire()
                w_sl = P.dma("sp", wbuf[kw][:].rearrange("p (k f) -> p k f", k=KC),
                             wA_slab_ap(wab, j), ww + [("cast_a", w_wab)],
                             sem="wd%d" % kw)
                for tc in range(qpt):
                    pk, bk = gbank[tc]
                    for k in range(KC):
                        mm_tc[tc] = P.op("pe", lambda e, kw=kw, k=k, jj=jj, tc=tc, pk=pk: e.matmul(
                            pk[:, jj * 256:(jj + 1) * 256],
                            xnb[:, k * tile + tc * 128: k * tile + tc * 128 + 128],
                            wbuf[kw][:, k * 256:(k + 1) * 256],
                            start=(k == 0), stop=(k == KC - 1)),
                            [("wd%d" % kw, w_sl), ("dve", d_xn_last)] +
                            ([bank_rel[bk]] if (k == 0 and jj == 0) else []),
                            "pe" if k == KC - 1 else None)
                rb_w.release(kw, "pe", mm_tc[qpt - 1])
            for tc in range(qpt):
                pk, bk = gbank[tc]
                av = P.op("act", lambda e, pk=pk, tc=tc, ns=ns: e.activation(
                    sgate[:, tc * DI + ns * 512: tc * DI + (ns + 1) * 512],
                    pk[:, 0:512], AF.Silu), [("pe", mm_tc[tc])], "act")
                gate_evac[(tc, ns)] = av
                bank_rel[bk] = ("act", av)

        if upto == 'proj':
            continue
        h_evac = {}
        for c in range(16):
            eng = "dve"
            kc_, wc = rb_cvt.acquire()
            off = c * (tile + 3)
            waits = wc + [("act", ssm_evac[c]), ("pool", p_halo),
                          ("initA", w_init)]
            P.op(eng, lambda e, kc_=kc_, c=c, off=off: e.tensor_scalar(
                cvt[kc_][:], sbm[:, off:off + tile], scw[:, c * 4 + 0:c * 4 + 1],
                scb[:, c:c + 1], ALU.mult, ALU.add), waits)
            civ = None
            for k in (1, 2, 3):
                civ = P.op(eng, lambda e, kc_=kc_, c=c, k=k, off=off: e.scalar_tensor_tensor(
                    cvt[kc_][:], sbm[:, off + k:off + k + tile],
                    scw[:, c * 4 + k:c * 4 + k + 1], cvt[kc_][:],
                    ALU.mult, ALU.add), inc=(eng if k == 3 else None))
            av = P.op("act", lambda e, kc_=kc_, c=c: e.activation(
                hbuf[:, c * tile:(c + 1) * tile], cvt[kc_][:], AF.Silu),
                [(eng, civ)], "act")
            h_evac[c] = av
            rb_cvt.release(kc_, "act", av)
        h_all = P.cnt["act"]


        if upto == 'conv':
            continue
        yrel[2] = yrel2[2] = srel[2] = bank_rel["sum"]
        yrel[3] = yrel2[3] = srel[3] = bank_rel["sq"]
        for q in range(qpt):
            g = i * qpt + q
            dtq = dtraw[:, q * Q:(q + 1) * Q]
            a_abs = P.op("act", lambda e, s=dtq: e.activation(
                Ltmp[:], s, AF.Abs, bias=sdtb[:, 0:1]),
                [("act", dt_evac), ("initA", w_init), ("dve", P.cnt["dve"])], "act")
            a_e = P.op("act", lambda e: e.activation(Ltmp[:], Ltmp[:], AF.Exp,
                                                     scale=-1.0), inc="act")
            a_l = P.op("act", lambda e: e.activation(Ltmp[:], Ltmp[:], AF.Ln,
                                                     bias=1.0), inc="act")
            d_r = P.op("dve", lambda e, s=dtq: e.tensor_scalar(
                dtv[:], s, sdtb[:, 0:1], 0.0, ALU.add, ALU.max),
                [("act", dt_evac), ("pe", P.cnt["pe"])], "dve")
            d_dt = P.op("dve", lambda e: e.tensor_tensor(
                dtv[:], dtv[:], Ltmp[:], ALU.add),
                [("act", a_l)], "dve")
            d_a = P.op("dve", lambda e: e.tensor_scalar(
                Ltmp[:], dtv[:], negeA[:, 0:1], None, ALU.mult, ALU.bypass),
                [("dve", d_eA)], "dve")
            d_L = P.op("dve", lambda e: e.tensor_tensor_scan(
                Lbuf[:], ones32[:], Ltmp[:], zero32[:, 0:1],
                ALU.mult, ALU.add), [("dve", d_a), ("pe", P.cnt["pe"])], "dve")
            a_ed = P.op("act", lambda e: e.activation(
                edec[:], Lbuf[:], AF.Exp, scale=-1.0,
                bias=Lbuf[:, 127:128]), [("dve", d_L)], "act")
            a_dt0 = P.op("act", lambda e: e.activation(
                eLend[:], Lbuf[:, 127:128], AF.Exp), [("dve", d_L)], "act")
            d_dtr = P.op("dve", lambda e: e.tensor_scalar(
                dtrow[:], ones32[:], eLend[:, 0:1], None, ALU.mult, ALU.bypass),
                [("act", a_dt0), ("pe", P.cnt["pe"])], "dve")
            a_hi = P.op("act", lambda e: e.activation(Lhi[0:32, :], Lbuf[:], AF.Copy),
                        [("dve", d_L), ("pe", P.cnt["pe"])], "act")
            d_lo = P.op("dve", lambda e: e.tensor_tensor(
                Ltmp[:], Lbuf[:], Lhi[0:32, :], ALU.subtract), [("act", a_hi)], "dve")
            a_lo = P.op("act", lambda e: e.activation(Llo[0:32, :], Ltmp[:], AF.Copy),
                        [("dve", d_lo)], "act")
            id32 = sid[0:32, 0:32]
            pe_t1 = P.op("pe", lambda e: e.transpose(pk3[:, 256:288], dtv[:], id32),
                         [("dve", d_dt), ("dve", d_a), trel, Grel, erel[1],
                          ("initA", w_init)], None)
            pe_t1 = P.op("pe", lambda e: e.transpose(pk3[:, 288:320], Lbuf[:], id32),
                         [("dve", d_L), ("act", a_hi), ("dve", d_lo)], None)
            pe_t1 = P.op("pe", lambda e: e.transpose(pk3[:, 320:352], edec[:], id32),
                         [("act", a_ed)], None)
            pe_t1 = P.op("pe", lambda e: e.transpose(pk3[:, 352:384], dtrow[:], id32),
                         [("dve", d_dtr)], "pe")
            a_tev = P.op("act", lambda e: e.activation(tev[:], pk3[:, 256:384], AF.Copy),
                         [("pe", pe_t1), ("dve", tev_dve_rel), ("pool", tev_pool_rel)],
                         "act")
            trel = ("act", a_tev)
            a_dtotT = a_tev
            d_nL = P.op("dve", lambda e: e.tensor_scalar(
                negLT[:], tev[:, 32:64], -1.0, None, ALU.mult, ALU.bypass),
                [("act", a_tev)], "dve")
            a_eL = P.op("act", lambda e: e.activation(expLT[:], tev[:, 32:64], AF.Exp),
                        [("act", a_tev)], "act")
            bt = bf16turn[0] % 2
            bf16turn[0] += 1
            pkb = pk5 if bt == 0 else pk6
            pe_t3 = P.op("pe", lambda e, q=q, pkb=pkb: e.transpose(
                pkb[:, 0:64], Bq[0:64, q * Q:(q + 1) * Q], sidb[0:64, 0:64]),
                [("act", B_evac), bf16rel[bt]], "pe")
            a_BT = P.op("act", lambda e, pkb=pkb: e.activation(
                BTs[:, 0:64], pkb[:, 0:64], AF.Copy), [("pe", pe_t3)], "act")
            bf16rel[bt] = ("act", a_BT)
            pe_G = P.op("pe", lambda e, q=q: e.matmul(
                pk3[:, 384:512], Bq[:, q * Q:(q + 1) * Q], Cq[:, q * Q:(q + 1) * Q],
                start=True, stop=True),
                [("act", B_evac), ("act", C_evac), Grel, ("act", a_tev),
                 erel[1]], "pe")
            a_G = P.op("act", lambda e: e.activation(Gs[:], pk3[:, 384:512], AF.Copy),
                       [("pe", pe_G), gsrel], "act")
            Grel = ("act", a_G)
            kh = g % 2
            hT_ev = 0
            for c in range(16):
                bt = bf16turn[0] % 2
                bf16turn[0] += 1
                pkb = pk5 if bt == 0 else pk6
                pe_th = P.op("pe", lambda e, c=c, q=q, pkb=pkb: e.transpose(
                    pkb[:, 128:256],
                    hbuf[:, c * tile + q * Q: c * tile + q * Q + 128], sidb[:]),
                    [("act", h_evac[c]), bf16rel[bt]] +
                    ([hTrel[kh]] if c == 0 else []), "pe")
                av = P.op("act", lambda e, c=c, kh=kh, pkb=pkb: e.activation(
                    hT[kh][:, c * 128:(c + 1) * 128],
                    pkb[:, 128:256], AF.Copy),
                    [("pe", pe_th)], "act")
                bf16rel[bt] = ("act", av)
                hT_ev = av
            if scan_heads == 0:
                continue
            p_gmc = P.op("pool", lambda e: e.tensor_tensor(
                Gm[0][:], Gs[:], scaus[:], ALU.mult),
                [("act", a_G), ("cast_c", w_caus), gmrel], "pool")
            gsrel = ("pool", p_gmc)
            d_yg_last = 0
            G4 = 4
            for hg in range(0, scan_heads, G4):
                hs = list(range(hg, min(hg + G4, scan_heads)))
                st = {h: {} for h in hs}
                for h in hs:
                    ks = h % 4
                    pke = pk2 if h % 2 == 0 else pk3
                    P.op("pe", lambda e, h=h, pke=pke: e.matmul(
                        pke[:, 0:128],
                        sselb[:, h * 128:(h + 1) * 128], Lhi[:],
                        start=True, stop=False),
                        [("act", a_hi), ("cast_d", w_sel), erel[h % 2],
                         ("dve", d_xn_last)] +
                        ([("act", a_tev), ("act", a_G)] if h % 2 == 1 else []),
                        None)
                    pe_rep = P.op("pe", lambda e, h=h, pke=pke: e.matmul(
                        pke[:, 0:128],
                        sselb[:, h * 128:(h + 1) * 128], Llo[:],
                        start=False, stop=True), [("act", a_lo)], "pe")
                    d_dc = P.op("dve", lambda e, ks=ks, h=h, pke=pke: e.tensor_scalar(
                        Dc[ks][:], pke[:, 0:128],
                        negLT[:, h:h + 1], 0.0, ALU.add, ALU.min),
                        [("pe", pe_rep), ("dve", d_nL), aE_prev[ks]], "dve")
                    erel[h % 2] = ("dve", d_dc)
                    a_E = P.op("act", lambda e, ks=ks: e.activation(
                        Eb[ks][:], Dc[ks][:], AF.Exp),
                        [("dve", d_dc), pm_prev[ks]], "act")
                    aE_prev[ks] = ("act", a_E)
                for h in hs:
                    ks = h % 4
                    p_m = P.op("pool", lambda e, ks=ks: e.tensor_tensor(
                        Mb[ks][:], Eb[ks][:], Gm[0][:], ALU.mult),
                        [aE_prev[ks], ("pool", p_gmc), y2_prev[ks]], "pool")
                    pm_prev[ks] = ("pool", p_m)
                    p_xd = P.op("dve", lambda e, ks=ks, h=h, kh=kh: e.tensor_scalar(
                        XdTb[ks][:], hT[kh][:, h * 64:(h + 1) * 64],
                        tev[:, h:h + 1], None, ALU.mult, ALU.bypass),
                        [("act", hT_ev), ("act", a_tev), y2_prev[ks],
                         xd2_prev[ks]], "dve")
                    p_xd2 = P.op("dve", lambda e, ks=ks, h=h: e.tensor_scalar(
                        XdT2b[ks][:], XdTb[ks][:], tev[:, 64 + h:64 + h + 1],
                        None, ALU.mult, ALU.bypass),
                        [("dve", p_xd), s_prev[ks]], "dve")
                    xd2_prev[ks] = ("dve", p_xd2)
                    st[h].update(p_m=p_m, p_xd=p_xd, p_xd2=p_xd2)
                for h in hs:
                    ks = h % 4
                    par = h % 4
                    pkp = (pk4, pk7, pk0, pk1)[par]
                    pe_yi = P.op("pe", lambda e, h=h, q=q, pkp=pkp: e.matmul(
                        pkp[:, 0:64], Cq[:, q * Q:(q + 1) * Q],
                        stateb[:, h * 64:(h + 1) * 64], start=True, stop=True),
                        [("act", C_evac), prev_state_b[h], yrel[par],
                         yrel2[par], srel[par]], "pe")
                    pe_y2 = P.op("pe", lambda e, ks=ks, pkp=pkp: e.matmul(
                        pkp[:, 64:128], Mb[ks][:],
                        XdTb[ks][:], start=True, stop=True),
                        [("pool", st[h]["p_m"]), ("dve", st[h]["p_xd"])], "pe")
                    y2_prev[ks] = ("pe", pe_y2)
                    pe_s = P.op("pe", lambda e, ks=ks, pkp=pkp: e.matmul(
                        pkp[:, 128:192], BTs[:],
                        XdT2b[ks][:], start=True, stop=True),
                        [("dve", st[h]["p_xd2"]), ("act", a_BT)], "pe")
                    st[h]["pe_s"] = pe_s
                for h in hs:
                    par = h % 4
                    pkp = (pk4, pk7, pk0, pk1)[par]
                    pe_s = st[h]["pe_s"]
                    a_yi = P.op("act", lambda e, h=h, kh=kh, pkp=pkp: e.activation(
                        ybuf[kh][:, h * 64:(h + 1) * 64],
                        pkp[:, 0:64], AF.Copy,
                        scale=expLT[:, h:h + 1]),
                        [("pe", pe_s), ("act", a_eL)] +
                        ([ybrel[kh]] if h == 0 else []), "act")
                    yrel[par] = ("act", a_yi)
                    d_yg = P.op("dve", lambda e, h=h, kh=kh, pkp=pkp: e.tensor_tensor(
                        ybuf[kh][:, h * 64:(h + 1) * 64],
                        ybuf[kh][:, h * 64:(h + 1) * 64],
                        pkp[:, 64:128], ALU.add),
                        [("pe", pe_s), ("act", a_yi)], "dve")
                    yrel2[par] = ("dve", d_yg)
                    d_yg_last = d_yg
                    d_st = P.op("dve", lambda e, h=h, pkp=pkp: e.scalar_tensor_tensor(
                        state[:, h * 64:(h + 1) * 64],
                        state[:, h * 64:(h + 1) * 64],
                        tev[0:64, 96 + h:97 + h],
                        pkp[0:64, 128:192],
                        ALU.mult, ALU.add), [("pe", pe_s), ("act", a_dtotT)], "dve")
                    srel[par] = ("dve", d_st)
                    p_stb = P.op("pool", lambda e, h=h: e.tensor_copy(
                        stateb[0:64, h * 64:(h + 1) * 64],
                        state[:, h * 64:(h + 1) * 64]),
                        [("dve", d_st)], "pool")
                    prev_state_b[h] = ("pool", p_stb)
            hTrel[kh] = ("dve", P.cnt["dve"])
            gmrel = ("pool", P.cnt["pool"])
            tev_dve_rel = P.cnt["dve"]
            tev_pool_rel = P.cnt["pool"]

            d_gate = P.op("dve", lambda e, kh=kh, q=q: e.tensor_tensor(
                ybuf[kh][:], ybuf[kh][:], sgate[:, q * DI:(q + 1) * DI], ALU.mult),
                [("dve", d_yg_last)] +
                [("act", gate_evac[(q, ns)]) for ns in range(4)], "dve")
            d_yg_last = d_gate

            ygT_ev = 0
            for c in range(16):
                bt = bf16turn[0] % 2
                bf16turn[0] += 1
                pkb = pk5 if bt == 0 else pk6
                pe_ty = P.op("pe", lambda e, c=c, kh=kh, pkb=pkb: e.transpose(
                    pkb[:, 128:256],
                    ybuf[kh][:, c * 128:(c + 1) * 128], sidb[:]),
                    [("dve", d_yg_last), bf16rel[bt]], "pe")
                av = P.op("act", lambda e, c=c, q=q, pkb=pkb: e.activation(
                    ygT[:, c * tile + q * Q: c * tile + q * Q + 128],
                    pkb[:, 128:256], AF.Copy),
                    [("pe", pe_ty)], "act")
                bf16rel[bt] = ("act", av)
                ygT_ev = av
            ybrel[kh] = ("act", ygT_ev)

        sbm_cons = [("dve", P.cnt["dve"]), ("pool", P.cnt["pool"])]
        if upto == 'scan':
            continue
        bank_rel["sum"] = srel[2]
        bank_rel["sq"] = srel[3]

        for dm in range(KC):
            kw, ww = rb_wo.acquire()
            w_sl = P.dma("sp", wobuf[kw][:].rearrange("p (k d) -> p k d", k=16),
                         wo_slab_ap(wob, dm), ww + [("cast_b", w_wob)],
                         sem="wod%d" % kw)
            pk, bk = (pk0, "sum") if (dm % 2 == 0) else (pk1, "sq")
            mmv = None
            for fc in range(16):
                mmv = P.op("pe", lambda e, kw=kw, fc=fc, pk=pk: e.matmul(
                    pk[:, 0:tile], wobuf[kw][:, fc * 128:(fc + 1) * 128],
                    ygT[:, fc * tile:(fc + 1) * tile],
                    start=(fc == 0), stop=(fc == 15)),
                    [("wod%d" % kw, w_sl), ("act", ygT_ev)] +
                    ([bank_rel[bk]] if fc == 0 else []),
                    "pe" if fc == 15 else None)
            rb_wo.release(kw, "pe", mmv)
            xc = xr[:, dm * T + t0: dm * T + t0 + tile]
            dv = P.op("dve", lambda e, x=xc, pk=pk: e.tensor_tensor(
                x, x, pk[:, 0:tile], ALU.add),
                [("pe", mmv), ("pool", p_x_last)], "dve")
            bank_rel[bk] = ("dve", dv)
            last_wout_dve = dv

    return




def emit_phase_B(P, sh, tile=256):
    ntiles = T // tile
    FF = 4096
    NFC = FF // 128
    xr = sh["xr"]; sid = sh["sid"]; sidb = sh["sidb"]
    onesb = sh["onesb"]; ones1b = sh["ones1b"]

    W1 = P.dram("B_W1", [FF, D], F32, kind="ExternalInput")
    W2 = P.dram("B_W2", [D, FF], F32, kind="ExternalInput")
    b1T = P.dram("B_b1T", [128, NFC], F32, kind="ExternalInput")
    b2T = P.dram("B_b2T", [128, KC], F32, kind="ExternalInput")

    w2ts = P.dram("B_w2t_scr", [NFC, 128, 1024], BF16)
    w1t = P.sbuf("w1t", [128, KC * FF], BF16)
    w2sl = [P.sbuf(f"w2sl{i}", [128, 8 * 1024], BF16) for i in range(2)]
    w2st = [P.sbuf(f"w2st{i}", [128, 1024], BF16) for i in range(2)]
    xn2 = P.sbuf("xn2", [128, KC * tile], BF16)
    ws = [P.sbuf(f"ws{i}", [128, 1024], F32) for i in range(2)]
    wtt = [P.sbuf(f"wtt{i}", [128, 1024], BF16) for i in range(2)]
    tmpb = P.sbuf("tmpb", [128, 1024], BF16)
    hball = P.sbuf("hball", [128, NFC * tile], BF16)
    xqb = [P.sbuf(f"xqb{i}", [128, tile], BF16) for i in range(2)]
    statr = P.sbuf("statr", [1, 5 * tile], F32)
    statb = P.sbuf("statb", [1, 4 * tile], BF16)
    acc1 = P.sbuf("acc1", [128, 2], F32)
    accb = P.sbuf("accb", [128, 4], BF16)
    thr = P.sbuf("thr", [128, 2], F32)
    nthr = P.sbuf("nthr", [128, 2], F32)
    rowt = P.sbuf("rowt", [1, 8], F32)
    rowb = P.sbuf("rowb", [1, 8], BF16)
    sb1 = P.sbuf("sb1", [128, NFC], F32)
    sb2 = P.sbuf("sb2", [128, KC], F32)
    red = P.sbuf("red", [128, 1], F32)

    pk0 = P.psum("pk0", [128, 512], F32)
    pk1 = P.psum("pk1", [128, 512], F32)
    pk2 = P.psum("pk2", [128, 512], F32)
    pk3 = P.psum("pk3", [128, 512], F32)
    pk4 = P.psum("pk4", [128, 512], F32)
    pk7 = P.psum("pk7", [128, 512], F32)
    pk5 = P.psum("pk5", [128, 1024], BF16)
    pk6 = P.psum("pk6", [128, 1024], BF16)

    P.dma("sp", sb1[:], b1T, sem="initB")
    P.dma("sp", sb2[:], b2T, sem="initB")
    w_init = P.cnt["initB"]
    P.op("dve", lambda e: e.memset(acc1[:], 0.0), [("initB", w_init)])
    d_init = P.op("dve", lambda e: e.memset(red[:], 0.0), inc="dve")
    fence_w = [("initB", w_init), ("dve", d_init)]
    P.op("pe", lambda e: e.nop(), fence_w)
    P.op("act", lambda e: e.nop(), fence_w)
    P.op("pool", lambda e: e.engine_nop(), fence_w)

    def w1_slab_ap(fc):
        return W1[fc * 128:(fc + 1) * 128, :]

    def w2_slab_ap(fc):
        r = W2.rearrange("(k p) f -> p k f", p=128)
        return r[:, :, fc * 128:(fc + 1) * 128]

    rb_ws = RB(2)
    d_red = 0
    for wi in (0, 1):
        for c in range(NFC if wi == 0 else NFC):
            kw, ww = rb_ws.acquire()
            ap = w1_slab_ap(c) if wi == 0 else w2_slab_ap(c)
            w_sl = P.dma("sp", ws[kw][:], ap, ww, sem="wd%d" % kw)
            P.op("dve", lambda e, kw=kw: e.reduce_sum(
                red[:], ws[kw][:], axis=AX.X, apply_absolute_value=True),
                [("wd%d" % kw, w_sl), ("dve", d_init)], "dve")
            d_red = P.op("dve", lambda e, wi=wi: e.tensor_tensor(
                acc1[:, wi:wi + 1], acc1[:, wi:wi + 1], red[:], ALU.add), inc="dve")
            rb_ws.release(kw, "dve", d_red)
    a_h = P.op("act", lambda e: e.activation(accb[:, 0:2], acc1[:, 0:2], AF.Copy),
               [("dve", d_red)], "act")
    d_l = P.op("dve", lambda e: e.tensor_tensor(thr[:, 0:2], acc1[:, 0:2],
                                                accb[:, 0:2], ALU.subtract),
               [("act", a_h)], "dve")
    a_l = P.op("act", lambda e: e.activation(accb[:, 2:4], thr[:, 0:2], AF.Copy),
               [("dve", d_l)], "act")
    m1 = P.op("pe", lambda e: e.matmul(pk0[0:1, 0:2], onesb[:], accb[:, 0:2],
                                       start=True, stop=False), [("act", a_h)], None)
    m2 = P.op("pe", lambda e: e.matmul(pk0[0:1, 0:2], onesb[:], accb[:, 2:4],
                                       start=False, stop=True), [("act", a_l)], "pe")
    d_t = P.op("dve", lambda e: e.tensor_scalar(rowt[0:1, 0:2], pk0[0:1, 0:2],
                                                0.5 / (FF * D), None, ALU.mult,
                                                ALU.bypass), [("pe", m2)], "dve")
    d_t2 = P.op("dve", lambda e: e.tensor_scalar(rowt[0:1, 2:4], rowt[0:1, 0:2], -1.0,
                                                 None, ALU.mult, ALU.bypass), inc="dve")
    a_th = P.op("act", lambda e: e.activation(rowb[0:1, 0:4], rowt[0:1, 0:4],
                                              AF.Copy), [("dve", d_t2)], "act")
    d_tl = P.op("dve", lambda e: e.tensor_tensor(rowt[0:1, 4:8], rowt[0:1, 0:4],
                                                 rowb[0:1, 0:4], ALU.subtract),
                [("act", a_th)], "dve")
    a_tl = P.op("act", lambda e: e.activation(rowb[0:1, 4:8], rowt[0:1, 4:8],
                                              AF.Copy), [("dve", d_tl)], "act")
    m3 = P.op("pe", lambda e: e.matmul(pk1[:, 0:4], ones1b[:], rowb[0:1, 0:4],
                                       start=True, stop=False), [("act", a_th)], None)
    m4 = P.op("pe", lambda e: e.matmul(pk1[:, 0:4], ones1b[:], rowb[0:1, 4:8],
                                       start=False, stop=True), [("act", a_tl)], "pe")
    a_thr = P.op("act", lambda e: e.activation(thr[:, 0:2], pk1[:, 0:2], AF.Copy),
                 [("pe", m4)], "act")
    a_nthr = P.op("act", lambda e: e.activation(nthr[:, 0:2], pk1[:, 2:4], AF.Copy),
                  [("pe", m4)], "act")

    rb_wt = RB(2)
    bf16rel = [("act", 0)] * 2
    bfturn = [0]
    w2st_rel = [("act", 0), ("act", 0)]
    for wi in (0, 1):
        for c in range(NFC if wi == 0 else NFC):
            kw, ww = rb_ws.acquire()
            ap = w1_slab_ap(c) if wi == 0 else w2_slab_ap(c)
            w_sl = P.dma("sp", ws[kw][:], ap, ww, sem="wd%d" % kw)
            kt, wt_ = rb_wt.acquire()
            P.op("dve", lambda e, kw=kw, wi=wi: e.tensor_scalar(
                tmpb[:], ws[kw][:], nthr[:, wi:wi + 1], None, ALU.is_lt, ALU.bypass),
                [("wd%d" % kw, w_sl), ("act", a_nthr), ("act", a_thr)] + wt_, "dve")
            d_tn = P.op("dve", lambda e, kw=kw, kt=kt, wi=wi: e.scalar_tensor_tensor(
                wtt[kt][:], ws[kw][:], thr[:, wi:wi + 1], tmpb[:],
                ALU.is_gt, ALU.subtract), inc="dve")
            rb_ws.release(kw, "dve", d_tn)
            lastpe = None
            for k in range(8):
                bt = bfturn[0] % 2
                bfturn[0] += 1
                pkb = pk5 if bt == 0 else pk6
                pe_t = P.op("pe", lambda e, kt=kt, k=k, pkb=pkb: e.transpose(
                    pkb[:, 0:128], wtt[kt][:, k * 128:(k + 1) * 128], sidb[:]),
                    [("dve", d_tn), bf16rel[bt]], "pe")
                lastpe = pe_t
                if wi == 0:
                    fn = lambda e, c=c, k=k, pkb=pkb: e.activation(
                        w1t[:, k * FF + c * 128: k * FF + (c + 1) * 128],
                        pkb[:, 0:128], AF.Copy)
                else:
                    fn = lambda e, c=c, k=k, pkb=pkb: e.activation(
                        w2st[c % 2][:, k * 128:(k + 1) * 128],
                        pkb[:, 0:128], AF.Copy)
                av = P.op("act", fn, [("pe", pe_t)] +
                          ([w2st_rel[c % 2]] if (wi == 1 and k == 0) else []),
                          "act")
                bf16rel[bt] = ("act", av)
            if wi == 1:
                sm = "d_w2s%d" % (c % 2)
                w2st_rel[c % 2] = (sm, P.dma(
                    "sp", w2ts[c], w2st[c % 2][:],
                    [("act", av)], sem=sm))
            rb_wt.release(kt, "pe", lastpe)
    wt_done = P.cnt["act"]

    rb_xq = RB(2)
    rb_w2 = RB(2)
    bank_rel = {"sum": ("dve", 0), "sq": ("dve", 0), "m": ("dve", 0),
                "r": ("dve", 0)}
    orel = [("dve", 0)] * 2
    hrel = [("act", 0)] * 2
    h_mm_prev = 0

    for i in range(ntiles):
        t0 = i * tile
        repv = emit_ln_stats_x(P, xr, tile, xqb, rb_xq, statr, statb, onesb,
                               ones1b, pk0, pk1, pk2[:, 0:tile], pk3[:, 0:tile],
                               bank_rel, extra_waits=[orel[0], orel[1]],
                               xoff=t0, xstride=T)
        d_xn_last = 0
        for c in range(KC):
            xc = xr[:, c * T + t0: c * T + t0 + tile]
            P.op("dve", lambda e, x=xc, c=c: e.tensor_tensor(
                xn2[:, c * tile:(c + 1) * tile], x, pk2[:, 0:tile], ALU.subtract),
                [("pe", repv["m"]), ("pe", h_mm_prev), repv["dtail"]], "dve")
            d_xn_last = P.op("dve", lambda e, c=c: e.tensor_tensor(
                xn2[:, c * tile:(c + 1) * tile], xn2[:, c * tile:(c + 1) * tile],
                pk3[:, 0:tile], ALU.mult),
                [("pe", repv["r"])], "dve")
        bank_rel["m"] = ("dve", d_xn_last)
        bank_rel["r"] = ("dve", d_xn_last)

        a_h_last = 0
        for fc in range(NFC):
            par = fc % 2
            pkh = pk4 if par == 0 else pk7
            mmh = None
            for k in range(KC):
                mmh = P.op("pe", lambda e, k=k, fc=fc, pkh=pkh: e.matmul(
                    pkh[:, 0:tile],
                    w1t[:, k * FF + fc * 128:k * FF + (fc + 1) * 128],
                    xn2[:, k * tile:(k + 1) * tile],
                    start=(k == 0), stop=(k == KC - 1)),
                    [("act", wt_done), ("dve", d_xn_last)] +
                    ([hrel[par]] if k == 0 else []),
                    "pe" if k == KC - 1 else None)
            a_h2 = P.op("act", lambda e, fc=fc, pkh=pkh: e.activation(
                hball[:, fc * tile:(fc + 1) * tile], pkh[:, 0:tile], AF.Gelu,
                bias=sb1[:, fc:fc + 1]),
                [("pe", mmh)] + ([orel[0], orel[1]] if fc < 2 else []), "act")
            hrel[par] = ("act", a_h2)
            a_h_last = a_h2
        for fcg in range(4):
            kw2, ww2 = rb_w2.acquire()
            w_sl2 = P.dma("sp", w2sl[kw2][:].rearrange("p (g f) -> p g f", g=8),
                          w2ts[fcg * 8:(fcg + 1) * 8].rearrange("g p f -> p g f"),
                          list(ww2) + [("d_w2s0", P.cnt.get("d_w2s0", 0)),
                                       ("d_w2s1", P.cnt.get("d_w2s1", 0))],
                          sem="w2d%d" % kw2)
            for dm in range(KC):
                pko = pk0 if dm % 2 == 0 else pk1
                mmo = None
                for fci in range(8):
                    fc = fcg * 8 + fci
                    mmo = P.op("pe", lambda e, fci=fci, fc=fc, dm=dm, pko=pko,
                               kw2=kw2: e.matmul(
                        pko[:, 0:tile],
                        w2sl[kw2][:, fci * 1024 + dm * 128: fci * 1024 + (dm + 1) * 128],
                        hball[:, fc * tile:(fc + 1) * tile],
                        start=(fci == 0), stop=(fci == 7)),
                        [("act", a_h_last), ("w2d%d" % kw2, w_sl2)] +
                        ([orel[dm % 2]] if fci == 0 else []),
                        "pe" if fci == 7 else None)
                if fcg == 0:
                    dv = P.op("dve", lambda e, dm=dm, pko=pko, t0=t0: e.scalar_tensor_tensor(
                        xr[:, dm * T + t0: dm * T + t0 + tile],
                        pko[:, 0:tile], sb2[:, dm:dm + 1],
                        xr[:, dm * T + t0: dm * T + t0 + tile], ALU.add, ALU.add),
                        [("pe", mmo)], "dve")
                else:
                    dv = P.op("dve", lambda e, dm=dm, pko=pko, t0=t0: e.tensor_tensor(
                        xr[:, dm * T + t0: dm * T + t0 + tile],
                        xr[:, dm * T + t0: dm * T + t0 + tile],
                        pko[:, 0:tile], ALU.add),
                        [("pe", mmo)], "dve")
                orel[dm % 2] = ("dve", dv)
            rb_w2.release(kw2, "pe", mmo)
        h_mm_prev = P.cnt["pe"]

    return


def emit_ln_stats_x(P, xbuf, tile, xqb, rb_xq, statr, statb, onesb, ones1b,
                    pk_sum, pk_sq, pk_m, pk_r, bank_rel, extra_waits=(),
                    rep_extra=(), xoff=0, xstride=None):
    if xstride is None:
        xstride = tile
    mmS = mmQ = None
    for c in range(KC):
        src = xbuf[:, c * xstride + xoff: c * xstride + xoff + tile]
        kb, w = rb_xq.acquire()
        a1 = P.op("act", lambda e, s=src, d=xqb[kb]: e.activation(d[:, 0:tile], s, AF.Copy),
                  list(w) + list(extra_waits), "act")
        mmS = P.op("pe", lambda e, d=xqb[kb], c=c: e.matmul(
            pk_sum[0:1, 0:tile], onesb[:], d[:, 0:tile],
            start=(c == 0), stop=(c == KC - 1)),
            [("act", a1)] + ([bank_rel["sum"]] if c == 0 else []), "pe")
        kb2, w2 = rb_xq.acquire()
        a2 = P.op("act", lambda e, s=src, d=xqb[kb2]: e.activation(d[:, 0:tile], s, AF.Square),
                  list(w2) + list(extra_waits), "act")
        mmQ = P.op("pe", lambda e, d=xqb[kb2], c=c: e.matmul(
            pk_sq[0:1, 0:tile], onesb[:], d[:, 0:tile],
            start=(c == 0), stop=(c == KC - 1)),
            [("act", a2)] + ([bank_rel["sq"]] if c == 0 else []), "pe")
        rb_xq.release(kb, "pe", mmS)
        rb_xq.release(kb2, "pe", mmQ)

    def sl(i):
        return statr[0:1, i * tile:(i + 1) * tile]

    inv = 1.0 / D
    d1 = P.op("dve", lambda e: e.tensor_scalar(sl(0), pk_sum[0:1, 0:tile], inv, None,
                                               ALU.mult, ALU.bypass), [("pe", mmS)], "dve")
    d2 = P.op("dve", lambda e: e.tensor_scalar(sl(1), pk_sq[0:1, 0:tile], inv, None,
                                               ALU.mult, ALU.bypass), [("pe", mmQ)], "dve")
    bank_rel["sum"] = ("dve", d1)
    bank_rel["sq"] = ("dve", d2)
    P.op("dve", lambda e: e.tensor_tensor(sl(2), sl(0), sl(0), ALU.mult))
    d4 = P.op("dve", lambda e: e.tensor_tensor(sl(1), sl(1), sl(2), ALU.subtract),
              inc="dve")
    d4b = P.op("dve", lambda e: e.tensor_scalar(sl(2), sl(1), 1e-5, None,
                                                ALU.add, ALU.bypass), [("dve", d4)], "dve")
    a3 = P.op("act", lambda e: e.activation(sl(2), sl(2), AF.Ln), [("dve", d4b)], "act")
    a4 = P.op("act", lambda e: e.activation(sl(3), sl(2), AF.Exp, scale=-0.5),
              [("act", a3)], "act")
    reps = [("m", sl(0), 0, pk_m, ("dve", d1), 4), ("r", sl(3), 2, pk_r, ("act", a4), 4)]
    out = {}
    prev_al = None
    for name, row, bi, pk, wv, tmpi in reps:
        def sb(i):
            return statb[0:1, i * tile:(i + 1) * tile]
        ah = P.op("act", lambda e, r=row, o=sb(bi): e.activation(o, r, AF.Copy),
                  [wv], "act")
        dl = P.op("dve", lambda e, r=row, h=sb(bi), o=sl(tmpi): e.tensor_tensor(
            o, r, h, ALU.subtract),
            [("act", ah)] + ([("act", prev_al)] if prev_al else []), "dve")
        al = P.op("act", lambda e, o=sb(bi + 1), r=sl(tmpi): e.activation(o, r, AF.Copy),
                  [("dve", dl)], "act")
        prev_al = al
        m1 = P.op("pe", lambda e, pk=pk, h=sb(bi): e.matmul(
            pk, ones1b[:], h, start=True, stop=False),
            [("act", ah), bank_rel[name]] + list(rep_extra), None)
        m2 = P.op("pe", lambda e, pk=pk, l=sb(bi + 1): e.matmul(
            pk, ones1b[:], l, start=False, stop=True),
            [("act", al)], "pe")
        out[name] = m2
    out["dtail"] = ("dve", P.cnt["dve"])
    return out


def wA_slab_ap(wab, j):
    r = wab.rearrange("(k p) f -> p k f", p=128)
    return r[:, :, j * 256:(j + 1) * 256]


def wo_slab_ap(wob, dm):
    r = wob.rearrange("(k p) d -> p k d", p=128)
    return r[:, :, dm * 128:(dm + 1) * 128]



def emit_phase_C(P, sh, tile=256):
    ntiles = T // tile
    NHD = 16
    TC = tile // 128
    xr = sh["xr"]; sid = sh["sid"]; sidb = sh["sidb"]
    onesb = sh["onesb"]; ones1b = sh["ones1b"]

    Wq = P.dram("C_Wq", [D, D], F32, kind="ExternalInput")
    Wk = P.dram("C_Wk", [D, D], F32, kind="ExternalInput")
    Wv = P.dram("C_Wv", [D, D], F32, kind="ExternalInput")
    Wg = P.dram("C_Wg", [D, D], F32, kind="ExternalInput")
    Wo = P.dram("C_Wo", [D, D], F32, kind="ExternalInput")
    bg = P.dram("C_bg", [1, D], F32, kind="ExternalInput")

    xn3 = P.sbuf("xn3", [128, KC * T], BF16)
    WS = [P.sbuf(f"WS{i}", [128, KC * D], BF16) for i in range(3)]
    ktm = P.sbuf("ktm", [128, TC * D], BF16)
    vA = P.sbuf("vA", [128, TC * NHD * 65], BF16)
    qb = P.sbuf("qb", [128, KC * tile], BF16)
    gb = P.sbuf("gb", [128, TC * D], BF16)
    o2 = P.sbuf("o2", [128, TC * D], BF16)
    o2T = P.sbuf("o2T", [128, KC * tile], BF16)
    KVA = P.sbuf("KVA", [128, NHD * 65], BF16)
    bgrep = P.sbuf("bgrep", [128, D], F32)
    bgror = P.sbuf("bgror", [1, D], F32)
    bgrow = P.sbuf("bgrow", [1, 2 * D], BF16)
    et1 = P.sbuf("et1", [128, 512], F32)
    et2 = P.sbuf("et2", [128, 512], F32)
    zrec = P.sbuf("zrec", [128, 2], F32)
    xqb = [P.sbuf(f"xqb{i}", [128, tile], BF16) for i in range(4)]
    statr = P.sbuf("statr", [1, 5 * tile], F32)
    statb = P.sbuf("statb", [1, 4 * tile], BF16)

    pk0 = P.psum("pk0", [128, 512], F32)
    pk1 = P.psum("pk1", [128, 512], F32)
    pk2 = P.psum("pk2", [128, 512], F32)
    pk3 = P.psum("pk3", [128, 512], F32)
    pk4 = P.psum("pk4", [128, 512], F32)
    pk7 = P.psum("pk7", [128, 512], F32)
    pk5 = P.psum("pk5", [128, 1024], BF16)
    pk6 = P.psum("pk6", [128, 1024], BF16)

    P.dma("sp", bgror[:], bg, sem="initC")
    w_init = P.cnt["initC"]

    def wrearr(w):
        return w.rearrange("(k p) f -> p k f", p=128)

    w_k = P.dma("pool", WS[0][:].rearrange("p (k f) -> p k f", k=KC),
                wrearr(Wk), sem="cast_k")
    w_v = P.dma("pool", WS[1][:].rearrange("p (k f) -> p k f", k=KC),
                wrearr(Wv), sem="cast_v")
    w_o = P.dma("pool", WS[2][:].rearrange("p (k f) -> p k f", k=KC),
                wrearr(Wo), sem="cast_o")
    d_init = P.op("dve", lambda e: e.memset(vA[:], 1.0), inc="dve")
    fence_w = [("initC", w_init), ("dve", d_init)]
    P.op("pe", lambda e: e.nop(), fence_w)
    P.op("act", lambda e: e.nop(), fence_w)
    P.op("pool", lambda e: e.engine_nop(), fence_w)

    a_bh = P.op("act", lambda e: e.activation(bgrow[0:1, 0:D], bgror[:], AF.Copy),
                [("initC", w_init)], "act")
    d_bl = P.op("dve", lambda e: e.tensor_tensor(bgrep[0:1, 0:D], bgror[:],
                                                 bgrow[0:1, 0:D], ALU.subtract),
                [("act", a_bh)], "dve")
    a_bl = P.op("act", lambda e: e.activation(bgrow[0:1, D:2 * D], bgrep[0:1, 0:D],
                                              AF.Copy), [("dve", d_bl)], "act")
    a_bgrep = 0
    for half in range(2):
        pkh = pk0 if half == 0 else pk1
        P.op("pe", lambda e, pkh=pkh, half=half: e.matmul(
            pkh[:, 0:512], ones1b[:], bgrow[0:1, half * 512:half * 512 + 512],
            start=True, stop=False), [("act", a_bh), ("dve", d_bl)], None)
        m_bg = P.op("pe", lambda e, pkh=pkh, half=half: e.matmul(
            pkh[:, 0:512], ones1b[:], bgrow[0:1, D + half * 512:D + half * 512 + 512],
            start=False, stop=True), [("act", a_bl)], "pe")
        a_bgrep = P.op("act", lambda e, pkh=pkh, half=half: e.activation(
            bgrep[:, half * 512:(half + 1) * 512], pkh[:, 0:512], AF.Copy),
            [("pe", m_bg)], "act")

    rb_xq = RB(4)
    bank_rel = {"sum": ("act", a_bgrep), "sq": ("act", a_bgrep),
                "m": ("dve", 0), "r": ("dve", 0)}
    d_xn_last = 0
    for i in range(ntiles):
        t0 = i * tile
        repv = emit_ln_stats_x(P, xr, tile, xqb, rb_xq, statr, statb, onesb,
                               ones1b, pk0, pk1, pk2[:, 0:tile], pk3[:, 0:tile],
                               bank_rel, xoff=t0, xstride=T)
        for c in range(KC):
            xc = xr[:, c * T + t0: c * T + t0 + tile]
            P.op("dve", lambda e, x=xc, c=c, t0=t0: e.tensor_tensor(
                xn3[:, c * T + t0: c * T + t0 + tile], x, pk2[:, 0:tile],
                ALU.subtract), [("pe", repv["m"]), repv["dtail"]], "dve")
            d_xn_last = P.op("dve", lambda e, c=c, t0=t0: e.tensor_tensor(
                xn3[:, c * T + t0: c * T + t0 + tile],
                xn3[:, c * T + t0: c * T + t0 + tile], pk3[:, 0:tile], ALU.mult),
                [("pe", repv["r"])], "dve")
        bank_rel["m"] = ("dve", d_xn_last)
        bank_rel["r"] = ("dve", d_xn_last)

    prel = [("act", 0), ("act", 0)]
    kv_first = [True] * 4
    pproj = [0]
    kv_mm_last = None
    ktm_rel = ("pe", 0)
    vA_rel = ("pe", 0)
    for i in range(ntiles):
        t0 = i * tile
        k_ev = v_ev = None
        for wi, wbuf, wdone in ((0, WS[0], w_k), (1, WS[1], w_v)):
            for half in range(2):
                for tc in range(TC):
                    par = pproj[0] % 2
                    pproj[0] += 1
                    pkh = pk4 if par == 0 else pk7
                    mmv = None
                    for k in range(KC):
                        mmv = P.op("pe", lambda e, k=k, half=half, tc=tc, t0=t0,
                                   pkh=pkh, wbuf=wbuf: e.matmul(
                            pkh[:, 0:512],
                            xn3[:, k * T + t0 + tc * 128: k * T + t0 + tc * 128 + 128],
                            wbuf[:, k * D + half * 512: k * D + half * 512 + 512],
                            start=(k == 0), stop=(k == KC - 1)),
                            [("cast_k" if wi == 0 else "cast_v", wdone),
                             ("dve", d_xn_last)] + ([prel[par]] if k == 0 else []) +
                            ([ktm_rel] if (wi == 0 and half == 0 and tc == 0) else []) +
                            ([vA_rel] if (wi == 1 and half == 0 and tc == 0) else []),
                            "pe" if k == KC - 1 else None)
                    if wi == 0:
                        d_e1 = P.op("dve", lambda e, pkh=pkh: e.tensor_scalar(
                            et1[:], pkh[:, 0:512], 0.0, None, ALU.min, ALU.bypass),
                            [("pe", mmv)], "dve")
                        a_e = P.op("act", lambda e: e.activation(et2[:], et1[:], AF.Exp),
                                   [("dve", d_e1)], "act")
                        k_ev = P.op("dve", lambda e, pkh=pkh, half=half, tc=tc: e.scalar_tensor_tensor(
                            ktm[:, tc * D + half * 512: tc * D + (half + 1) * 512],
                            pkh[:, 0:512], 0.0, et2[:], ALU.max, ALU.add),
                            [("act", a_e)], "dve")
                        prel[par] = ("dve", k_ev)
                    else:
                        dst = vA[:, tc * NHD * 65:(tc + 1) * NHD * 65].rearrange(
                            "p (h c) -> p h c", c=65)[:, half * 8:(half + 1) * 8, 0:64]
                        v_ev = P.op("act", lambda e, pkh=pkh, dst=dst: e.activation(
                            dst, pkh[:, 0:512].rearrange("p (h c) -> p h c", c=64),
                            AF.Copy), [("pe", mmv)], "act")
                        prel[par] = ("act", v_ev)
        for h in range(NHD):
            bank = (pk0, pk1, pk2, pk3)[h // 4]
            reg = (h % 4) * 65
            for tc in range(TC):
                first = kv_first[h // 4]
                kv_first[h // 4] = False
                last = (i == ntiles - 1) and (tc == TC - 1)
                kv_mm_last = P.op("pe", lambda e, h=h, tc=tc, bank=bank, reg=reg,
                                  first=first, last=last: e.matmul(
                    bank[0:64, reg:reg + 65],
                    ktm[:, tc * D + h * 64: tc * D + h * 64 + 64],
                    vA[:, tc * NHD * 65 + h * 65: tc * NHD * 65 + (h + 1) * 65],
                    start=first, stop=last, skip_group_check=True),
                    [("dve", k_ev), ("act", v_ev)] +
                    ([bank_rel["m"], bank_rel["r"], bank_rel["sum"], bank_rel["sq"]]
                     if first else []),
                    "pe")
        ktm_rel = ("pe", kv_mm_last)
        vA_rel = ("pe", kv_mm_last)

    a_kv = 0
    for b in range(4):
        bank = (pk0, pk1, pk2, pk3)[b]
        for rw in (0, 64):
            a_kv = P.op("act", lambda e, b=b, bank=bank, rw=rw: e.activation(
                KVA[rw:rw + 64, b * 260:(b + 1) * 260], bank[0:64, 0:260], AF.Copy),
                [("pe", kv_mm_last)], "act")

    w_q = P.dma("pool", WS[0][:].rearrange("p (k f) -> p k f", k=KC),
                wrearr(Wq), [ktm_rel, ("dve", P.cnt["dve"])], sem="cast_k")
    w_g = P.dma("pool", WS[1][:].rearrange("p (k f) -> p k f", k=KC),
                wrearr(Wg), [vA_rel, ("act", P.cnt["act"])], sem="cast_v")
    orel = [("dve", 0), ("dve", 0)]
    grel = ("act", 0)
    hrel = [("dve", 0), ("dve", 0)]
    qrel = [("dve", 0), ("dve", 0)]
    bfrel = [("act", 0), ("act", 0)]
    bft = [0]
    for i in range(ntiles):
        t0 = i * tile
        q_ev = None
        for fc in range(KC):
            pkq = pk2 if fc % 2 == 0 else pk3
            mmq = None
            for k in range(KC):
                mmq = P.op("pe", lambda e, k=k, fc=fc, t0=t0, pkq=pkq: e.matmul(
                    pkq[:, 0:tile],
                    WS[0][:, k * D + fc * 128: k * D + (fc + 1) * 128],
                    xn3[:, k * T + t0: k * T + t0 + tile],
                    start=(k == 0), stop=(k == KC - 1)),
                    [("cast_k", w_q), ("dve", d_xn_last)] +
                    ([qrel[fc % 2], ("act", a_kv)] if k == 0 else []),
                    "pe" if k == KC - 1 else None)
            d_e1 = P.op("dve", lambda e, pkq=pkq: e.tensor_scalar(
                et1[:, 0:tile], pkq[:, 0:tile], 0.0, None, ALU.min, ALU.bypass),
                [("pe", mmq)], "dve")
            a_e = P.op("act", lambda e: e.activation(et2[:, 0:tile], et1[:, 0:tile],
                                                     AF.Exp), [("dve", d_e1)], "act")
            q_ev = P.op("dve", lambda e, pkq=pkq, fc=fc: e.scalar_tensor_tensor(
                qb[:, fc * tile:(fc + 1) * tile], pkq[:, 0:tile], 0.0, et2[:, 0:tile],
                ALU.max, ALU.add), [("act", a_e)], "dve")
            qrel[fc % 2] = ("dve", q_ev)
        g_ev = None
        for half in range(2):
            for tc in range(TC):
                mmg = None
                for k in range(KC):
                    mmg = P.op("pe", lambda e, k=k, half=half, tc=tc, t0=t0: e.matmul(
                        pk4[:, 0:512],
                        xn3[:, k * T + t0 + tc * 128: k * T + t0 + tc * 128 + 128],
                        WS[1][:, k * D + half * 512: k * D + half * 512 + 512],
                        start=(k == 0), stop=(k == KC - 1)),
                        [("cast_v", w_g), ("dve", d_xn_last)] +
                        ([grel, hrel[0]] if k == 0 else []),
                        "pe" if k == KC - 1 else None)
                d_g1 = P.op("dve", lambda e, half=half: e.tensor_tensor(
                    et1[:], pk4[:, 0:512], bgrep[:, half * 512:(half + 1) * 512],
                    ALU.add), [("pe", mmg), ("act", a_bgrep)], "dve")
                g_ev = P.op("act", lambda e, half=half, tc=tc: e.activation(
                    gb[:, tc * D + half * 512: tc * D + (half + 1) * 512], et1[:],
                    AF.Sigmoid), [("dve", d_g1)], "act")
                grel = ("act", g_ev)
        d_o_last = 0
        o2_pe_rel = P.cnt["pe"] if i == 0 else o2_pe_rel
        for h in range(NHD):
            par = h % 2
            pkh = pk4 if par == 0 else pk7
            fc2 = h // 2
            r0 = (h % 2) * 64
            for tc in range(TC):
                mmo = P.op("pe", lambda e, h=h, tc=tc, pkh=pkh, fc2=fc2, r0=r0,
                           t0=t0: e.matmul(
                    pkh[:, 0:65],
                    qb[r0:r0 + 64, fc2 * tile + tc * 128: fc2 * tile + tc * 128 + 128],
                    KVA[r0:r0 + 64, h * 65:(h + 1) * 65],
                    start=True, stop=True),
                    [("dve", q_ev), ("act", a_kv), hrel[par],
                     (grel if (h == 0 and tc == 0) else ("act", 0))], "pe")
                d_z = P.op("dve", lambda e, pkh=pkh: e.tensor_scalar(
                    zrec[:, 0:1], pkh[:, 64:65], 8.0, 8e-6, ALU.mult, ALU.max),
                    [("pe", mmo)], "dve")
                d_zr = P.op("dve", lambda e: e.reciprocal(zrec[:, 1:2], zrec[:, 0:1]),
                            [("dve", d_z)], "dve")
                d_o = P.op("dve", lambda e, pkh=pkh, h=h, tc=tc: e.scalar_tensor_tensor(
                    o2[:, tc * D + h * 64: tc * D + (h + 1) * 64],
                    pkh[:, 0:64], zrec[:, 1:2],
                    gb[:, tc * D + h * 64: tc * D + (h + 1) * 64],
                    ALU.mult, ALU.mult),
                    [("dve", d_zr), ("act", g_ev), ("pe", o2_pe_rel)], "dve")
                hrel[par] = ("dve", d_o)
                d_o_last = d_o
        to_ev = 0
        for tc in range(TC):
            for fc2 in range(KC):
                bt = bft[0] % 2
                bft[0] += 1
                pkb = pk5 if bt == 0 else pk6
                pe_t = P.op("pe", lambda e, tc=tc, fc2=fc2, pkb=pkb: e.transpose(
                    pkb[:, 0:128], o2[:, tc * D + fc2 * 128: tc * D + (fc2 + 1) * 128],
                    sidb[:]), [("dve", d_o_last), bfrel[bt]], "pe")
                to_ev = P.op("act", lambda e, tc=tc, fc2=fc2, pkb=pkb: e.activation(
                    o2T[:, fc2 * tile + tc * 128: fc2 * tile + tc * 128 + 128],
                    pkb[:, 0:128], AF.Copy), [("pe", pe_t)], "act")
                bfrel[bt] = ("act", to_ev)
        o2_pe_rel = 0
        for dm in range(KC):
            pko = pk0 if dm % 2 == 0 else pk1
            mmw = None
            for fc2 in range(KC):
                mmw = P.op("pe", lambda e, dm=dm, fc2=fc2, pko=pko: e.matmul(
                    pko[:, 0:tile],
                    WS[2][:, fc2 * D + dm * 128: fc2 * D + (dm + 1) * 128],
                    o2T[:, fc2 * tile:(fc2 + 1) * tile],
                    start=(fc2 == 0), stop=(fc2 == KC - 1)),
                    [("cast_o", w_o), ("act", to_ev)] +
                    ([orel[dm % 2], ("act", a_kv)] if fc2 == 0 else []),
                    "pe" if fc2 == KC - 1 else None)
            dv = P.op("dve", lambda e, dm=dm, pko=pko, t0=t0: e.tensor_tensor(
                xr[:, dm * T + t0: dm * T + t0 + tile],
                xr[:, dm * T + t0: dm * T + t0 + tile],
                pko[:, 0:tile], ALU.add), [("pe", mmw)], "dve")
            orel[dm % 2] = ("dve", dv)
            o2_pe_rel = mmw

    return



def emit_phase_D(P, sh, tile=512):
    ntiles = T // tile
    HID = 3072
    NFC = HID // 128
    xr = sh["xr"]; sid = sh["sid"]; sidb = sh["sidb"]
    onesb = sh["onesb"]; ones1b = sh["ones1b"]

    wg = P.dram("D_wg", [HID, D], I32, kind="ExternalInput")
    wu = P.dram("D_wu", [HID, D], I32, kind="ExternalInput")
    wd = P.dram("D_wd", [D, HID], I32, kind="ExternalInput")

    wgts = P.dram("D_wgt_scr", [NFC, 128, 1024], BF16)
    wuts = P.dram("D_wut_scr", [NFC, 128, 1024], BF16)
    wdt = P.sbuf("wdt", [128, NFC * D], BF16)
    wst = [P.sbuf(f"wst{i}", [128, 1024], BF16) for i in range(2)]
    wgl = [P.sbuf(f"wgl{i}", [128, 1024], BF16) for i in range(2)]
    wul = [P.sbuf(f"wul{i}", [128, 1024], BF16) for i in range(2)]
    xn4 = P.sbuf("xn4", [128, KC * tile], BF16)
    wsl = [P.sbuf(f"wsl{i}", [128, KC * 128], BF16) for i in range(2)]
    gball = P.sbuf("gball", [128, tile], BF16)
    uball = P.sbuf("uball", [128, tile], BF16)
    hball = P.sbuf("hball", [128, NFC * tile], BF16)
    xqb = [P.sbuf(f"xqb{i}", [128, tile], BF16) for i in range(4)]
    statr = P.sbuf("statr", [1, 5 * tile], F32)
    statb = P.sbuf("statb", [1, 4 * tile], BF16)

    pk0 = P.psum("pk0", [128, 512], F32)
    pk1 = P.psum("pk1", [128, 512], F32)
    pk2 = P.psum("pk2", [128, 512], F32)
    pk3 = P.psum("pk3", [128, 512], F32)
    pk4 = P.psum("pk4", [128, 512], F32)
    pk7 = P.psum("pk7", [128, 512], F32)
    pk5 = P.psum("pk5", [128, 1024], BF16)
    pk6 = P.psum("pk6", [128, 1024], BF16)

    rb_w = RB(2)
    bfrel = [("act", 0), ("act", 0)]
    bft = [0]
    wst_rel = [("act", 0), ("act", 0)]
    for wi, wsrc in ((0, wg), (1, wu), (2, wd)):
        for fc in range(NFC):
            kw, ww = rb_w.acquire()
            if wi < 2:
                ap = wsrc[fc * 128:(fc + 1) * 128, :].rearrange("p (k f) -> p k f", k=KC)
            else:
                ap = wsrc.rearrange("(k p) f -> p k f", p=128)[:, :, fc * 128:(fc + 1) * 128]
            w_sl = P.dma("pool", wsl[kw][:].rearrange("p (k f) -> p k f", k=KC), ap,
                         ww, sem="dwc%d" % kw)
            lastpe = None
            for k in range(KC):
                bt = bft[0] % 2
                bft[0] += 1
                pkb = pk5 if bt == 0 else pk6
                pe_t = P.op("pe", lambda e, kw=kw, k=k, pkb=pkb: e.transpose(
                    pkb[:, 0:128], wsl[kw][:, k * 128:(k + 1) * 128], sidb[:]),
                    [("dwc%d" % kw, w_sl), bfrel[bt]], "pe")
                lastpe = pe_t
                if wi < 2:
                    fn = lambda e, fc=fc, k=k, pkb=pkb: e.activation(
                        wst[fc % 2][:, k * 128:(k + 1) * 128], pkb[:, 0:128],
                        AF.Copy)
                else:
                    fn = lambda e, fc=fc, k=k, pkb=pkb: e.activation(
                        wdt[:, fc * D + k * 128: fc * D + (k + 1) * 128],
                        pkb[:, 0:128], AF.Copy)
                av = P.op("act", fn, [("pe", pe_t)] +
                          ([wst_rel[fc % 2]] if (wi < 2 and k == 0) else []),
                          "act")
                bfrel[bt] = ("act", av)
            if wi < 2:
                scr = wgts if wi == 0 else wuts
                sm = "d_wst%d" % (fc % 2)
                wst_rel[fc % 2] = (sm, P.dma(
                    "sp", scr[fc], wst[fc % 2][:],
                    [("act", av)], sem=sm))
            rb_w.release(kw, "pe", lastpe)
    wt_done = P.cnt["act"]

    rb_xq = RB(4)
    rb_wg = RB(2)
    rb_wu = RB(2)
    wst_done = [("d_wst0", P.cnt.get("d_wst0", 0)),
                ("d_wst1", P.cnt.get("d_wst1", 0))]
    bank_rel = {"sum": ("dve", 0), "sq": ("dve", 0), "m": ("dve", 0),
                "r": ("dve", 0)}
    orel = [("dve", 0)] * 2
    grel = [("act", 0)] * 2
    urel = [("act", 0)] * 2
    h_mm_prev = 0
    s2 = 1.0 / (127.0 * 127.0)

    for i in range(ntiles):
        t0 = i * tile
        repv = emit_ln_stats_x(P, xr, tile, xqb, rb_xq, statr, statb, onesb,
                               ones1b, pk0, pk1, pk2[:, 0:tile], pk3[:, 0:tile],
                               bank_rel,
                               extra_waits=[orel[0], orel[1], urel[0], urel[1]],
                               xoff=t0, xstride=T)
        d_xn_last = 0
        for c in range(KC):
            xc = xr[:, c * T + t0: c * T + t0 + tile]
            P.op("dve", lambda e, x=xc, c=c: e.tensor_tensor(
                xn4[:, c * tile:(c + 1) * tile], x, pk2[:, 0:tile], ALU.subtract),
                [("pe", repv["m"]), ("pe", h_mm_prev), repv["dtail"]], "dve")
            d_xn_last = P.op("dve", lambda e, c=c: e.tensor_tensor(
                xn4[:, c * tile:(c + 1) * tile], xn4[:, c * tile:(c + 1) * tile],
                pk3[:, 0:tile], ALU.mult), [("pe", repv["r"])], "dve")
        bank_rel["m"] = ("dve", d_xn_last)
        bank_rel["r"] = ("dve", d_xn_last)

        d_h_last = 0
        for fc in range(NFC):
            pkg = pk4 if fc % 2 == 0 else pk7
            pku = pk2 if fc % 2 == 0 else pk3
            kwg, wwg = rb_wg.acquire()
            w_g = P.dma("sp", wgl[kwg][:], wgts[fc],
                        wwg + wst_done, sem="wgd%d" % kwg)
            kwu, wwu = rb_wu.acquire()
            w_u = P.dma("sp", wul[kwu][:], wuts[fc],
                        wwu + wst_done, sem="wud%d" % kwu)
            mmg = mmu = None
            for k in range(KC):
                mmg = P.op("pe", lambda e, k=k, fc=fc, pkg=pkg, kwg=kwg: e.matmul(
                    pkg[:, 0:tile], wgl[kwg][:, k * 128:(k + 1) * 128],
                    xn4[:, k * tile:(k + 1) * tile],
                    start=(k == 0), stop=(k == KC - 1)),
                    [("act", wt_done), ("dve", d_xn_last), ("wgd%d" % kwg, w_g)] +
                    ([grel[fc % 2]] if k == 0 else []),
                    "pe" if k == KC - 1 else None)
            rb_wg.release(kwg, "pe", mmg)
            for k in range(KC):
                mmu = P.op("pe", lambda e, k=k, fc=fc, pku=pku, kwu=kwu: e.matmul(
                    pku[:, 0:tile], wul[kwu][:, k * 128:(k + 1) * 128],
                    xn4[:, k * tile:(k + 1) * tile],
                    start=(k == 0), stop=(k == KC - 1)),
                    [("act", wt_done), ("dve", d_xn_last), ("wud%d" % kwu, w_u)] +
                    ([urel[fc % 2], bank_rel["m"], bank_rel["r"]] if k == 0 else []),
                    "pe" if k == KC - 1 else None)
            rb_wu.release(kwu, "pe", mmu)
            a_g = P.op("act", lambda e, pkg=pkg: e.activation(
                gball[:], pkg[:, 0:tile], AF.Silu, scale=1.0 / 127.0),
                [("pe", mmg), ("dve", d_h_last)], "act")
            grel[fc % 2] = ("act", a_g)
            a_u = P.op("act", lambda e, pku=pku: e.activation(
                uball[:], pku[:, 0:tile], AF.Copy),
                [("pe", mmu), ("dve", d_h_last)], "act")
            urel[fc % 2] = ("act", a_u)
            d_h_last = P.op("dve", lambda e, fc=fc: e.tensor_tensor(
                hball[:, fc * tile:(fc + 1) * tile], gball[:], uball[:], ALU.mult),
                [("act", a_g), ("act", a_u)], "dve")
        h_mm_prev = P.cnt["pe"]

        dv = None
        for dm in range(KC):
            pko = pk0 if dm % 2 == 0 else pk1
            mmo = None
            for fc in range(NFC):
                mmo = P.op("pe", lambda e, fc=fc, dm=dm, pko=pko: e.matmul(
                    pko[:, 0:tile], wdt[:, fc * D + dm * 128: fc * D + (dm + 1) * 128],
                    hball[:, fc * tile:(fc + 1) * tile],
                    start=(fc == 0), stop=(fc == NFC - 1)),
                    [("dve", d_h_last)] + ([orel[dm % 2]] if fc == 0 else []),
                    "pe" if fc == NFC - 1 else None)
            dv = P.op("dve", lambda e, dm=dm, pko=pko, t0=t0: e.scalar_tensor_tensor(
                xr[:, dm * T + t0: dm * T + t0 + tile], pko[:, 0:tile], s2,
                xr[:, dm * T + t0: dm * T + t0 + tile], ALU.mult, ALU.add),
                [("pe", mmo)], "dve")
            orel[dm % 2] = ("dve", dv)

    return



def build_fused(sim_safe=True, relax=("pe", "act", "dve", "pool"), phases="ABCD"):
    P = PB(sim_safe=sim_safe, relax=relax)
    sr = globals().get("_STRICT_RNG")
    if sr:
        P.strict_rng[sr[0]] = (sr[1], sr[2])
    x_in = P.dram("x_in", [D, T], F32, kind="ExternalInput")
    identf = P.dram("identf", [128, 128], F32, kind="ExternalInput")
    x_out = P.dram("x_out", [D, T], F32, kind="ExternalOutput")

    xr = P.sbuf("xr", [128, KC * T], F32)
    sid = P.sbuf("sid", [128, 128], F32)
    sidb = P.sbuf("sidb", [128, 128], BF16)
    onesb = P.sbuf("onesb", [128, 1], BF16)
    ones1b = P.sbuf("ones1b", [1, 128], BF16)

    P.dma("sp", xr[:].rearrange("p (c t) -> p c t", c=KC),
          x_in.rearrange("(c p) t -> p c t", p=128), sem="init0")
    P.dma("sp", sid[:], identf, sem="init0")
    w0 = P.cnt["init0"]
    P.op("dve", lambda e: e.tensor_copy(sidb[:], sid[:]), [("init0", w0)])
    P.op("dve", lambda e: e.memset(onesb[:], 1.0))
    d0 = P.op("dve", lambda e: e.memset(ones1b[:], 1.0), inc="dve")
    fw = [("init0", w0), ("dve", d0)]
    P.op("pe", lambda e: e.nop(), fw)
    P.op("act", lambda e: e.nop(), fw)
    P.op("dve", lambda e: e.engine_nop(), fw)
    P.op("pool", lambda e: e.engine_nop(), fw)
    P.op("sp", lambda e: e.nop(), fw)

    sh = dict(xr=xr, sid=sid, sidb=sidb, onesb=onesb, ones1b=ones1b)
    all_phases = (("pA_", emit_phase_A), ("pB_", emit_phase_B),
                  ("pC_", emit_phase_C), ("pD_", emit_phase_D))
    sel = phases
    for pfx, emit in all_phases:
        if pfx[1] not in sel:
            continue
        P.push_scope()
        P.name_prefix = pfx
        emit(P, sh)
        P.name_prefix = ""
        P.fence()
        P.pop_scope()

    fin = P.dma("sp", x_out.rearrange("(c p) t -> p c t", p=128),
                xr[:].rearrange("p (c t) -> p c t", c=KC), sem="fin")
    P.op("sp", lambda e: e.nop(), [("fin", fin)])
    return P


def _prep_inputs(inputs):
    f32 = np.float32
    w_in = np.asarray(inputs["L0_attn_w_in"], f32)
    pad = np.zeros((D, 96), f32)
    pad64 = np.zeros((D, 64), f32)
    pad128 = np.zeros((D, 128), f32)
    wA = np.ascontiguousarray(np.concatenate(
        [w_in[:, 0:4096], w_in[:, 4096:4128], pad,
         w_in[:, 4128:4192], pad64, w_in[:, 4192:4256], pad64, pad128], axis=1))
    assert wA.shape == (D, PROJ_PAD)
    s = np.arange(128)[:, None]
    t = np.arange(128)[None, :]
    seli = np.zeros((128, NH * 128), f32)
    for h in range(NH):
        seli[h, h * 128:(h + 1) * 128] = 1.0
    com = dict(
        A_wA=wA,
        A_convT=np.ascontiguousarray(np.asarray(inputs["L0_attn_conv_weight"], f32).T),
        A_cbT=np.ascontiguousarray(np.asarray(inputs["L0_attn_conv_bias"], f32).reshape(16, 128).T),
        A_nscT=np.ascontiguousarray(np.asarray(inputs["L0_attn_norm_scale"], f32).reshape(8, 128).T),
        A_alog=np.ascontiguousarray(np.asarray(inputs["L0_attn_A_log"], f32).reshape(NH, 1)),
        A_dtbias=np.ascontiguousarray(np.asarray(inputs["L0_attn_dt_bias"], f32).reshape(NH, 1)),
        A_wo_in=np.ascontiguousarray(np.asarray(inputs["L0_attn_w_out"], f32)),
        A_caus=(s <= t).astype(f32),
        A_seli=seli,
        B_W1=np.asarray(inputs["L0_ffn_W1"], f32),
        B_W2=np.asarray(inputs["L0_ffn_W2"], f32),
        B_b1T=np.ascontiguousarray(np.asarray(inputs["L0_ffn_b1"], f32).reshape(32, 128).T),
        B_b2T=np.ascontiguousarray(np.asarray(inputs["L0_ffn_b2"], f32).reshape(8, 128).T),
        C_Wq=np.asarray(inputs["L1_attn_Wq"], f32),
        C_Wk=np.asarray(inputs["L1_attn_Wk"], f32),
        C_Wv=np.asarray(inputs["L1_attn_Wv"], f32),
        C_Wg=np.asarray(inputs["L1_attn_Wg"], f32),
        C_Wo=np.asarray(inputs["L1_attn_Wo"], f32),
        C_bg=np.ascontiguousarray(np.asarray(inputs["L1_attn_bg"], f32).reshape(1, D)),
        D_wg=np.asarray(inputs["L1_ffn_w_gate"], np.int32),
        D_wu=np.asarray(inputs["L1_ffn_w_up"], np.int32),
        D_wd=np.asarray(inputs["L1_ffn_w_down"], np.int32),
        identf=np.eye(128, dtype=f32),
    )
    xs = np.asarray(inputs["L0_attn_x"], np.float32)
    maps = []
    for b in range(xs.shape[0]):
        m = dict(com)
        m["x_in"] = np.ascontiguousarray(xs[b].T)
        maps.append(m)
    return maps


def kernel(**inputs):
    maps = _prep_inputs(inputs)
    nc = build_fused().build()
    res = run_bass_kernel_spmd(nc, maps, core_ids=list(range(len(maps))))
    out = np.stack([np.ascontiguousarray(r["x_out"].T) for r in res.results])
    return out.astype(np.float32)


# version 10
# speedup vs baseline: 2.2872x; 1.0240x over previous
import numpy as np
import ml_dtypes
from contextlib import ExitStack

import concourse.bass as bass
import concourse.mybir as mybir
from concourse.bass_utils import run_bass_kernel_spmd
from concourse.alu_op_type import AluOpType as ALU

F32 = mybir.dt.float32
BF16 = mybir.dt.bfloat16
I32 = mybir.dt.int32
AF = mybir.ActivationFunctionType
AX = mybir.AxisListType
bfdt = ml_dtypes.bfloat16

D = 1024
T = 2048
DI = 2048
NH = 32
DS = 64
HD = 64
KC = 8
Q = 128
PROJ_PAD = 4608
N_CORES = 8

ENGS = ("sp", "act", "dve", "pool", "pe")
_ENG_ATTR = {"sp": "sync", "act": "scalar", "dve": "vector", "pool": "gpsimd", "pe": "tensor"}


class _RecInstr:
    def then_inc(self, sem, n):
        return None


class _Rec:

    def __init__(self):
        self.w = []
        self.r = []
        self.calls = []

    def __getattr__(self, nm):
        rec = self

        def call(*a, **kw):
            rec.calls.append((nm, a))
            aps = [v for v in list(a) + list(kw.values()) if _apish(v)]
            if aps:
                rec.w.append(aps[0])
                rec.r.extend(aps[1:])
            return _RecInstr()

        return call


def _apish(v):
    cn = type(v).__name__
    return cn == "AP" or cn.endswith("TensorHandle")


_TSET = {"Exp": "nle", "Ln": "nle", "Softplus": "nle",
         "Silu": "silu", "Gelu": "gelu", "Sigmoid": "sigmoid"}


def _ap_rect(x):
    cn = type(x).__name__
    if cn.endswith("TensorHandle"):
        return (x.name, 0, 1 << 30, 0, 1 << 30)
    t = x.tensor
    esz = mybir.dt.size(x.dtype)
    tsz = mybir.dt.size(t.dtype)
    wb = tsz
    for s in list(t.shape)[1:]:
        wb *= int(s)
    off_b = x.offset * esz
    plo, flo = off_b // wb, off_b % wb
    pspan = fspan = 0
    for stride, n in x.ap:
        ext = (int(n) - 1) * int(stride) * esz
        if ext == 0:
            continue
        st_b = int(stride) * esz
        if ext < 0:
            return (x.name, 0, 1 << 30, 0, 1 << 30)
        if st_b % wb == 0:
            pspan += ext // wb
        elif st_b < wb:
            fspan += ext
        else:
            return (x.name, 0, 1 << 30, 0, 1 << 30)
    if flo + fspan + esz > wb:
        return (x.name, plo, plo + pspan + (flo + fspan) // wb + 1, 0, wb)
    return (x.name, plo, plo + pspan + 1, flo, flo + fspan + esz)


class PB:

    def __init__(self, sim_safe=False, serial=False, relax=()):
        self.nc = bass.Bass("TRN2", target_bir_lowering=False)
        self.sim_safe = sim_safe
        self.serial = serial
        self.relax = frozenset(relax)
        self.st = ExitStack()
        self.scopes = []
        self.ops = {e: [] for e in ENGS}
        self.sems = {}
        self.cnt = {}
        self._acc = {e: {} for e in ("act", "dve", "pool")}
        self._bar = {}
        self._tbl = None
        self.strict_rng = {}
        for s in ("pe", "act", "dve", "pool"):
            self.mksem(s)

    def mksem(self, s):
        if s not in self.sems:
            self.sems[s] = self.st.enter_context(self.nc.semaphore(s))
            self.cnt[s] = 0

    def push_scope(self):
        self.scopes.append(ExitStack())

    def pop_scope(self):
        self.scopes.pop().close()

    def _stack(self):
        return self.scopes[-1] if getattr(self, "scopes", None) else self.st

    def sbuf(self, name, shape, dtype):
        name = getattr(self, "name_prefix", "") + name
        return self._stack().enter_context(self.nc.sbuf_tensor(name, shape, dtype))

    def psum(self, name, shape, dtype):
        name = getattr(self, "name_prefix", "") + name
        return self._stack().enter_context(self.nc.psum_tensor(name, shape, dtype))

    def fence(self, sp=True):
        w = [(s, self.cnt[s]) for s in list(self.cnt) if self.cnt[s] > 0]
        self.op("pe", lambda e: e.nop(), w)
        self.op("act", lambda e: e.nop(), w)
        self.op("dve", lambda e: e.engine_nop(), w)
        self.op("pool", lambda e: e.engine_nop(), w)
        if sp:
            self.op("sp", lambda e: e.nop(), w)
        self._bar = dict(self.cnt)
        for acc in self._acc.values():
            for nm in list(acc):
                kept = [e for e in acc[nm] if e[1] > self._bar.get(e[0], 0)]
                if kept:
                    acc[nm] = kept
                else:
                    del acc[nm]

    def dram(self, name, shape, dtype, kind=None):
        if kind is None:
            return self.nc.dram_tensor(name, shape, dtype).ap()
        return self.nc.dram_tensor(name, shape, dtype, kind=kind).ap()

    def op(self, eng, fn, waits=(), inc=None, n=1, track=True):
        waits = tuple(waits)
        if self.serial and eng in ("pe", "act", "dve", "pool"):
            if self.serial is True:
                tgt = ("pe", "act", "dve", "pool")
            else:
                tgt = self.serial if eng == "pe" else ()
            waits = waits + tuple((s, self.cnt[s]) for s in tgt if s != eng)
            if inc is None and (self.serial is True or eng in ("act", "dve", "pool")):
                inc = eng
        rects = None
        if self.sim_safe and eng in ("pe", "act", "dve", "pool"):
            rng = self.strict_rng.get(eng)
            if eng not in self.relax or (
                    rng is not None and rng[0] <= self.cnt[eng] < rng[1]):
                waits = waits + ((eng, self.cnt[eng]),)
            elif eng != "pe":
                try:
                    rec = _Rec()
                    fn(rec)
                    rects = ([(True, _ap_rect(a)) for a in rec.w]
                             + [(False, _ap_rect(a)) for a in rec.r])
                except Exception:
                    rects = None
                    waits = waits + ((eng, self.cnt[eng]),)
                else:
                    need = {}
                    acc = self._acc[eng]
                    for isw, (nm, pl, ph, fl, fh) in rects:
                        for (sem, c, oisw, opl, oph, ofl, ofh) in acc.get(nm, ()):
                            if ((isw or oisw) and pl < oph and opl < ph
                                    and fl < ofh and ofl < fh):
                                if c > need.get(sem, -1):
                                    need[sem] = c
                    if eng == "act":
                        for nm_c, a_c in rec.calls:
                            if nm_c != "activation" or len(a_c) < 3:
                                continue
                            f = str(a_c[2]).rsplit(".", 1)[-1]
                            ts = _TSET.get(f)
                            if ts is not None and ts != self._tbl:
                                self._tbl = ts
                                if self.cnt[eng] > need.get(eng, -1):
                                    need[eng] = self.cnt[eng]
                    for sem, c in sorted(need.items()):
                        if c > self._bar.get(sem, 0):
                            waits = waits + ((sem, c),)
            if inc is None:
                inc = eng
        v = None
        if inc is not None:
            self.cnt[inc] += n
            v = self.cnt[inc]
        if rects is not None:
            al = self._acc[eng]
            for isw, rect in rects:
                al.setdefault(rect[0], []).append((inc, v, isw) + rect[1:])
        self.ops[eng].append((waits, fn, inc, n))
        return v

    def dma(self, eng, dst, src, waits=(), sem=None, **kw):
        if sem is None:
            sem = "d_" + eng
        self.mksem(sem)
        return self.op(eng, lambda e: e.dma_start(dst, src, **kw), waits, sem, 16)

    def build(self):
        blk = self.st.enter_context(self.nc.Block())
        for e in ENGS:
            ops = self.ops[e]

            def body(eng, ops=ops):
                lastw = {}
                for waits, fn, inc, n in ops:
                    for s, v in waits:
                        if v > 0 and lastw.get(s, 0) < v:
                            eng.wait_ge(self.sems[s], v)
                            lastw[s] = v
                    instr = fn(eng)
                    if inc is not None:
                        assert instr is not None
                        instr.then_inc(self.sems[inc], n)

            getattr(blk, _ENG_ATTR[e])(body)
        self.st.close()
        return self.nc


class RB:

    def __init__(self, n):
        self.rel = [("pe", 0)] * n
        self.i = 0

    def acquire(self):
        k = self.i % len(self.rel)
        self.i += 1
        sem, v = self.rel[k]
        return k, [(sem, v)]

    def release(self, k, sem, v):
        self.rel[k] = (sem, v)



def emit_ln_stats(P, xr, t0, tile, xqb, rb_xq, statr, statb, onesb, ones1b,
                  pk_sum, pk_sq, pk_m, pk_r, bank_rel, pk_s=None, extra_waits=(),
                  rep_extra=()):
    mmS = mmQ = None
    for c in range(KC):
        src = xr[:, c * T + t0: c * T + t0 + tile]
        kb, w = rb_xq.acquire()
        a1 = P.op("act", lambda e, s=src, d=xqb[kb]: e.activation(d[:, 0:tile], s, AF.Copy),
                  list(w) + list(extra_waits), "act")
        mmS = P.op("pe", lambda e, d=xqb[kb], c=c: e.matmul(
            pk_sum[0:1, 0:tile], onesb[:], d[:, 0:tile],
            start=(c == 0), stop=(c == KC - 1)),
            [("act", a1)] + ([bank_rel["sum"]] if c == 0 else []), "pe")
        kb2, w2 = rb_xq.acquire()
        a2 = P.op("act", lambda e, s=src, d=xqb[kb2]: e.activation(d[:, 0:tile], s, AF.Square),
                  list(w2), "act")
        mmQ = P.op("pe", lambda e, d=xqb[kb2], c=c: e.matmul(
            pk_sq[0:1, 0:tile], onesb[:], d[:, 0:tile],
            start=(c == 0), stop=(c == KC - 1)),
            [("act", a2)] + ([bank_rel["sq"]] if c == 0 else []), "pe")
        rb_xq.release(kb, "pe", mmS)
        rb_xq.release(kb2, "pe", mmQ)

    def sl(i):
        return statr[0:1, i * tile:(i + 1) * tile]

    inv = 1.0 / D
    d1 = P.op("dve", lambda e: e.tensor_scalar(sl(0), pk_sum[0:1, 0:tile], inv, None,
                                               ALU.mult, ALU.bypass), [("pe", mmS)], "dve")
    d2 = P.op("dve", lambda e: e.tensor_scalar(sl(1), pk_sq[0:1, 0:tile], inv, None,
                                               ALU.mult, ALU.bypass), [("pe", mmQ)], "dve")
    bank_rel["sum"] = ("dve", d1)
    bank_rel["sq"] = ("dve", d2)
    P.op("dve", lambda e: e.tensor_tensor(sl(2), sl(0), sl(0), ALU.mult))
    d4 = P.op("dve", lambda e: e.tensor_tensor(sl(1), sl(1), sl(2), ALU.subtract),
              inc="dve")
    d4b = P.op("dve", lambda e: e.tensor_scalar(sl(2), sl(1), 1e-5, None,
                                                ALU.add, ALU.bypass), [("dve", d4)], "dve")
    a3 = P.op("act", lambda e: e.activation(sl(2), sl(2), AF.Ln),
              [("dve", d4b)], "act")
    a4 = P.op("act", lambda e: e.activation(sl(3), sl(2), AF.Exp, scale=-0.5),
              [("act", a3)], "act")
    have_s = pk_s is not None
    if have_s:
        d5 = P.op("dve", lambda e: e.tensor_scalar(sl(4), sl(1), 1e-5, None,
                                                   ALU.add, ALU.bypass), [("dve", d4)], "dve")
        P.op("dve", lambda e: e.reciprocal(sl(4), sl(4)))
        P.op("dve", lambda e: e.tensor_tensor(sl(4), sl(1), sl(4), ALU.mult))
        d6 = P.op("dve", lambda e: e.tensor_scalar(sl(4), sl(4), 1e-6, None,
                                                   ALU.add, ALU.bypass), inc="dve")
        a5 = P.op("act", lambda e: e.activation(sl(4), sl(4), AF.Ln), [("dve", d6)], "act")
        a6 = P.op("act", lambda e: e.activation(sl(4), sl(4), AF.Exp, scale=-0.5),
                  inc="act")
        d7 = P.op("dve", lambda e: e.tensor_tensor(sl(5), sl(3), sl(4), ALU.mult),
                  [("act", a6)], "dve")

    def sb(i):
        return statb[0:1, i * tile:(i + 1) * tile]

    reps = [("m", sl(0), 0, pk_m, d1, 6), ("r", sl(3), 2, pk_r, a4, 7)]
    if have_s:
        reps.append(("s", sl(5), 4, pk_s, d7, 2))
    out = {}
    for name, row, bi, pk, wv, tmpi in reps:
        wsem = "dve" if name in ("m", "s") else "act"
        ah = P.op("act", lambda e, r=row, o=sb(bi): e.activation(o, r, AF.Copy),
                  [(wsem, wv)], "act")
        dl = P.op("dve", lambda e, r=row, h=sb(bi), o=sl(tmpi): e.tensor_tensor(o, r, h, ALU.subtract),
                  [("act", ah)], "dve")
        al = P.op("act", lambda e, o=sb(bi + 1), r=sl(tmpi): e.activation(o, r, AF.Copy),
                  [("dve", dl)], "act")
        m1 = P.op("pe", lambda e, pk=pk, h=sb(bi): e.matmul(
            pk, ones1b[:], h, start=True, stop=False),
            [("act", ah), bank_rel[name]] + list(rep_extra), None)
        m2 = P.op("pe", lambda e, pk=pk, l=sb(bi + 1): e.matmul(
            pk, ones1b[:], l, start=False, stop=True),
            [("act", al)], "pe")
        out[name] = m2
    out["dtail"] = ("dve", P.cnt["dve"])
    return out



def emit_phase_A(P, sh, tile=256):
    ntiles = T // tile
    qpt = tile // Q
    upto = 'full'; max_tiles = None; scan_heads = NH; no_ops = ()
    xr = sh["xr"]; sid = sh["sid"]; sidb = sh["sidb"]
    onesb = sh["onesb"]; ones1b = sh["ones1b"]

    wA = P.dram("A_wA", [D, PROJ_PAD], F32, kind="ExternalInput")
    convT = P.dram("A_convT", [DI, 4], F32, kind="ExternalInput")
    cbT = P.dram("A_cbT", [128, 16], F32, kind="ExternalInput")
    nscT = P.dram("A_nscT", [128, 8], F32, kind="ExternalInput")
    alog = P.dram("A_alog", [NH, 1], F32, kind="ExternalInput")
    dtbias = P.dram("A_dtbias", [NH, 1], F32, kind="ExternalInput")
    wo_in = P.dram("A_wo_in", [DI, D], F32, kind="ExternalInput")
    caus = P.dram("A_caus", [128, 128], F32, kind="ExternalInput")
    seli = P.dram("A_seli", [128, NH * 128], F32, kind="ExternalInput")
    wab = P.dram("A_wab_scr", [D, PROJ_PAD], BF16)
    wob = P.dram("A_wob_scr", [DI, D], BF16)
    dbg_hooks = set()

    xnb = P.sbuf("xnb", [128, KC * tile], BF16)
    wbuf = [P.sbuf(f"wbuf{i}", [128, KC * 256], BF16) for i in range(3)]
    wobuf = [P.sbuf(f"wobuf{i}", [128, 16 * 128], BF16) for i in range(3)]
    sgate = P.sbuf("sgate", [128, qpt * DI], BF16)
    sbm = P.sbuf("sbm", [128, 16 * (tile + 3)], BF16)
    hbuf = P.sbuf("hbuf", [128, 16 * tile], BF16)
    hT = [P.sbuf(f"hT{i}", [128, DI], BF16) for i in range(2)]
    ybuf = [P.sbuf(f"ybuf{i}", [128, DI], BF16) for i in range(2)]
    ygT = P.sbuf("ygT", [128, 16 * tile], BF16)
    state = P.sbuf("state", [64, NH * HD], F32)
    stateb = P.sbuf("stateb", [128, NH * HD], BF16)
    Bq = P.sbuf("Bq", [128, tile], BF16)
    Cq = P.sbuf("Cq", [128, tile], BF16)
    dtraw = P.sbuf("dtraw", [32, tile], F32)
    lnb = [P.sbuf(f"lnb{i}", [128, tile], F32) for i in range(2)]
    t1b = [P.sbuf(f"t1b{i}", [128, tile], F32) for i in range(2)]
    cvt = [P.sbuf(f"cvt{i}", [128, tile], BF16) for i in range(2)]
    xqb = [P.sbuf(f"xqb{i}", [128, tile], BF16) for i in range(4)]
    statr = P.sbuf("statr", [1, 8 * tile], F32)
    statb = P.sbuf("statb", [1, 6 * tile], BF16)
    dtv = P.sbuf("dtv", [32, 128], F32)
    Lbuf = P.sbuf("Lbuf", [32, 128], F32)
    edec = P.sbuf("edec", [32, 128], F32)
    dtrow = P.sbuf("dtrow", [32, 128], F32)
    eLend = P.sbuf("eLend", [32, 1], F32)
    Lhi = P.sbuf("Lhi", [128, 128], BF16)
    Llo = P.sbuf("Llo", [128, 128], BF16)
    Ltmp = P.sbuf("Ltmp", [32, 128], F32)
    tev = P.sbuf("tev", [128, 128], F32)
    negLT = P.sbuf("negLT", [128, 32], F32)
    expLT = P.sbuf("expLT", [128, 32], F32)
    Gs = P.sbuf("Gs", [128, 128], BF16)
    Gm = [P.sbuf(f"Gm{i}", [128, 128], BF16) for i in range(2)]
    Dc = [P.sbuf(f"Dc{i}", [128, 128], F32) for i in range(4)]
    Eb = [P.sbuf(f"Eb{i}", [128, 128], F32) for i in range(4)]
    Mb = [P.sbuf(f"Mb{i}", [128, 128], BF16) for i in range(4)]
    XdTb = [P.sbuf(f"XdTb{i}", [128, 64], BF16) for i in range(4)]
    XdT2b = [P.sbuf(f"XdT2b{i}", [128, 64], BF16) for i in range(4)]
    BTs = P.sbuf("BTs", [128, 128], BF16)
    scaus = P.sbuf("scaus", [128, 128], BF16)
    sselb = P.sbuf("sselb", [128, NH * 128], BF16)
    scw = P.sbuf("scw", [128, 16 * 4], F32)
    scb = P.sbuf("scb", [128, 16], F32)
    snsc = P.sbuf("snsc", [128, 8], F32)
    salog = P.sbuf("salog", [32, 1], F32)
    sdtb = P.sbuf("sdtb", [32, 1], F32)
    negeA = P.sbuf("negeA", [32, 1], F32)
    ones32 = P.sbuf("ones32", [32, 128], F32)
    zero32 = P.sbuf("zero32", [32, 1], F32)

    pk0 = P.psum("pk0", [128, 512], F32)
    pk1 = P.psum("pk1", [128, 512], F32)
    pk2 = P.psum("pk2", [128, 512], F32)
    pk3 = P.psum("pk3", [128, 512], F32)
    pk4 = P.psum("pk4", [128, 512], F32)
    pk5 = P.psum("pk5", [128, 1024], BF16)
    pk6 = P.psum("pk6", [128, 1024], BF16)
    pk7 = P.psum("pk7", [128, 512], F32)

    P.dma("sp", scw[:].rearrange("p (c k) -> p c k", c=16),
          convT.rearrange("(c p) k -> p c k", p=128), sem="initA")
    P.dma("sp", scb[:], cbT, sem="initA")
    P.dma("sp", snsc[:], nscT, sem="initA")
    P.dma("sp", salog[:], alog, sem="initA")
    P.dma("sp", sdtb[:], dtbias, sem="initA")
    w_init = P.cnt["initA"]
    w_wab = P.dma("pool", wab, wA, max_dma_last_dim=8176, sem="cast_a")
    w_wob = P.dma("pool", wob, wo_in, max_dma_last_dim=8176, sem="cast_b")
    w_caus = P.dma("pool", scaus[:], caus, sem="cast_c")
    w_sel = P.dma("pool", sselb[:], seli, sem="cast_d")

    P.op("dve", lambda e: e.memset(ones32[:], 1.0))
    P.op("dve", lambda e: e.memset(zero32[:], 0.0))
    P.op("dve", lambda e: e.memset(state[:], 0.0))
    P.op("dve", lambda e: e.memset(BTs[:], 0.0))
    P.op("dve", lambda e: e.memset(Bq[:], 0.0))
    P.op("dve", lambda e: e.memset(Cq[:], 0.0))
    P.op("dve", lambda e: e.memset(Lhi[:], 0.0))
    P.op("dve", lambda e: e.memset(Llo[:], 0.0))
    d_init = P.op("dve", lambda e: e.memset(stateb[:], 0.0), inc="dve")
    p_halo = P.op("pool", lambda e: e.memset(sbm[:], 0.0), inc="pool")
    a_eA = P.op("act", lambda e: e.activation(negeA[:], salog[:], AF.Exp),
                [("initA", w_init)], "act")
    d_eA = P.op("dve", lambda e: e.tensor_scalar(negeA[:], negeA[:], -1.0, None,
                                                 ALU.mult, ALU.bypass),
                [("act", a_eA)], "dve")

    fence_w = [("initA", w_init), ("dve", d_init), ("pool", p_halo),
               ("cast_c", w_caus), ("cast_d", w_sel), ("act", a_eA),
               ("dve", d_eA)]
    P.op("pe", lambda e: e.nop(), fence_w)
    P.op("act", lambda e: e.nop(), fence_w)
    P.op("dve", lambda e: e.engine_nop(), fence_w)
    P.op("pool", lambda e: e.engine_nop(), fence_w)

    rb_xq = RB(4)
    rb_t1 = RB(2)
    rb_ln = RB(2)
    rb_cvt = RB(2)
    rb_w = RB(3)
    rb_wo = RB(3)
    yrel = [("act", 0)] * 4
    yrel2 = [("dve", 0)] * 4
    srel = [("dve", 0)] * 4
    erel = [("dve", 0)] * 2
    trel = ("act", 0)
    bf16rel = [("act", 0)] * 2
    bf16turn = [0]
    Grel = ("act", 0)
    hTrel = [("pool", 0)] * 2
    ybrel = [("act", 0)] * 2
    gmrel = ("pool", 0)
    gsrel = ("pool", 0)
    tev_dve_rel = 0
    tev_pool_rel = 0
    aE_prev = [("act", 0)] * 4
    pm_prev = [("pool", 0)] * 4
    y2_prev = [("pe", 0)] * 4
    xd2_prev = [("dve", 0)] * 4
    s_prev = [("pe", 0)] * 4
    bank_rel = {"sum": ("dve", 0), "sq": ("dve", 0), "m": ("dve", 0),
                "r": ("dve", 0), "s": ("dve", 0)}
    sbm_cons = [("dve", 0), ("pool", 0)]
    prev_state_b = [("dve", d_init)] * NH
    last_wout_dve = 0
    statefin = []

    for i in range(ntiles if max_tiles is None else max_tiles):
        t0 = i * tile

        repv = emit_ln_stats(P, xr, t0, tile, xqb, rb_xq, statr, statb, onesb,
                             ones1b, pk0, pk1, pk2[:, 0:tile], pk3[:, 0:tile],
                             bank_rel, pk_s=pk2[:, 256:256 + tile],
                             extra_waits=[("initA", w_init),
                                          ("dve", last_wout_dve)],
                             rep_extra=[trel, Grel, erel[0], erel[1]])

        p_x_last = 0
        for c in range(KC):
            xc = xr[:, c * T + t0: c * T + t0 + tile]
            kt, wt = rb_t1.acquire()
            d_t1 = P.op("dve", lambda e, x=xc, kt=kt: e.tensor_tensor(
                t1b[kt][:], x, pk2[:, 0:tile], ALU.subtract),
                wt + [("pe", repv["m"]), ("pe", repv["s"]),
                      ("dve", last_wout_dve), repv["dtail"]], "dve")
            kl, wl = rb_ln.acquire()
            d_ln = P.op("dve", lambda e, kl=kl, kt=kt: e.tensor_tensor(
                lnb[kl][:], t1b[kt][:], pk3[:, 0:tile], ALU.mult),
                wl + [("pe", repv["r"])], "dve")
            p_x = P.op("pool", lambda e, x=xc, kl=kl: e.tensor_tensor(
                x, x, lnb[kl][:], ALU.add), [("dve", d_ln)], "pool")
            rb_ln.release(kl, "pool", p_x)
            d_xn = P.op("dve", lambda e, kt=kt, c=c: e.scalar_tensor_tensor(
                xnb[:, c * tile:(c + 1) * tile], t1b[kt][:], snsc[:, c:c + 1],
                pk2[:, 256:256 + tile], ALU.mult, ALU.mult),
                [("pe", repv["s"])], "dve")
            rb_t1.release(kt, "dve", d_xn)
            p_x_last = p_x
        d_xn_last = P.cnt["dve"]
        bank_rel["m"] = ("dve", d_xn_last)
        bank_rel["r"] = ("dve", d_xn_last)
        bank_rel["s"] = ("dve", d_xn_last)


        if upto == 'ln':
            continue
        if i > 0:
            for c in range(16):
                off = c * (tile + 3)
                p_halo = P.op("pool", lambda e, off=off: e.tensor_copy(
                    sbm[:, off:off + 3], sbm[:, off + tile:off + tile + 3]),
                    sbm_cons if c == 0 else [], "pool")

        ssm_evac = {}
        dt_evac = B_evac = C_evac = None
        for j in range(8, 18):
            kw, ww = rb_w.acquire()
            w_sl = P.dma("sp", wbuf[kw][:].rearrange("p (k f) -> p k f", k=KC),
                         wA_slab_ap(wab, j), ww + [("cast_a", w_wab)],
                         sem="wd%d" % kw)
            mmv = None
            for sub in range(2):
                fc = j * 2 + sub
                if fc >= 35:
                    continue
                pk, bk = (pk0, "sum") if (fc % 2 == 0) else (pk1, "sq")
                for k in range(KC):
                    mmv = P.op("pe", lambda e, kw=kw, k=k, sub=sub, pk=pk: e.matmul(
                        pk[:, 0:tile],
                        wbuf[kw][:, k * 256 + sub * 128: k * 256 + sub * 128 + 128],
                        xnb[:, k * tile:(k + 1) * tile],
                        start=(k == 0), stop=(k == KC - 1)),
                        [("wd%d" % kw, w_sl), ("dve", d_xn_last)] +
                        ([bank_rel[bk]] if k == 0 else []),
                        "pe" if k == KC - 1 else None)
                if fc <= 31:
                    c = fc - 16
                    off = c * (tile + 3)
                    av = P.op("act", lambda e, pk=pk, off=off: e.activation(
                        sbm[:, off + 3:off + 3 + tile], pk[:, 0:tile], AF.Copy),
                        [("pe", mmv), ("pool", p_halo)], "act")
                    ssm_evac[c] = av
                elif fc == 32:
                    av = dt_evac = P.op("act", lambda e, pk=pk: e.activation(
                        dtraw[:, 0:tile], pk[0:32, 0:tile], AF.Copy),
                        [("pe", mmv)], "act")
                elif fc == 33:
                    av = B_evac = P.op("act", lambda e, pk=pk: e.activation(
                        Bq[0:64, 0:tile], pk[0:64, 0:tile], AF.Copy),
                        [("pe", mmv)], "act")
                else:
                    av = C_evac = P.op("act", lambda e, pk=pk: e.activation(
                        Cq[0:64, 0:tile], pk[0:64, 0:tile], AF.Copy),
                        [("pe", mmv)], "act")
                bank_rel[bk] = ("act", av)
            rb_w.release(kw, "pe", mmv)

        gate_evac = {}
        gbank = {0: (pk0, "sum"), 1: (pk1, "sq")}
        for ns in range(4):
            mm_tc = {0: None, 1: None}
            for jj in range(2):
                j = ns * 2 + jj
                kw, ww = rb_w.acquire()
                w_sl = P.dma("sp", wbuf[kw][:].rearrange("p (k f) -> p k f", k=KC),
                             wA_slab_ap(wab, j), ww + [("cast_a", w_wab)],
                             sem="wd%d" % kw)
                for tc in range(qpt):
                    pk, bk = gbank[tc]
                    for k in range(KC):
                        mm_tc[tc] = P.op("pe", lambda e, kw=kw, k=k, jj=jj, tc=tc, pk=pk: e.matmul(
                            pk[:, jj * 256:(jj + 1) * 256],
                            xnb[:, k * tile + tc * 128: k * tile + tc * 128 + 128],
                            wbuf[kw][:, k * 256:(k + 1) * 256],
                            start=(k == 0), stop=(k == KC - 1)),
                            [("wd%d" % kw, w_sl), ("dve", d_xn_last)] +
                            ([bank_rel[bk]] if (k == 0 and jj == 0) else []),
                            "pe" if k == KC - 1 else None)
                rb_w.release(kw, "pe", mm_tc[qpt - 1])
            for tc in range(qpt):
                pk, bk = gbank[tc]
                av = P.op("act", lambda e, pk=pk, tc=tc, ns=ns: e.activation(
                    sgate[:, tc * DI + ns * 512: tc * DI + (ns + 1) * 512],
                    pk[:, 0:512], AF.Silu), [("pe", mm_tc[tc])], "act")
                gate_evac[(tc, ns)] = av
                bank_rel[bk] = ("act", av)

        if upto == 'proj':
            continue
        h_evac = {}
        for c in range(16):
            eng = "dve"
            kc_, wc = rb_cvt.acquire()
            off = c * (tile + 3)
            waits = wc + [("act", ssm_evac[c]), ("pool", p_halo),
                          ("initA", w_init)]
            P.op(eng, lambda e, kc_=kc_, c=c, off=off: e.tensor_scalar(
                cvt[kc_][:], sbm[:, off:off + tile], scw[:, c * 4 + 0:c * 4 + 1],
                scb[:, c:c + 1], ALU.mult, ALU.add), waits)
            civ = None
            for k in (1, 2, 3):
                civ = P.op(eng, lambda e, kc_=kc_, c=c, k=k, off=off: e.scalar_tensor_tensor(
                    cvt[kc_][:], sbm[:, off + k:off + k + tile],
                    scw[:, c * 4 + k:c * 4 + k + 1], cvt[kc_][:],
                    ALU.mult, ALU.add), inc=(eng if k == 3 else None))
            av = P.op("act", lambda e, kc_=kc_, c=c: e.activation(
                hbuf[:, c * tile:(c + 1) * tile], cvt[kc_][:], AF.Silu),
                [(eng, civ)], "act")
            h_evac[c] = av
            rb_cvt.release(kc_, "act", av)
        h_all = P.cnt["act"]


        if upto == 'conv':
            continue
        yrel[2] = yrel2[2] = srel[2] = bank_rel["sum"]
        yrel[3] = yrel2[3] = srel[3] = bank_rel["sq"]
        for q in range(qpt):
            g = i * qpt + q
            dtq = dtraw[:, q * Q:(q + 1) * Q]
            a_abs = P.op("act", lambda e, s=dtq: e.activation(
                Ltmp[:], s, AF.Abs, bias=sdtb[:, 0:1]),
                [("act", dt_evac), ("initA", w_init), ("dve", P.cnt["dve"])], "act")
            a_e = P.op("act", lambda e: e.activation(Ltmp[:], Ltmp[:], AF.Exp,
                                                     scale=-1.0), inc="act")
            a_l = P.op("act", lambda e: e.activation(Ltmp[:], Ltmp[:], AF.Ln,
                                                     bias=1.0), inc="act")
            d_r = P.op("dve", lambda e, s=dtq: e.tensor_scalar(
                dtv[:], s, sdtb[:, 0:1], 0.0, ALU.add, ALU.max),
                [("act", dt_evac), ("pe", P.cnt["pe"])], "dve")
            d_dt = P.op("dve", lambda e: e.tensor_tensor(
                dtv[:], dtv[:], Ltmp[:], ALU.add),
                [("act", a_l)], "dve")
            d_a = P.op("dve", lambda e: e.tensor_scalar(
                Ltmp[:], dtv[:], negeA[:, 0:1], None, ALU.mult, ALU.bypass),
                [("dve", d_eA)], "dve")
            d_L = P.op("dve", lambda e: e.tensor_tensor_scan(
                Lbuf[:], ones32[:], Ltmp[:], zero32[:, 0:1],
                ALU.mult, ALU.add), [("dve", d_a), ("pe", P.cnt["pe"])], "dve")
            a_ed = P.op("act", lambda e: e.activation(
                edec[:], Lbuf[:], AF.Exp, scale=-1.0,
                bias=Lbuf[:, 127:128]), [("dve", d_L)], "act")
            a_dt0 = P.op("act", lambda e: e.activation(
                eLend[:], Lbuf[:, 127:128], AF.Exp), [("dve", d_L)], "act")
            d_dtr = P.op("dve", lambda e: e.tensor_scalar(
                dtrow[:], ones32[:], eLend[:, 0:1], None, ALU.mult, ALU.bypass),
                [("act", a_dt0), ("pe", P.cnt["pe"])], "dve")
            a_hi = P.op("act", lambda e: e.activation(Lhi[0:32, :], Lbuf[:], AF.Copy),
                        [("dve", d_L), ("pe", P.cnt["pe"])], "act")
            d_lo = P.op("dve", lambda e: e.tensor_tensor(
                Ltmp[:], Lbuf[:], Lhi[0:32, :], ALU.subtract), [("act", a_hi)], "dve")
            a_lo = P.op("act", lambda e: e.activation(Llo[0:32, :], Ltmp[:], AF.Copy),
                        [("dve", d_lo)], "act")
            id32 = sid[0:32, 0:32]
            pe_t1 = P.op("pe", lambda e: e.transpose(pk3[:, 256:288], dtv[:], id32),
                         [("dve", d_dt), ("dve", d_a), trel, Grel, erel[1],
                          ("initA", w_init)], None)
            pe_t1 = P.op("pe", lambda e: e.transpose(pk3[:, 288:320], Lbuf[:], id32),
                         [("dve", d_L), ("act", a_hi), ("dve", d_lo)], None)
            pe_t1 = P.op("pe", lambda e: e.transpose(pk3[:, 320:352], edec[:], id32),
                         [("act", a_ed)], None)
            pe_t1 = P.op("pe", lambda e: e.transpose(pk3[:, 352:384], dtrow[:], id32),
                         [("dve", d_dtr)], "pe")
            a_tev = P.op("act", lambda e: e.activation(tev[:], pk3[:, 256:384], AF.Copy),
                         [("pe", pe_t1), ("dve", tev_dve_rel), ("pool", tev_pool_rel)],
                         "act")
            trel = ("act", a_tev)
            a_dtotT = a_tev
            d_nL = P.op("dve", lambda e: e.tensor_scalar(
                negLT[:], tev[:, 32:64], -1.0, None, ALU.mult, ALU.bypass),
                [("act", a_tev)], "dve")
            a_eL = P.op("act", lambda e: e.activation(expLT[:], tev[:, 32:64], AF.Exp),
                        [("act", a_tev)], "act")
            bt = bf16turn[0] % 2
            bf16turn[0] += 1
            pkb = pk5 if bt == 0 else pk6
            pe_t3 = P.op("pe", lambda e, q=q, pkb=pkb: e.transpose(
                pkb[:, 0:64], Bq[0:64, q * Q:(q + 1) * Q], sidb[0:64, 0:64]),
                [("act", B_evac), bf16rel[bt]], "pe")
            a_BT = P.op("act", lambda e, pkb=pkb: e.activation(
                BTs[:, 0:64], pkb[:, 0:64], AF.Copy), [("pe", pe_t3)], "act")
            bf16rel[bt] = ("act", a_BT)
            pe_G = P.op("pe", lambda e, q=q: e.matmul(
                pk3[:, 384:512], Bq[:, q * Q:(q + 1) * Q], Cq[:, q * Q:(q + 1) * Q],
                start=True, stop=True),
                [("act", B_evac), ("act", C_evac), Grel, ("act", a_tev),
                 erel[1]], "pe")
            a_G = P.op("act", lambda e: e.activation(Gs[:], pk3[:, 384:512], AF.Copy),
                       [("pe", pe_G), gsrel], "act")
            Grel = ("act", a_G)
            kh = g % 2
            hT_ev = 0
            for c in range(16):
                bt = bf16turn[0] % 2
                bf16turn[0] += 1
                pkb = pk5 if bt == 0 else pk6
                pe_th = P.op("pe", lambda e, c=c, q=q, pkb=pkb: e.transpose(
                    pkb[:, 128:256],
                    hbuf[:, c * tile + q * Q: c * tile + q * Q + 128], sidb[:]),
                    [("act", h_evac[c]), bf16rel[bt]] +
                    ([hTrel[kh]] if c == 0 else []), "pe")
                av = P.op("act", lambda e, c=c, kh=kh, pkb=pkb: e.activation(
                    hT[kh][:, c * 128:(c + 1) * 128],
                    pkb[:, 128:256], AF.Copy),
                    [("pe", pe_th)], "act")
                bf16rel[bt] = ("act", av)
                hT_ev = av
            if scan_heads == 0:
                continue
            p_gmc = P.op("pool", lambda e: e.tensor_tensor(
                Gm[0][:], Gs[:], scaus[:], ALU.mult),
                [("act", a_G), ("cast_c", w_caus), gmrel], "pool")
            gsrel = ("pool", p_gmc)
            d_yg_last = 0
            G4 = 4
            for hg in range(0, scan_heads, G4):
                hs = list(range(hg, min(hg + G4, scan_heads)))
                st = {h: {} for h in hs}
                for h in hs:
                    ks = h % 4
                    pke = pk2 if h % 2 == 0 else pk3
                    P.op("pe", lambda e, h=h, pke=pke: e.matmul(
                        pke[:, 0:128],
                        sselb[:, h * 128:(h + 1) * 128], Lhi[:],
                        start=True, stop=False),
                        [("act", a_hi), ("cast_d", w_sel), erel[h % 2],
                         ("dve", d_xn_last)] +
                        ([("act", a_tev), ("act", a_G)] if h % 2 == 1 else []),
                        None)
                    pe_rep = P.op("pe", lambda e, h=h, pke=pke: e.matmul(
                        pke[:, 0:128],
                        sselb[:, h * 128:(h + 1) * 128], Llo[:],
                        start=False, stop=True), [("act", a_lo)], "pe")
                    d_dc = P.op("dve", lambda e, ks=ks, h=h, pke=pke: e.tensor_scalar(
                        Dc[ks][:], pke[:, 0:128],
                        negLT[:, h:h + 1], 0.0, ALU.add, ALU.min),
                        [("pe", pe_rep), ("dve", d_nL), aE_prev[ks]], "dve")
                    erel[h % 2] = ("dve", d_dc)
                    a_E = P.op("act", lambda e, ks=ks: e.activation(
                        Eb[ks][:], Dc[ks][:], AF.Exp),
                        [("dve", d_dc), pm_prev[ks]], "act")
                    aE_prev[ks] = ("act", a_E)
                for h in hs:
                    ks = h % 4
                    p_m = P.op("pool", lambda e, ks=ks: e.tensor_tensor(
                        Mb[ks][:], Eb[ks][:], Gm[0][:], ALU.mult),
                        [aE_prev[ks], ("pool", p_gmc), y2_prev[ks]], "pool")
                    pm_prev[ks] = ("pool", p_m)
                    p_xd = P.op("dve", lambda e, ks=ks, h=h, kh=kh: e.tensor_scalar(
                        XdTb[ks][:], hT[kh][:, h * 64:(h + 1) * 64],
                        tev[:, h:h + 1], None, ALU.mult, ALU.bypass),
                        [("act", hT_ev), ("act", a_tev), y2_prev[ks],
                         xd2_prev[ks]], "dve")
                    p_xd2 = P.op("dve", lambda e, ks=ks, h=h: e.tensor_scalar(
                        XdT2b[ks][:], XdTb[ks][:], tev[:, 64 + h:64 + h + 1],
                        None, ALU.mult, ALU.bypass),
                        [("dve", p_xd), s_prev[ks]], "dve")
                    xd2_prev[ks] = ("dve", p_xd2)
                    st[h].update(p_m=p_m, p_xd=p_xd, p_xd2=p_xd2)
                for h in hs:
                    ks = h % 4
                    par = h % 4
                    pkp = (pk4, pk7, pk0, pk1)[par]
                    pe_yi = P.op("pe", lambda e, h=h, q=q, pkp=pkp: e.matmul(
                        pkp[:, 0:64], Cq[:, q * Q:(q + 1) * Q],
                        stateb[:, h * 64:(h + 1) * 64], start=True, stop=True),
                        [("act", C_evac), prev_state_b[h], yrel[par],
                         yrel2[par], srel[par]], "pe")
                    pe_y2 = P.op("pe", lambda e, ks=ks, pkp=pkp: e.matmul(
                        pkp[:, 64:128], Mb[ks][:],
                        XdTb[ks][:], start=True, stop=True),
                        [("pool", st[h]["p_m"]), ("dve", st[h]["p_xd"])], "pe")
                    y2_prev[ks] = ("pe", pe_y2)
                    pe_s = P.op("pe", lambda e, ks=ks, pkp=pkp: e.matmul(
                        pkp[:, 128:192], BTs[:],
                        XdT2b[ks][:], start=True, stop=True),
                        [("dve", st[h]["p_xd2"]), ("act", a_BT)], "pe")
                    st[h]["pe_s"] = pe_s
                for h in hs:
                    par = h % 4
                    pkp = (pk4, pk7, pk0, pk1)[par]
                    pe_s = st[h]["pe_s"]
                    a_yi = P.op("act", lambda e, h=h, kh=kh, pkp=pkp: e.activation(
                        ybuf[kh][:, h * 64:(h + 1) * 64],
                        pkp[:, 0:64], AF.Copy,
                        scale=expLT[:, h:h + 1]),
                        [("pe", pe_s), ("act", a_eL)] +
                        ([ybrel[kh]] if h == 0 else []), "act")
                    yrel[par] = ("act", a_yi)
                    d_yg = P.op("dve", lambda e, h=h, kh=kh, pkp=pkp: e.tensor_tensor(
                        ybuf[kh][:, h * 64:(h + 1) * 64],
                        ybuf[kh][:, h * 64:(h + 1) * 64],
                        pkp[:, 64:128], ALU.add),
                        [("pe", pe_s), ("act", a_yi)], "dve")
                    yrel2[par] = ("dve", d_yg)
                    d_yg_last = d_yg
                    d_st = P.op("dve", lambda e, h=h, pkp=pkp: e.scalar_tensor_tensor(
                        state[:, h * 64:(h + 1) * 64],
                        state[:, h * 64:(h + 1) * 64],
                        tev[0:64, 96 + h:97 + h],
                        pkp[0:64, 128:192],
                        ALU.mult, ALU.add), [("pe", pe_s), ("act", a_dtotT)], "dve")
                    srel[par] = ("dve", d_st)
                    p_stb = P.op("pool", lambda e, h=h: e.tensor_copy(
                        stateb[0:64, h * 64:(h + 1) * 64],
                        state[:, h * 64:(h + 1) * 64]),
                        [("dve", d_st)], "pool")
                    prev_state_b[h] = ("pool", p_stb)
            hTrel[kh] = ("dve", P.cnt["dve"])
            gmrel = ("pool", P.cnt["pool"])
            tev_dve_rel = P.cnt["dve"]
            tev_pool_rel = P.cnt["pool"]

            d_gate = P.op("dve", lambda e, kh=kh, q=q: e.tensor_tensor(
                ybuf[kh][:], ybuf[kh][:], sgate[:, q * DI:(q + 1) * DI], ALU.mult),
                [("dve", d_yg_last)] +
                [("act", gate_evac[(q, ns)]) for ns in range(4)], "dve")
            d_yg_last = d_gate

            ygT_ev = 0
            for c in range(16):
                bt = bf16turn[0] % 2
                bf16turn[0] += 1
                pkb = pk5 if bt == 0 else pk6
                pe_ty = P.op("pe", lambda e, c=c, kh=kh, pkb=pkb: e.transpose(
                    pkb[:, 128:256],
                    ybuf[kh][:, c * 128:(c + 1) * 128], sidb[:]),
                    [("dve", d_yg_last), bf16rel[bt]], "pe")
                av = P.op("act", lambda e, c=c, q=q, pkb=pkb: e.activation(
                    ygT[:, c * tile + q * Q: c * tile + q * Q + 128],
                    pkb[:, 128:256], AF.Copy),
                    [("pe", pe_ty)], "act")
                bf16rel[bt] = ("act", av)
                ygT_ev = av
            ybrel[kh] = ("act", ygT_ev)

        sbm_cons = [("dve", P.cnt["dve"]), ("pool", P.cnt["pool"])]
        if upto == 'scan':
            continue
        bank_rel["sum"] = srel[2]
        bank_rel["sq"] = srel[3]

        for dm in range(KC):
            kw, ww = rb_wo.acquire()
            w_sl = P.dma("sp", wobuf[kw][:].rearrange("p (k d) -> p k d", k=16),
                         wo_slab_ap(wob, dm), ww + [("cast_b", w_wob)],
                         sem="wod%d" % kw)
            pk, bk = (pk0, "sum") if (dm % 2 == 0) else (pk1, "sq")
            mmv = None
            for fc in range(16):
                mmv = P.op("pe", lambda e, kw=kw, fc=fc, pk=pk: e.matmul(
                    pk[:, 0:tile], wobuf[kw][:, fc * 128:(fc + 1) * 128],
                    ygT[:, fc * tile:(fc + 1) * tile],
                    start=(fc == 0), stop=(fc == 15)),
                    [("wod%d" % kw, w_sl), ("act", ygT_ev)] +
                    ([bank_rel[bk]] if fc == 0 else []),
                    "pe" if fc == 15 else None)
            rb_wo.release(kw, "pe", mmv)
            xc = xr[:, dm * T + t0: dm * T + t0 + tile]
            dv = P.op("dve", lambda e, x=xc, pk=pk: e.tensor_tensor(
                x, x, pk[:, 0:tile], ALU.add),
                [("pe", mmv), ("pool", p_x_last)], "dve")
            bank_rel[bk] = ("dve", dv)
            last_wout_dve = dv

    return




def emit_phase_B(P, sh, tile=256):
    ntiles = T // tile
    FF = 4096
    NFC = FF // 128
    xr = sh["xr"]; sid = sh["sid"]; sidb = sh["sidb"]
    onesb = sh["onesb"]; ones1b = sh["ones1b"]

    W1 = P.dram("B_W1", [FF, D], F32, kind="ExternalInput")
    W2 = P.dram("B_W2", [D, FF], F32, kind="ExternalInput")
    b1T = P.dram("B_b1T", [128, NFC], F32, kind="ExternalInput")
    b2T = P.dram("B_b2T", [128, KC], F32, kind="ExternalInput")

    w2ts = P.dram("B_w2t_scr", [NFC, 128, 1024], BF16)
    w1t = P.sbuf("w1t", [128, KC * FF], BF16)
    w2sl = [P.sbuf(f"w2sl{i}", [128, 8 * 1024], BF16) for i in range(2)]
    w2st = [P.sbuf(f"w2st{i}", [128, 1024], BF16) for i in range(2)]
    xn2 = P.sbuf("xn2", [128, KC * tile], BF16)
    ws = [P.sbuf(f"ws{i}", [128, 1024], F32) for i in range(2)]
    wtt = [P.sbuf(f"wtt{i}", [128, 1024], BF16) for i in range(2)]
    tmpb = P.sbuf("tmpb", [128, 1024], BF16)
    hball = P.sbuf("hball", [128, NFC * tile], BF16)
    xqb = [P.sbuf(f"xqb{i}", [128, tile], BF16) for i in range(2)]
    statr = P.sbuf("statr", [1, 5 * tile], F32)
    statb = P.sbuf("statb", [1, 4 * tile], BF16)
    acc1 = P.sbuf("acc1", [128, 2], F32)
    accb = P.sbuf("accb", [128, 4], BF16)
    thr = P.sbuf("thr", [128, 2], F32)
    nthr = P.sbuf("nthr", [128, 2], F32)
    rowt = P.sbuf("rowt", [1, 8], F32)
    rowb = P.sbuf("rowb", [1, 8], BF16)
    sb1 = P.sbuf("sb1", [128, NFC], F32)
    sb2 = P.sbuf("sb2", [128, KC], F32)
    red = P.sbuf("red", [128, 1], F32)

    pk0 = P.psum("pk0", [128, 512], F32)
    pk1 = P.psum("pk1", [128, 512], F32)
    pk2 = P.psum("pk2", [128, 512], F32)
    pk3 = P.psum("pk3", [128, 512], F32)
    pk4 = P.psum("pk4", [128, 512], F32)
    pk7 = P.psum("pk7", [128, 512], F32)
    pk5 = P.psum("pk5", [128, 1024], BF16)
    pk6 = P.psum("pk6", [128, 1024], BF16)

    P.dma("sp", sb1[:], b1T, sem="initB")
    P.dma("sp", sb2[:], b2T, sem="initB")
    w_init = P.cnt["initB"]
    P.op("dve", lambda e: e.memset(acc1[:], 0.0), [("initB", w_init)])
    d_init = P.op("dve", lambda e: e.memset(red[:], 0.0), inc="dve")
    fence_w = [("initB", w_init), ("dve", d_init)]
    P.op("pe", lambda e: e.nop(), fence_w)
    P.op("act", lambda e: e.nop(), fence_w)
    P.op("pool", lambda e: e.engine_nop(), fence_w)

    def w1_slab_ap(fc):
        return W1[fc * 128:(fc + 1) * 128, :]

    def w2_slab_ap(fc):
        r = W2.rearrange("(k p) f -> p k f", p=128)
        return r[:, :, fc * 128:(fc + 1) * 128]

    rb_ws = RB(2)
    d_red = 0
    for wi in (0, 1):
        for c in range(NFC if wi == 0 else NFC):
            kw, ww = rb_ws.acquire()
            ap = w1_slab_ap(c) if wi == 0 else w2_slab_ap(c)
            w_sl = P.dma("sp", ws[kw][:], ap, ww, sem="wd%d" % kw)
            P.op("dve", lambda e, kw=kw: e.reduce_sum(
                red[:], ws[kw][:], axis=AX.X, apply_absolute_value=True),
                [("wd%d" % kw, w_sl), ("dve", d_init)], "dve")
            d_red = P.op("dve", lambda e, wi=wi: e.tensor_tensor(
                acc1[:, wi:wi + 1], acc1[:, wi:wi + 1], red[:], ALU.add), inc="dve")
            rb_ws.release(kw, "dve", d_red)
    a_h = P.op("act", lambda e: e.activation(accb[:, 0:2], acc1[:, 0:2], AF.Copy),
               [("dve", d_red)], "act")
    d_l = P.op("dve", lambda e: e.tensor_tensor(thr[:, 0:2], acc1[:, 0:2],
                                                accb[:, 0:2], ALU.subtract),
               [("act", a_h)], "dve")
    a_l = P.op("act", lambda e: e.activation(accb[:, 2:4], thr[:, 0:2], AF.Copy),
               [("dve", d_l)], "act")
    m1 = P.op("pe", lambda e: e.matmul(pk0[0:1, 0:2], onesb[:], accb[:, 0:2],
                                       start=True, stop=False), [("act", a_h)], None)
    m2 = P.op("pe", lambda e: e.matmul(pk0[0:1, 0:2], onesb[:], accb[:, 2:4],
                                       start=False, stop=True), [("act", a_l)], "pe")
    d_t = P.op("dve", lambda e: e.tensor_scalar(rowt[0:1, 0:2], pk0[0:1, 0:2],
                                                0.5 / (FF * D), None, ALU.mult,
                                                ALU.bypass), [("pe", m2)], "dve")
    d_t2 = P.op("dve", lambda e: e.tensor_scalar(rowt[0:1, 2:4], rowt[0:1, 0:2], -1.0,
                                                 None, ALU.mult, ALU.bypass), inc="dve")
    a_th = P.op("act", lambda e: e.activation(rowb[0:1, 0:4], rowt[0:1, 0:4],
                                              AF.Copy), [("dve", d_t2)], "act")
    d_tl = P.op("dve", lambda e: e.tensor_tensor(rowt[0:1, 4:8], rowt[0:1, 0:4],
                                                 rowb[0:1, 0:4], ALU.subtract),
                [("act", a_th)], "dve")
    a_tl = P.op("act", lambda e: e.activation(rowb[0:1, 4:8], rowt[0:1, 4:8],
                                              AF.Copy), [("dve", d_tl)], "act")
    m3 = P.op("pe", lambda e: e.matmul(pk1[:, 0:4], ones1b[:], rowb[0:1, 0:4],
                                       start=True, stop=False), [("act", a_th)], None)
    m4 = P.op("pe", lambda e: e.matmul(pk1[:, 0:4], ones1b[:], rowb[0:1, 4:8],
                                       start=False, stop=True), [("act", a_tl)], "pe")
    a_thr = P.op("act", lambda e: e.activation(thr[:, 0:2], pk1[:, 0:2], AF.Copy),
                 [("pe", m4)], "act")
    a_nthr = P.op("act", lambda e: e.activation(nthr[:, 0:2], pk1[:, 2:4], AF.Copy),
                  [("pe", m4)], "act")

    rb_wt = RB(2)
    bf16rel = [("act", 0)] * 2
    bfturn = [0]
    w2st_rel = [("act", 0), ("act", 0)]
    for wi in (0, 1):
        for c in range(NFC if wi == 0 else NFC):
            kw, ww = rb_ws.acquire()
            ap = w1_slab_ap(c) if wi == 0 else w2_slab_ap(c)
            w_sl = P.dma("sp", ws[kw][:], ap, ww, sem="wd%d" % kw)
            kt, wt_ = rb_wt.acquire()
            P.op("dve", lambda e, kw=kw, wi=wi: e.tensor_scalar(
                tmpb[:], ws[kw][:], nthr[:, wi:wi + 1], None, ALU.is_lt, ALU.bypass),
                [("wd%d" % kw, w_sl), ("act", a_nthr), ("act", a_thr)] + wt_, "dve")
            d_tn = P.op("dve", lambda e, kw=kw, kt=kt, wi=wi: e.scalar_tensor_tensor(
                wtt[kt][:], ws[kw][:], thr[:, wi:wi + 1], tmpb[:],
                ALU.is_gt, ALU.subtract), inc="dve")
            rb_ws.release(kw, "dve", d_tn)
            lastpe = None
            for k in range(8):
                bt = bfturn[0] % 2
                bfturn[0] += 1
                pkb = pk5 if bt == 0 else pk6
                pe_t = P.op("pe", lambda e, kt=kt, k=k, pkb=pkb: e.transpose(
                    pkb[:, 0:128], wtt[kt][:, k * 128:(k + 1) * 128], sidb[:]),
                    [("dve", d_tn), bf16rel[bt]], "pe")
                lastpe = pe_t
                if wi == 0:
                    fn = lambda e, c=c, k=k, pkb=pkb: e.activation(
                        w1t[:, k * FF + c * 128: k * FF + (c + 1) * 128],
                        pkb[:, 0:128], AF.Copy)
                else:
                    fn = lambda e, c=c, k=k, pkb=pkb: e.activation(
                        w2st[c % 2][:, k * 128:(k + 1) * 128],
                        pkb[:, 0:128], AF.Copy)
                av = P.op("act", fn, [("pe", pe_t)] +
                          ([w2st_rel[c % 2]] if (wi == 1 and k == 0) else []),
                          "act")
                bf16rel[bt] = ("act", av)
            if wi == 1:
                sm = "d_w2s%d" % (c % 2)
                w2st_rel[c % 2] = (sm, P.dma(
                    "sp", w2ts[c], w2st[c % 2][:],
                    [("act", av)], sem=sm))
            rb_wt.release(kt, "pe", lastpe)
    wt_done = P.cnt["act"]

    rb_xq = RB(2)
    rb_w2 = RB(2)
    bank_rel = {"sum": ("dve", 0), "sq": ("dve", 0), "m": ("dve", 0),
                "r": ("dve", 0)}
    orel = [("dve", 0)] * 2
    hrel = [("act", 0)] * 2
    h_mm_prev = 0

    for i in range(ntiles):
        t0 = i * tile
        repv = emit_ln_stats_x(P, xr, tile, xqb, rb_xq, statr, statb, onesb,
                               ones1b, pk0, pk1, pk2[:, 0:tile], pk3[:, 0:tile],
                               bank_rel, extra_waits=[orel[0], orel[1]],
                               xoff=t0, xstride=T)
        d_xn_last = 0
        for c in range(KC):
            xc = xr[:, c * T + t0: c * T + t0 + tile]
            P.op("dve", lambda e, x=xc, c=c: e.tensor_tensor(
                xn2[:, c * tile:(c + 1) * tile], x, pk2[:, 0:tile], ALU.subtract),
                [("pe", repv["m"]), ("pe", h_mm_prev), repv["dtail"]], "dve")
            d_xn_last = P.op("dve", lambda e, c=c: e.tensor_tensor(
                xn2[:, c * tile:(c + 1) * tile], xn2[:, c * tile:(c + 1) * tile],
                pk3[:, 0:tile], ALU.mult),
                [("pe", repv["r"])], "dve")
        bank_rel["m"] = ("dve", d_xn_last)
        bank_rel["r"] = ("dve", d_xn_last)

        a_h_last = 0
        for fc in range(NFC):
            par = fc % 2
            pkh = pk4 if par == 0 else pk7
            mmh = None
            for k in range(KC):
                mmh = P.op("pe", lambda e, k=k, fc=fc, pkh=pkh: e.matmul(
                    pkh[:, 0:tile],
                    w1t[:, k * FF + fc * 128:k * FF + (fc + 1) * 128],
                    xn2[:, k * tile:(k + 1) * tile],
                    start=(k == 0), stop=(k == KC - 1)),
                    [("act", wt_done), ("dve", d_xn_last)] +
                    ([hrel[par]] if k == 0 else []),
                    "pe" if k == KC - 1 else None)
            a_h2 = P.op("act", lambda e, fc=fc, pkh=pkh: e.activation(
                hball[:, fc * tile:(fc + 1) * tile], pkh[:, 0:tile], AF.Gelu,
                bias=sb1[:, fc:fc + 1]),
                [("pe", mmh)] + ([orel[0], orel[1]] if fc < 2 else []), "act")
            hrel[par] = ("act", a_h2)
            a_h_last = a_h2
        for fcg in range(4):
            kw2, ww2 = rb_w2.acquire()
            w_sl2 = P.dma("sp", w2sl[kw2][:].rearrange("p (g f) -> p g f", g=8),
                          w2ts[fcg * 8:(fcg + 1) * 8].rearrange("g p f -> p g f"),
                          list(ww2) + [("d_w2s0", P.cnt.get("d_w2s0", 0)),
                                       ("d_w2s1", P.cnt.get("d_w2s1", 0))],
                          sem="w2d%d" % kw2)
            for dm in range(KC):
                pko = pk0 if dm % 2 == 0 else pk1
                mmo = None
                for fci in range(8):
                    fc = fcg * 8 + fci
                    mmo = P.op("pe", lambda e, fci=fci, fc=fc, dm=dm, pko=pko,
                               kw2=kw2: e.matmul(
                        pko[:, 0:tile],
                        w2sl[kw2][:, fci * 1024 + dm * 128: fci * 1024 + (dm + 1) * 128],
                        hball[:, fc * tile:(fc + 1) * tile],
                        start=(fci == 0), stop=(fci == 7)),
                        [("act", a_h_last), ("w2d%d" % kw2, w_sl2)] +
                        ([orel[dm % 2]] if fci == 0 else []),
                        "pe" if fci == 7 else None)
                if fcg == 0:
                    dv = P.op("dve", lambda e, dm=dm, pko=pko, t0=t0: e.scalar_tensor_tensor(
                        xr[:, dm * T + t0: dm * T + t0 + tile],
                        pko[:, 0:tile], sb2[:, dm:dm + 1],
                        xr[:, dm * T + t0: dm * T + t0 + tile], ALU.add, ALU.add),
                        [("pe", mmo)], "dve")
                else:
                    dv = P.op("dve", lambda e, dm=dm, pko=pko, t0=t0: e.tensor_tensor(
                        xr[:, dm * T + t0: dm * T + t0 + tile],
                        xr[:, dm * T + t0: dm * T + t0 + tile],
                        pko[:, 0:tile], ALU.add),
                        [("pe", mmo)], "dve")
                orel[dm % 2] = ("dve", dv)
            rb_w2.release(kw2, "pe", mmo)
        h_mm_prev = P.cnt["pe"]

    return


def emit_ln_stats_x(P, xbuf, tile, xqb, rb_xq, statr, statb, onesb, ones1b,
                    pk_sum, pk_sq, pk_m, pk_r, bank_rel, extra_waits=(),
                    rep_extra=(), xoff=0, xstride=None):
    if xstride is None:
        xstride = tile
    mmS = mmQ = None
    for c in range(KC):
        src = xbuf[:, c * xstride + xoff: c * xstride + xoff + tile]
        kb, w = rb_xq.acquire()
        a1 = P.op("act", lambda e, s=src, d=xqb[kb]: e.activation(d[:, 0:tile], s, AF.Copy),
                  list(w) + list(extra_waits), "act")
        mmS = P.op("pe", lambda e, d=xqb[kb], c=c: e.matmul(
            pk_sum[0:1, 0:tile], onesb[:], d[:, 0:tile],
            start=(c == 0), stop=(c == KC - 1)),
            [("act", a1)] + ([bank_rel["sum"]] if c == 0 else []), "pe")
        kb2, w2 = rb_xq.acquire()
        a2 = P.op("act", lambda e, s=src, d=xqb[kb2]: e.activation(d[:, 0:tile], s, AF.Square),
                  list(w2) + list(extra_waits), "act")
        mmQ = P.op("pe", lambda e, d=xqb[kb2], c=c: e.matmul(
            pk_sq[0:1, 0:tile], onesb[:], d[:, 0:tile],
            start=(c == 0), stop=(c == KC - 1)),
            [("act", a2)] + ([bank_rel["sq"]] if c == 0 else []), "pe")
        rb_xq.release(kb, "pe", mmS)
        rb_xq.release(kb2, "pe", mmQ)

    def sl(i):
        return statr[0:1, i * tile:(i + 1) * tile]

    inv = 1.0 / D
    d1 = P.op("dve", lambda e: e.tensor_scalar(sl(0), pk_sum[0:1, 0:tile], inv, None,
                                               ALU.mult, ALU.bypass), [("pe", mmS)], "dve")
    d2 = P.op("dve", lambda e: e.tensor_scalar(sl(1), pk_sq[0:1, 0:tile], inv, None,
                                               ALU.mult, ALU.bypass), [("pe", mmQ)], "dve")
    bank_rel["sum"] = ("dve", d1)
    bank_rel["sq"] = ("dve", d2)
    P.op("dve", lambda e: e.tensor_tensor(sl(2), sl(0), sl(0), ALU.mult))
    d4 = P.op("dve", lambda e: e.tensor_tensor(sl(1), sl(1), sl(2), ALU.subtract),
              inc="dve")
    d4b = P.op("dve", lambda e: e.tensor_scalar(sl(2), sl(1), 1e-5, None,
                                                ALU.add, ALU.bypass), [("dve", d4)], "dve")
    a3 = P.op("act", lambda e: e.activation(sl(2), sl(2), AF.Ln), [("dve", d4b)], "act")
    a4 = P.op("act", lambda e: e.activation(sl(3), sl(2), AF.Exp, scale=-0.5),
              [("act", a3)], "act")
    reps = [("m", sl(0), 0, pk_m, ("dve", d1), 4), ("r", sl(3), 2, pk_r, ("act", a4), 4)]
    out = {}
    prev_al = None
    for name, row, bi, pk, wv, tmpi in reps:
        def sb(i):
            return statb[0:1, i * tile:(i + 1) * tile]
        ah = P.op("act", lambda e, r=row, o=sb(bi): e.activation(o, r, AF.Copy),
                  [wv], "act")
        dl = P.op("dve", lambda e, r=row, h=sb(bi), o=sl(tmpi): e.tensor_tensor(
            o, r, h, ALU.subtract),
            [("act", ah)] + ([("act", prev_al)] if prev_al else []), "dve")
        al = P.op("act", lambda e, o=sb(bi + 1), r=sl(tmpi): e.activation(o, r, AF.Copy),
                  [("dve", dl)], "act")
        prev_al = al
        m1 = P.op("pe", lambda e, pk=pk, h=sb(bi): e.matmul(
            pk, ones1b[:], h, start=True, stop=False),
            [("act", ah), bank_rel[name]] + list(rep_extra), None)
        m2 = P.op("pe", lambda e, pk=pk, l=sb(bi + 1): e.matmul(
            pk, ones1b[:], l, start=False, stop=True),
            [("act", al)], "pe")
        out[name] = m2
    out["dtail"] = ("dve", P.cnt["dve"])
    return out


def wA_slab_ap(wab, j):
    r = wab.rearrange("(k p) f -> p k f", p=128)
    return r[:, :, j * 256:(j + 1) * 256]


def wo_slab_ap(wob, dm):
    r = wob.rearrange("(k p) d -> p k d", p=128)
    return r[:, :, dm * 128:(dm + 1) * 128]



def emit_phase_C(P, sh, tile=256):
    ntiles = T // tile
    NHD = 16
    TC = tile // 128
    xr = sh["xr"]; sid = sh["sid"]; sidb = sh["sidb"]
    onesb = sh["onesb"]; ones1b = sh["ones1b"]

    Wq = P.dram("C_Wq", [D, D], F32, kind="ExternalInput")
    Wk = P.dram("C_Wk", [D, D], F32, kind="ExternalInput")
    Wv = P.dram("C_Wv", [D, D], F32, kind="ExternalInput")
    Wg = P.dram("C_Wg", [D, D], F32, kind="ExternalInput")
    Wo = P.dram("C_Wo", [D, D], F32, kind="ExternalInput")
    bg = P.dram("C_bg", [1, D], F32, kind="ExternalInput")

    xn3 = P.sbuf("xn3", [128, KC * T], BF16)
    WS = [P.sbuf(f"WS{i}", [128, KC * D], BF16) for i in range(3)]
    ktm = P.sbuf("ktm", [128, TC * D], BF16)
    vA = P.sbuf("vA", [128, TC * NHD * 65], BF16)
    qb = P.sbuf("qb", [128, KC * tile], BF16)
    gb = P.sbuf("gb", [128, TC * D], BF16)
    o2 = P.sbuf("o2", [128, TC * D], BF16)
    o2T = P.sbuf("o2T", [128, KC * tile], BF16)
    KVA = P.sbuf("KVA", [128, NHD * 65], BF16)
    bgrep = P.sbuf("bgrep", [128, D], F32)
    bgror = P.sbuf("bgror", [1, D], F32)
    bgrow = P.sbuf("bgrow", [1, 2 * D], BF16)
    et1 = P.sbuf("et1", [128, 512], F32)
    et2 = P.sbuf("et2", [128, 512], F32)
    zrec = P.sbuf("zrec", [128, 2], F32)
    xqb = [P.sbuf(f"xqb{i}", [128, tile], BF16) for i in range(4)]
    statr = P.sbuf("statr", [1, 5 * tile], F32)
    statb = P.sbuf("statb", [1, 4 * tile], BF16)

    pk0 = P.psum("pk0", [128, 512], F32)
    pk1 = P.psum("pk1", [128, 512], F32)
    pk2 = P.psum("pk2", [128, 512], F32)
    pk3 = P.psum("pk3", [128, 512], F32)
    pk4 = P.psum("pk4", [128, 512], F32)
    pk7 = P.psum("pk7", [128, 512], F32)
    pk5 = P.psum("pk5", [128, 1024], BF16)
    pk6 = P.psum("pk6", [128, 1024], BF16)

    P.dma("sp", bgror[:], bg, sem="initC")
    w_init = P.cnt["initC"]

    def wrearr(w):
        return w.rearrange("(k p) f -> p k f", p=128)

    w_k = P.dma("pool", WS[0][:].rearrange("p (k f) -> p k f", k=KC),
                wrearr(Wk), sem="cast_k")
    w_v = P.dma("pool", WS[1][:].rearrange("p (k f) -> p k f", k=KC),
                wrearr(Wv), sem="cast_v")
    w_o = P.dma("pool", WS[2][:].rearrange("p (k f) -> p k f", k=KC),
                wrearr(Wo), sem="cast_o")
    d_init = P.op("dve", lambda e: e.memset(vA[:], 1.0), inc="dve")
    fence_w = [("initC", w_init), ("dve", d_init)]
    P.op("pe", lambda e: e.nop(), fence_w)
    P.op("act", lambda e: e.nop(), fence_w)
    P.op("pool", lambda e: e.engine_nop(), fence_w)

    a_bh = P.op("act", lambda e: e.activation(bgrow[0:1, 0:D], bgror[:], AF.Copy),
                [("initC", w_init)], "act")
    d_bl = P.op("dve", lambda e: e.tensor_tensor(bgrep[0:1, 0:D], bgror[:],
                                                 bgrow[0:1, 0:D], ALU.subtract),
                [("act", a_bh)], "dve")
    a_bl = P.op("act", lambda e: e.activation(bgrow[0:1, D:2 * D], bgrep[0:1, 0:D],
                                              AF.Copy), [("dve", d_bl)], "act")
    a_bgrep = 0
    for half in range(2):
        pkh = pk0 if half == 0 else pk1
        P.op("pe", lambda e, pkh=pkh, half=half: e.matmul(
            pkh[:, 0:512], ones1b[:], bgrow[0:1, half * 512:half * 512 + 512],
            start=True, stop=False), [("act", a_bh), ("dve", d_bl)], None)
        m_bg = P.op("pe", lambda e, pkh=pkh, half=half: e.matmul(
            pkh[:, 0:512], ones1b[:], bgrow[0:1, D + half * 512:D + half * 512 + 512],
            start=False, stop=True), [("act", a_bl)], "pe")
        a_bgrep = P.op("act", lambda e, pkh=pkh, half=half: e.activation(
            bgrep[:, half * 512:(half + 1) * 512], pkh[:, 0:512], AF.Copy),
            [("pe", m_bg)], "act")

    rb_xq = RB(4)
    bank_rel = {"sum": ("act", a_bgrep), "sq": ("act", a_bgrep),
                "m": ("dve", 0), "r": ("dve", 0)}
    d_xn_last = 0
    for i in range(ntiles):
        t0 = i * tile
        repv = emit_ln_stats_x(P, xr, tile, xqb, rb_xq, statr, statb, onesb,
                               ones1b, pk0, pk1, pk2[:, 0:tile], pk3[:, 0:tile],
                               bank_rel, xoff=t0, xstride=T)
        for c in range(KC):
            xc = xr[:, c * T + t0: c * T + t0 + tile]
            P.op("dve", lambda e, x=xc, c=c, t0=t0: e.tensor_tensor(
                xn3[:, c * T + t0: c * T + t0 + tile], x, pk2[:, 0:tile],
                ALU.subtract), [("pe", repv["m"]), repv["dtail"]], "dve")
            d_xn_last = P.op("dve", lambda e, c=c, t0=t0: e.tensor_tensor(
                xn3[:, c * T + t0: c * T + t0 + tile],
                xn3[:, c * T + t0: c * T + t0 + tile], pk3[:, 0:tile], ALU.mult),
                [("pe", repv["r"])], "dve")
        bank_rel["m"] = ("dve", d_xn_last)
        bank_rel["r"] = ("dve", d_xn_last)

    prel = [("act", 0), ("act", 0)]
    kv_first = [True] * 4
    pproj = [0]
    kv_mm_last = None
    ktm_rel = ("pe", 0)
    vA_rel = ("pe", 0)
    for i in range(ntiles):
        t0 = i * tile
        k_ev = v_ev = None
        for wi, wbuf, wdone in ((0, WS[0], w_k), (1, WS[1], w_v)):
            for half in range(2):
                for tc in range(TC):
                    par = pproj[0] % 2
                    pproj[0] += 1
                    pkh = pk4 if par == 0 else pk7
                    mmv = None
                    for k in range(KC):
                        mmv = P.op("pe", lambda e, k=k, half=half, tc=tc, t0=t0,
                                   pkh=pkh, wbuf=wbuf: e.matmul(
                            pkh[:, 0:512],
                            xn3[:, k * T + t0 + tc * 128: k * T + t0 + tc * 128 + 128],
                            wbuf[:, k * D + half * 512: k * D + half * 512 + 512],
                            start=(k == 0), stop=(k == KC - 1)),
                            [("cast_k" if wi == 0 else "cast_v", wdone),
                             ("dve", d_xn_last)] + ([prel[par]] if k == 0 else []) +
                            ([ktm_rel] if (wi == 0 and half == 0 and tc == 0) else []) +
                            ([vA_rel] if (wi == 1 and half == 0 and tc == 0) else []),
                            "pe" if k == KC - 1 else None)
                    if wi == 0:
                        d_e1 = P.op("dve", lambda e, pkh=pkh: e.tensor_scalar(
                            et1[:], pkh[:, 0:512], 0.0, None, ALU.min, ALU.bypass),
                            [("pe", mmv)], "dve")
                        a_e = P.op("act", lambda e: e.activation(et2[:], et1[:], AF.Exp),
                                   [("dve", d_e1)], "act")
                        k_ev = P.op("dve", lambda e, pkh=pkh, half=half, tc=tc: e.scalar_tensor_tensor(
                            ktm[:, tc * D + half * 512: tc * D + (half + 1) * 512],
                            pkh[:, 0:512], 0.0, et2[:], ALU.max, ALU.add),
                            [("act", a_e)], "dve")
                        prel[par] = ("dve", k_ev)
                    else:
                        dst = vA[:, tc * NHD * 65:(tc + 1) * NHD * 65].rearrange(
                            "p (h c) -> p h c", c=65)[:, half * 8:(half + 1) * 8, 0:64]
                        v_ev = P.op("act", lambda e, pkh=pkh, dst=dst: e.activation(
                            dst, pkh[:, 0:512].rearrange("p (h c) -> p h c", c=64),
                            AF.Copy), [("pe", mmv)], "act")
                        prel[par] = ("act", v_ev)
        for h in range(NHD):
            bank = (pk0, pk1, pk2, pk3)[h // 4]
            reg = (h % 4) * 65
            for tc in range(TC):
                first = kv_first[h // 4]
                kv_first[h // 4] = False
                last = (i == ntiles - 1) and (tc == TC - 1)
                kv_mm_last = P.op("pe", lambda e, h=h, tc=tc, bank=bank, reg=reg,
                                  first=first, last=last: e.matmul(
                    bank[0:64, reg:reg + 65],
                    ktm[:, tc * D + h * 64: tc * D + h * 64 + 64],
                    vA[:, tc * NHD * 65 + h * 65: tc * NHD * 65 + (h + 1) * 65],
                    start=first, stop=last, skip_group_check=True),
                    [("dve", k_ev), ("act", v_ev)] +
                    ([bank_rel["m"], bank_rel["r"], bank_rel["sum"], bank_rel["sq"]]
                     if first else []),
                    "pe")
        ktm_rel = ("pe", kv_mm_last)
        vA_rel = ("pe", kv_mm_last)

    a_kv = 0
    for b in range(4):
        bank = (pk0, pk1, pk2, pk3)[b]
        for rw in (0, 64):
            a_kv = P.op("act", lambda e, b=b, bank=bank, rw=rw: e.activation(
                KVA[rw:rw + 64, b * 260:(b + 1) * 260], bank[0:64, 0:260], AF.Copy),
                [("pe", kv_mm_last)], "act")

    w_q = P.dma("pool", WS[0][:].rearrange("p (k f) -> p k f", k=KC),
                wrearr(Wq), [ktm_rel, ("dve", P.cnt["dve"])], sem="cast_k")
    w_g = P.dma("pool", WS[1][:].rearrange("p (k f) -> p k f", k=KC),
                wrearr(Wg), [vA_rel, ("act", P.cnt["act"])], sem="cast_v")
    orel = [("dve", 0), ("dve", 0)]
    grel = ("act", 0)
    hrel = [("dve", 0), ("dve", 0)]
    qrel = [("dve", 0), ("dve", 0)]
    bfrel = [("act", 0), ("act", 0)]
    bft = [0]
    for i in range(ntiles):
        t0 = i * tile
        q_ev = None
        for fc in range(KC):
            pkq = pk2 if fc % 2 == 0 else pk3
            mmq = None
            for k in range(KC):
                mmq = P.op("pe", lambda e, k=k, fc=fc, t0=t0, pkq=pkq: e.matmul(
                    pkq[:, 0:tile],
                    WS[0][:, k * D + fc * 128: k * D + (fc + 1) * 128],
                    xn3[:, k * T + t0: k * T + t0 + tile],
                    start=(k == 0), stop=(k == KC - 1)),
                    [("cast_k", w_q), ("dve", d_xn_last)] +
                    ([qrel[fc % 2], ("act", a_kv)] if k == 0 else []),
                    "pe" if k == KC - 1 else None)
            d_e1 = P.op("dve", lambda e, pkq=pkq: e.tensor_scalar(
                et1[:, 0:tile], pkq[:, 0:tile], 0.0, None, ALU.min, ALU.bypass),
                [("pe", mmq)], "dve")
            a_e = P.op("act", lambda e: e.activation(et2[:, 0:tile], et1[:, 0:tile],
                                                     AF.Exp), [("dve", d_e1)], "act")
            q_ev = P.op("dve", lambda e, pkq=pkq, fc=fc: e.scalar_tensor_tensor(
                qb[:, fc * tile:(fc + 1) * tile], pkq[:, 0:tile], 0.0, et2[:, 0:tile],
                ALU.max, ALU.add), [("act", a_e)], "dve")
            qrel[fc % 2] = ("dve", q_ev)
        g_ev = None
        for half in range(2):
            for tc in range(TC):
                mmg = None
                for k in range(KC):
                    mmg = P.op("pe", lambda e, k=k, half=half, tc=tc, t0=t0: e.matmul(
                        pk4[:, 0:512],
                        xn3[:, k * T + t0 + tc * 128: k * T + t0 + tc * 128 + 128],
                        WS[1][:, k * D + half * 512: k * D + half * 512 + 512],
                        start=(k == 0), stop=(k == KC - 1)),
                        [("cast_v", w_g), ("dve", d_xn_last)] +
                        ([grel, hrel[0]] if k == 0 else []),
                        "pe" if k == KC - 1 else None)
                d_g1 = P.op("dve", lambda e, half=half: e.tensor_tensor(
                    et1[:], pk4[:, 0:512], bgrep[:, half * 512:(half + 1) * 512],
                    ALU.add), [("pe", mmg), ("act", a_bgrep)], "dve")
                g_ev = P.op("act", lambda e, half=half, tc=tc: e.activation(
                    gb[:, tc * D + half * 512: tc * D + (half + 1) * 512], et1[:],
                    AF.Sigmoid), [("dve", d_g1)], "act")
                grel = ("act", g_ev)
        d_o_last = 0
        o2_pe_rel = P.cnt["pe"] if i == 0 else o2_pe_rel
        for h in range(NHD):
            par = h % 2
            pkh = pk4 if par == 0 else pk7
            fc2 = h // 2
            r0 = (h % 2) * 64
            for tc in range(TC):
                mmo = P.op("pe", lambda e, h=h, tc=tc, pkh=pkh, fc2=fc2, r0=r0,
                           t0=t0: e.matmul(
                    pkh[:, 0:65],
                    qb[r0:r0 + 64, fc2 * tile + tc * 128: fc2 * tile + tc * 128 + 128],
                    KVA[r0:r0 + 64, h * 65:(h + 1) * 65],
                    start=True, stop=True),
                    [("dve", q_ev), ("act", a_kv), hrel[par],
                     (grel if (h == 0 and tc == 0) else ("act", 0))], "pe")
                d_z = P.op("dve", lambda e, pkh=pkh: e.tensor_scalar(
                    zrec[:, 0:1], pkh[:, 64:65], 8.0, 8e-6, ALU.mult, ALU.max),
                    [("pe", mmo)], "dve")
                d_zr = P.op("dve", lambda e: e.reciprocal(zrec[:, 1:2], zrec[:, 0:1]),
                            [("dve", d_z)], "dve")
                d_o = P.op("dve", lambda e, pkh=pkh, h=h, tc=tc: e.scalar_tensor_tensor(
                    o2[:, tc * D + h * 64: tc * D + (h + 1) * 64],
                    pkh[:, 0:64], zrec[:, 1:2],
                    gb[:, tc * D + h * 64: tc * D + (h + 1) * 64],
                    ALU.mult, ALU.mult),
                    [("dve", d_zr), ("act", g_ev), ("pe", o2_pe_rel)], "dve")
                hrel[par] = ("dve", d_o)
                d_o_last = d_o
        to_ev = 0
        for tc in range(TC):
            for fc2 in range(KC):
                bt = bft[0] % 2
                bft[0] += 1
                pkb = pk5 if bt == 0 else pk6
                pe_t = P.op("pe", lambda e, tc=tc, fc2=fc2, pkb=pkb: e.transpose(
                    pkb[:, 0:128], o2[:, tc * D + fc2 * 128: tc * D + (fc2 + 1) * 128],
                    sidb[:]), [("dve", d_o_last), bfrel[bt]], "pe")
                to_ev = P.op("act", lambda e, tc=tc, fc2=fc2, pkb=pkb: e.activation(
                    o2T[:, fc2 * tile + tc * 128: fc2 * tile + tc * 128 + 128],
                    pkb[:, 0:128], AF.Copy), [("pe", pe_t)], "act")
                bfrel[bt] = ("act", to_ev)
        o2_pe_rel = 0
        for dm in range(KC):
            pko = pk0 if dm % 2 == 0 else pk1
            mmw = None
            for fc2 in range(KC):
                mmw = P.op("pe", lambda e, dm=dm, fc2=fc2, pko=pko: e.matmul(
                    pko[:, 0:tile],
                    WS[2][:, fc2 * D + dm * 128: fc2 * D + (dm + 1) * 128],
                    o2T[:, fc2 * tile:(fc2 + 1) * tile],
                    start=(fc2 == 0), stop=(fc2 == KC - 1)),
                    [("cast_o", w_o), ("act", to_ev)] +
                    ([orel[dm % 2], ("act", a_kv)] if fc2 == 0 else []),
                    "pe" if fc2 == KC - 1 else None)
            dv = P.op("dve", lambda e, dm=dm, pko=pko, t0=t0: e.tensor_tensor(
                xr[:, dm * T + t0: dm * T + t0 + tile],
                xr[:, dm * T + t0: dm * T + t0 + tile],
                pko[:, 0:tile], ALU.add), [("pe", mmw)], "dve")
            orel[dm % 2] = ("dve", dv)
            o2_pe_rel = mmw

    return



def emit_phase_D(P, sh, tile=512):
    ntiles = T // tile
    HID = 3072
    NFC = HID // 128
    xr = sh["xr"]; sid = sh["sid"]; sidb = sh["sidb"]
    onesb = sh["onesb"]; ones1b = sh["ones1b"]

    wg = P.dram("D_wg", [HID, D], I32, kind="ExternalInput")
    wu = P.dram("D_wu", [HID, D], I32, kind="ExternalInput")
    wd = P.dram("D_wd", [D, HID], I32, kind="ExternalInput")

    wgts = P.dram("D_wgt_scr", [NFC, 128, 1024], BF16)
    wuts = P.dram("D_wut_scr", [NFC, 128, 1024], BF16)
    wdt = P.sbuf("wdt", [128, NFC * D], BF16)
    wst = [P.sbuf(f"wst{i}", [128, 1024], BF16) for i in range(2)]
    wgl = [P.sbuf(f"wgl{i}", [128, 1024], BF16) for i in range(3)]
    wul = [P.sbuf(f"wul{i}", [128, 1024], BF16) for i in range(3)]
    xn4 = P.sbuf("xn4", [128, KC * tile], BF16)
    wsl = [P.sbuf(f"wsl{i}", [128, KC * 128], BF16) for i in range(2)]
    gball = P.sbuf("gball", [128, tile], BF16)
    uball = P.sbuf("uball", [128, tile], BF16)
    hball = P.sbuf("hball", [128, NFC * tile], BF16)
    xqb = [P.sbuf(f"xqb{i}", [128, tile], BF16) for i in range(4)]
    statr = P.sbuf("statr", [1, 5 * tile], F32)
    statb = P.sbuf("statb", [1, 4 * tile], BF16)

    pk0 = P.psum("pk0", [128, 512], F32)
    pk1 = P.psum("pk1", [128, 512], F32)
    pk2 = P.psum("pk2", [128, 512], F32)
    pk3 = P.psum("pk3", [128, 512], F32)
    pk4 = P.psum("pk4", [128, 512], F32)
    pk7 = P.psum("pk7", [128, 512], F32)
    pk5 = P.psum("pk5", [128, 1024], BF16)
    pk6 = P.psum("pk6", [128, 1024], BF16)

    rb_w = RB(2)
    bfrel = [("act", 0), ("act", 0)]
    bft = [0]
    wst_rel = [("act", 0), ("act", 0)]
    for wi, wsrc in ((0, wg), (1, wu), (2, wd)):
        for fc in range(NFC):
            kw, ww = rb_w.acquire()
            if wi < 2:
                ap = wsrc[fc * 128:(fc + 1) * 128, :].rearrange("p (k f) -> p k f", k=KC)
            else:
                ap = wsrc.rearrange("(k p) f -> p k f", p=128)[:, :, fc * 128:(fc + 1) * 128]
            w_sl = P.dma("pool", wsl[kw][:].rearrange("p (k f) -> p k f", k=KC), ap,
                         ww, sem="dwc%d" % kw)
            lastpe = None
            for k in range(KC):
                bt = bft[0] % 2
                bft[0] += 1
                pkb = pk5 if bt == 0 else pk6
                pe_t = P.op("pe", lambda e, kw=kw, k=k, pkb=pkb: e.transpose(
                    pkb[:, 0:128], wsl[kw][:, k * 128:(k + 1) * 128], sidb[:]),
                    [("dwc%d" % kw, w_sl), bfrel[bt]], "pe")
                lastpe = pe_t
                if wi < 2:
                    fn = lambda e, fc=fc, k=k, pkb=pkb: e.activation(
                        wst[fc % 2][:, k * 128:(k + 1) * 128], pkb[:, 0:128],
                        AF.Copy)
                else:
                    fn = lambda e, fc=fc, k=k, pkb=pkb: e.activation(
                        wdt[:, fc * D + k * 128: fc * D + (k + 1) * 128],
                        pkb[:, 0:128], AF.Copy)
                av = P.op("act", fn, [("pe", pe_t)] +
                          ([wst_rel[fc % 2]] if (wi < 2 and k == 0) else []),
                          "act")
                bfrel[bt] = ("act", av)
            if wi < 2:
                scr = wgts if wi == 0 else wuts
                sm = "d_wst%d" % (fc % 2)
                wst_rel[fc % 2] = (sm, P.dma(
                    "sp", scr[fc], wst[fc % 2][:],
                    [("act", av)], sem=sm))
            rb_w.release(kw, "pe", lastpe)
    wt_done = P.cnt["act"]

    rb_xq = RB(4)
    rb_wg = RB(3)
    rb_wu = RB(3)
    wst_done = [("d_wst0", P.cnt.get("d_wst0", 0)),
                ("d_wst1", P.cnt.get("d_wst1", 0))]
    bank_rel = {"sum": ("dve", 0), "sq": ("dve", 0), "m": ("dve", 0),
                "r": ("dve", 0)}
    orel = [("dve", 0)] * 2
    grel = [("act", 0)] * 2
    urel = [("act", 0)] * 2
    h_mm_prev = 0
    s2 = 1.0 / (127.0 * 127.0)

    for i in range(ntiles):
        t0 = i * tile
        repv = emit_ln_stats_x(P, xr, tile, xqb, rb_xq, statr, statb, onesb,
                               ones1b, pk0, pk1, pk2[:, 0:tile], pk3[:, 0:tile],
                               bank_rel,
                               extra_waits=[orel[0], orel[1], urel[0], urel[1]],
                               xoff=t0, xstride=T)
        d_xn_last = 0
        for c in range(KC):
            xc = xr[:, c * T + t0: c * T + t0 + tile]
            P.op("dve", lambda e, x=xc, c=c: e.tensor_tensor(
                xn4[:, c * tile:(c + 1) * tile], x, pk2[:, 0:tile], ALU.subtract),
                [("pe", repv["m"]), ("pe", h_mm_prev), repv["dtail"]], "dve")
            d_xn_last = P.op("dve", lambda e, c=c: e.tensor_tensor(
                xn4[:, c * tile:(c + 1) * tile], xn4[:, c * tile:(c + 1) * tile],
                pk3[:, 0:tile], ALU.mult), [("pe", repv["r"])], "dve")
        bank_rel["m"] = ("dve", d_xn_last)
        bank_rel["r"] = ("dve", d_xn_last)

        d_h_last = 0
        for fc in range(NFC):
            pkg = pk4 if fc % 2 == 0 else pk7
            pku = pk2 if fc % 2 == 0 else pk3
            kwg, wwg = rb_wg.acquire()
            w_g = P.dma("sp", wgl[kwg][:], wgts[fc],
                        wwg + wst_done, sem="wgd%d" % kwg)
            kwu, wwu = rb_wu.acquire()
            w_u = P.dma("sp", wul[kwu][:], wuts[fc],
                        wwu + wst_done, sem="wud%d" % kwu)
            mmg = mmu = None
            for k in range(KC):
                mmg = P.op("pe", lambda e, k=k, fc=fc, pkg=pkg, kwg=kwg: e.matmul(
                    pkg[:, 0:tile], wgl[kwg][:, k * 128:(k + 1) * 128],
                    xn4[:, k * tile:(k + 1) * tile],
                    start=(k == 0), stop=(k == KC - 1)),
                    [("act", wt_done), ("dve", d_xn_last), ("wgd%d" % kwg, w_g)] +
                    ([grel[fc % 2]] if k == 0 else []),
                    "pe" if k == KC - 1 else None)
            rb_wg.release(kwg, "pe", mmg)
            for k in range(KC):
                mmu = P.op("pe", lambda e, k=k, fc=fc, pku=pku, kwu=kwu: e.matmul(
                    pku[:, 0:tile], wul[kwu][:, k * 128:(k + 1) * 128],
                    xn4[:, k * tile:(k + 1) * tile],
                    start=(k == 0), stop=(k == KC - 1)),
                    [("act", wt_done), ("dve", d_xn_last), ("wud%d" % kwu, w_u)] +
                    ([urel[fc % 2], bank_rel["m"], bank_rel["r"]] if k == 0 else []),
                    "pe" if k == KC - 1 else None)
            rb_wu.release(kwu, "pe", mmu)
            a_g = P.op("act", lambda e, pkg=pkg: e.activation(
                gball[:], pkg[:, 0:tile], AF.Silu, scale=1.0 / 127.0),
                [("pe", mmg), ("dve", d_h_last)], "act")
            grel[fc % 2] = ("act", a_g)
            a_u = P.op("act", lambda e, pku=pku: e.activation(
                uball[:], pku[:, 0:tile], AF.Copy),
                [("pe", mmu), ("dve", d_h_last)], "act")
            urel[fc % 2] = ("act", a_u)
            d_h_last = P.op("dve", lambda e, fc=fc: e.tensor_tensor(
                hball[:, fc * tile:(fc + 1) * tile], gball[:], uball[:], ALU.mult),
                [("act", a_g), ("act", a_u)], "dve")
        h_mm_prev = P.cnt["pe"]

        dv = None
        for dm in range(KC):
            pko = pk0 if dm % 2 == 0 else pk1
            mmo = None
            for fc in range(NFC):
                mmo = P.op("pe", lambda e, fc=fc, dm=dm, pko=pko: e.matmul(
                    pko[:, 0:tile], wdt[:, fc * D + dm * 128: fc * D + (dm + 1) * 128],
                    hball[:, fc * tile:(fc + 1) * tile],
                    start=(fc == 0), stop=(fc == NFC - 1)),
                    [("dve", d_h_last)] + ([orel[dm % 2]] if fc == 0 else []),
                    "pe" if fc == NFC - 1 else None)
            dv = P.op("dve", lambda e, dm=dm, pko=pko, t0=t0: e.scalar_tensor_tensor(
                xr[:, dm * T + t0: dm * T + t0 + tile], pko[:, 0:tile], s2,
                xr[:, dm * T + t0: dm * T + t0 + tile], ALU.mult, ALU.add),
                [("pe", mmo)], "dve")
            orel[dm % 2] = ("dve", dv)

    return



def build_fused(sim_safe=True, relax=("pe", "act", "dve", "pool"), phases="ABCD"):
    P = PB(sim_safe=sim_safe, relax=relax)
    sr = globals().get("_STRICT_RNG")
    if sr:
        P.strict_rng[sr[0]] = (sr[1], sr[2])
    x_in = P.dram("x_in", [D, T], F32, kind="ExternalInput")
    identf = P.dram("identf", [128, 128], F32, kind="ExternalInput")
    x_out = P.dram("x_out", [D, T], F32, kind="ExternalOutput")

    xr = P.sbuf("xr", [128, KC * T], F32)
    sid = P.sbuf("sid", [128, 128], F32)
    sidb = P.sbuf("sidb", [128, 128], BF16)
    onesb = P.sbuf("onesb", [128, 1], BF16)
    ones1b = P.sbuf("ones1b", [1, 128], BF16)

    P.dma("sp", xr[:].rearrange("p (c t) -> p c t", c=KC),
          x_in.rearrange("(c p) t -> p c t", p=128), sem="init0")
    P.dma("sp", sid[:], identf, sem="init0")
    w0 = P.cnt["init0"]
    P.op("dve", lambda e: e.tensor_copy(sidb[:], sid[:]), [("init0", w0)])
    P.op("dve", lambda e: e.memset(onesb[:], 1.0))
    d0 = P.op("dve", lambda e: e.memset(ones1b[:], 1.0), inc="dve")
    fw = [("init0", w0), ("dve", d0)]
    P.op("pe", lambda e: e.nop(), fw)
    P.op("act", lambda e: e.nop(), fw)
    P.op("dve", lambda e: e.engine_nop(), fw)
    P.op("pool", lambda e: e.engine_nop(), fw)
    P.op("sp", lambda e: e.nop(), fw)

    sh = dict(xr=xr, sid=sid, sidb=sidb, onesb=onesb, ones1b=ones1b)
    all_phases = (("pA_", emit_phase_A), ("pB_", emit_phase_B),
                  ("pC_", emit_phase_C), ("pD_", emit_phase_D))
    sel = phases
    for pfx, emit in all_phases:
        if pfx[1] not in sel:
            continue
        P.push_scope()
        P.name_prefix = pfx
        emit(P, sh)
        P.name_prefix = ""
        P.fence()
        P.pop_scope()

    fin = P.dma("sp", x_out.rearrange("(c p) t -> p c t", p=128),
                xr[:].rearrange("p (c t) -> p c t", c=KC), sem="fin")
    P.op("sp", lambda e: e.nop(), [("fin", fin)])
    return P


def _prep_inputs(inputs):
    f32 = np.float32
    w_in = np.asarray(inputs["L0_attn_w_in"], f32)
    pad = np.zeros((D, 96), f32)
    pad64 = np.zeros((D, 64), f32)
    pad128 = np.zeros((D, 128), f32)
    wA = np.ascontiguousarray(np.concatenate(
        [w_in[:, 0:4096], w_in[:, 4096:4128], pad,
         w_in[:, 4128:4192], pad64, w_in[:, 4192:4256], pad64, pad128], axis=1))
    assert wA.shape == (D, PROJ_PAD)
    s = np.arange(128)[:, None]
    t = np.arange(128)[None, :]
    seli = np.zeros((128, NH * 128), f32)
    for h in range(NH):
        seli[h, h * 128:(h + 1) * 128] = 1.0
    com = dict(
        A_wA=wA,
        A_convT=np.ascontiguousarray(np.asarray(inputs["L0_attn_conv_weight"], f32).T),
        A_cbT=np.ascontiguousarray(np.asarray(inputs["L0_attn_conv_bias"], f32).reshape(16, 128).T),
        A_nscT=np.ascontiguousarray(np.asarray(inputs["L0_attn_norm_scale"], f32).reshape(8, 128).T),
        A_alog=np.ascontiguousarray(np.asarray(inputs["L0_attn_A_log"], f32).reshape(NH, 1)),
        A_dtbias=np.ascontiguousarray(np.asarray(inputs["L0_attn_dt_bias"], f32).reshape(NH, 1)),
        A_wo_in=np.ascontiguousarray(np.asarray(inputs["L0_attn_w_out"], f32)),
        A_caus=(s <= t).astype(f32),
        A_seli=seli,
        B_W1=np.asarray(inputs["L0_ffn_W1"], f32),
        B_W2=np.asarray(inputs["L0_ffn_W2"], f32),
        B_b1T=np.ascontiguousarray(np.asarray(inputs["L0_ffn_b1"], f32).reshape(32, 128).T),
        B_b2T=np.ascontiguousarray(np.asarray(inputs["L0_ffn_b2"], f32).reshape(8, 128).T),
        C_Wq=np.asarray(inputs["L1_attn_Wq"], f32),
        C_Wk=np.asarray(inputs["L1_attn_Wk"], f32),
        C_Wv=np.asarray(inputs["L1_attn_Wv"], f32),
        C_Wg=np.asarray(inputs["L1_attn_Wg"], f32),
        C_Wo=np.asarray(inputs["L1_attn_Wo"], f32),
        C_bg=np.ascontiguousarray(np.asarray(inputs["L1_attn_bg"], f32).reshape(1, D)),
        D_wg=np.asarray(inputs["L1_ffn_w_gate"], np.int32),
        D_wu=np.asarray(inputs["L1_ffn_w_up"], np.int32),
        D_wd=np.asarray(inputs["L1_ffn_w_down"], np.int32),
        identf=np.eye(128, dtype=f32),
    )
    xs = np.asarray(inputs["L0_attn_x"], np.float32)
    maps = []
    for b in range(xs.shape[0]):
        m = dict(com)
        m["x_in"] = np.ascontiguousarray(xs[b].T)
        maps.append(m)
    return maps


def kernel(**inputs):
    maps = _prep_inputs(inputs)
    nc = build_fused().build()
    res = run_bass_kernel_spmd(nc, maps, core_ids=list(range(len(maps))))
    out = np.stack([np.ascontiguousarray(r["x_out"].T) for r in res.results])
    return out.astype(np.float32)


# version 11
# speedup vs baseline: 2.3189x; 1.0139x over previous
import numpy as np
import ml_dtypes
from contextlib import ExitStack

import concourse.bass as bass
import concourse.mybir as mybir
from concourse.bass_utils import run_bass_kernel_spmd
from concourse.alu_op_type import AluOpType as ALU

F32 = mybir.dt.float32
BF16 = mybir.dt.bfloat16
I32 = mybir.dt.int32
AF = mybir.ActivationFunctionType
AX = mybir.AxisListType
bfdt = ml_dtypes.bfloat16

D = 1024
T = 2048
DI = 2048
NH = 32
DS = 64
HD = 64
KC = 8
Q = 128
PROJ_PAD = 4608
N_CORES = 8

ENGS = ("sp", "act", "dve", "pool", "pe")
_ENG_ATTR = {"sp": "sync", "act": "scalar", "dve": "vector", "pool": "gpsimd", "pe": "tensor"}


class _RecInstr:
    def then_inc(self, sem, n):
        return None


class _Rec:

    def __init__(self):
        self.w = []
        self.r = []
        self.calls = []

    def __getattr__(self, nm):
        rec = self

        def call(*a, **kw):
            rec.calls.append((nm, a))
            aps = [v for v in list(a) + list(kw.values()) if _apish(v)]
            if aps:
                rec.w.append(aps[0])
                rec.r.extend(aps[1:])
            return _RecInstr()

        return call


def _apish(v):
    cn = type(v).__name__
    return cn == "AP" or cn.endswith("TensorHandle")


_TSET = {"Exp": "nle", "Ln": "nle", "Softplus": "nle",
         "Silu": "silu", "Gelu": "gelu", "Sigmoid": "sigmoid"}


def _ap_rect(x):
    cn = type(x).__name__
    if cn.endswith("TensorHandle"):
        return (x.name, 0, 1 << 30, 0, 1 << 30)
    t = x.tensor
    esz = mybir.dt.size(x.dtype)
    tsz = mybir.dt.size(t.dtype)
    wb = tsz
    for s in list(t.shape)[1:]:
        wb *= int(s)
    off_b = x.offset * esz
    plo, flo = off_b // wb, off_b % wb
    pspan = fspan = 0
    for stride, n in x.ap:
        ext = (int(n) - 1) * int(stride) * esz
        if ext == 0:
            continue
        st_b = int(stride) * esz
        if ext < 0:
            return (x.name, 0, 1 << 30, 0, 1 << 30)
        if st_b % wb == 0:
            pspan += ext // wb
        elif st_b < wb:
            fspan += ext
        else:
            return (x.name, 0, 1 << 30, 0, 1 << 30)
    if flo + fspan + esz > wb:
        return (x.name, plo, plo + pspan + (flo + fspan) // wb + 1, 0, wb)
    return (x.name, plo, plo + pspan + 1, flo, flo + fspan + esz)


class PB:

    def __init__(self, sim_safe=False, serial=False, relax=()):
        self.nc = bass.Bass("TRN2", target_bir_lowering=False)
        self.sim_safe = sim_safe
        self.serial = serial
        self.relax = frozenset(relax)
        self.st = ExitStack()
        self.scopes = []
        self.ops = {e: [] for e in ENGS}
        self.sems = {}
        self.cnt = {}
        self._acc = {e: {} for e in ("act", "dve", "pool")}
        self._bar = {}
        self._tbl = None
        self.strict_rng = {}
        for s in ("pe", "act", "dve", "pool"):
            self.mksem(s)

    def mksem(self, s):
        if s not in self.sems:
            self.sems[s] = self.st.enter_context(self.nc.semaphore(s))
            self.cnt[s] = 0

    def push_scope(self):
        self.scopes.append(ExitStack())

    def pop_scope(self):
        self.scopes.pop().close()

    def _stack(self):
        return self.scopes[-1] if getattr(self, "scopes", None) else self.st

    def sbuf(self, name, shape, dtype):
        name = getattr(self, "name_prefix", "") + name
        return self._stack().enter_context(self.nc.sbuf_tensor(name, shape, dtype))

    def psum(self, name, shape, dtype):
        name = getattr(self, "name_prefix", "") + name
        return self._stack().enter_context(self.nc.psum_tensor(name, shape, dtype))

    def fence(self, sp=True):
        w = [(s, self.cnt[s]) for s in list(self.cnt) if self.cnt[s] > 0]
        self.op("pe", lambda e: e.nop(), w)
        self.op("act", lambda e: e.nop(), w)
        self.op("dve", lambda e: e.engine_nop(), w)
        self.op("pool", lambda e: e.engine_nop(), w)
        if sp:
            self.op("sp", lambda e: e.nop(), w)
        self._bar = dict(self.cnt)
        for acc in self._acc.values():
            for nm in list(acc):
                kept = [e for e in acc[nm] if e[1] > self._bar.get(e[0], 0)]
                if kept:
                    acc[nm] = kept
                else:
                    del acc[nm]

    def dram(self, name, shape, dtype, kind=None):
        if kind is None:
            return self.nc.dram_tensor(name, shape, dtype).ap()
        return self.nc.dram_tensor(name, shape, dtype, kind=kind).ap()

    def op(self, eng, fn, waits=(), inc=None, n=1, track=True):
        waits = tuple(waits)
        if self.serial and eng in ("pe", "act", "dve", "pool"):
            if self.serial is True:
                tgt = ("pe", "act", "dve", "pool")
            else:
                tgt = self.serial if eng == "pe" else ()
            waits = waits + tuple((s, self.cnt[s]) for s in tgt if s != eng)
            if inc is None and (self.serial is True or eng in ("act", "dve", "pool")):
                inc = eng
        rects = None
        if self.sim_safe and eng in ("pe", "act", "dve", "pool"):
            rng = self.strict_rng.get(eng)
            if eng not in self.relax or (
                    rng is not None and rng[0] <= self.cnt[eng] < rng[1]):
                waits = waits + ((eng, self.cnt[eng]),)
            elif eng != "pe":
                try:
                    rec = _Rec()
                    fn(rec)
                    rects = ([(True, _ap_rect(a)) for a in rec.w]
                             + [(False, _ap_rect(a)) for a in rec.r])
                except Exception:
                    rects = None
                    waits = waits + ((eng, self.cnt[eng]),)
                else:
                    need = {}
                    acc = self._acc[eng]
                    for isw, (nm, pl, ph, fl, fh) in rects:
                        for (sem, c, oisw, opl, oph, ofl, ofh) in acc.get(nm, ()):
                            if ((isw or oisw) and pl < oph and opl < ph
                                    and fl < ofh and ofl < fh):
                                if c > need.get(sem, -1):
                                    need[sem] = c
                    if eng == "act":
                        for nm_c, a_c in rec.calls:
                            if nm_c != "activation" or len(a_c) < 3:
                                continue
                            f = str(a_c[2]).rsplit(".", 1)[-1]
                            ts = _TSET.get(f)
                            if ts is not None and ts != self._tbl:
                                self._tbl = ts
                                if self.cnt[eng] > need.get(eng, -1):
                                    need[eng] = self.cnt[eng]
                    for sem, c in sorted(need.items()):
                        if c > self._bar.get(sem, 0):
                            waits = waits + ((sem, c),)
            if inc is None:
                inc = eng
        v = None
        if inc is not None:
            self.cnt[inc] += n
            v = self.cnt[inc]
        if rects is not None:
            al = self._acc[eng]
            for isw, rect in rects:
                al.setdefault(rect[0], []).append((inc, v, isw) + rect[1:])
        self.ops[eng].append((waits, fn, inc, n))
        return v

    def dma(self, eng, dst, src, waits=(), sem=None, **kw):
        if sem is None:
            sem = "d_" + eng
        self.mksem(sem)
        return self.op(eng, lambda e: e.dma_start(dst, src, **kw), waits, sem, 16)

    def build(self):
        blk = self.st.enter_context(self.nc.Block())
        for e in ENGS:
            ops = self.ops[e]

            def body(eng, ops=ops):
                lastw = {}
                for waits, fn, inc, n in ops:
                    for s, v in waits:
                        if v > 0 and lastw.get(s, 0) < v:
                            eng.wait_ge(self.sems[s], v)
                            lastw[s] = v
                    instr = fn(eng)
                    if inc is not None:
                        assert instr is not None
                        instr.then_inc(self.sems[inc], n)

            getattr(blk, _ENG_ATTR[e])(body)
        self.st.close()
        return self.nc


class RB:

    def __init__(self, n):
        self.rel = [("pe", 0)] * n
        self.i = 0

    def acquire(self):
        k = self.i % len(self.rel)
        self.i += 1
        sem, v = self.rel[k]
        return k, [(sem, v)]

    def release(self, k, sem, v):
        self.rel[k] = (sem, v)



def emit_ln_stats(P, xr, t0, tile, xqb, rb_xq, statr, statb, onesb, ones1b,
                  pk_sum, pk_sq, pk_m, pk_r, bank_rel, pk_s=None, extra_waits=(),
                  rep_extra=()):
    mmS = mmQ = None
    for c in range(KC):
        src = xr[:, c * T + t0: c * T + t0 + tile]
        kb, w = rb_xq.acquire()
        a1 = P.op("act", lambda e, s=src, d=xqb[kb]: e.activation(d[:, 0:tile], s, AF.Copy),
                  list(w) + list(extra_waits), "act")
        mmS = P.op("pe", lambda e, d=xqb[kb], c=c: e.matmul(
            pk_sum[0:1, 0:tile], onesb[:], d[:, 0:tile],
            start=(c == 0), stop=(c == KC - 1)),
            [("act", a1)] + ([bank_rel["sum"]] if c == 0 else []), "pe")
        kb2, w2 = rb_xq.acquire()
        a2 = P.op("act", lambda e, s=src, d=xqb[kb2]: e.activation(d[:, 0:tile], s, AF.Square),
                  list(w2), "act")
        mmQ = P.op("pe", lambda e, d=xqb[kb2], c=c: e.matmul(
            pk_sq[0:1, 0:tile], onesb[:], d[:, 0:tile],
            start=(c == 0), stop=(c == KC - 1)),
            [("act", a2)] + ([bank_rel["sq"]] if c == 0 else []), "pe")
        rb_xq.release(kb, "pe", mmS)
        rb_xq.release(kb2, "pe", mmQ)

    def sl(i):
        return statr[0:1, i * tile:(i + 1) * tile]

    inv = 1.0 / D
    d1 = P.op("dve", lambda e: e.tensor_scalar(sl(0), pk_sum[0:1, 0:tile], inv, None,
                                               ALU.mult, ALU.bypass), [("pe", mmS)], "dve")
    d2 = P.op("dve", lambda e: e.tensor_scalar(sl(1), pk_sq[0:1, 0:tile], inv, None,
                                               ALU.mult, ALU.bypass), [("pe", mmQ)], "dve")
    bank_rel["sum"] = ("dve", d1)
    bank_rel["sq"] = ("dve", d2)
    P.op("dve", lambda e: e.tensor_tensor(sl(2), sl(0), sl(0), ALU.mult))
    d4 = P.op("dve", lambda e: e.tensor_tensor(sl(1), sl(1), sl(2), ALU.subtract),
              inc="dve")
    d4b = P.op("dve", lambda e: e.tensor_scalar(sl(2), sl(1), 1e-5, None,
                                                ALU.add, ALU.bypass), [("dve", d4)], "dve")
    a3 = P.op("act", lambda e: e.activation(sl(2), sl(2), AF.Ln),
              [("dve", d4b)], "act")
    a4 = P.op("act", lambda e: e.activation(sl(3), sl(2), AF.Exp, scale=-0.5),
              [("act", a3)], "act")
    have_s = pk_s is not None
    if have_s:
        d5 = P.op("dve", lambda e: e.tensor_scalar(sl(4), sl(1), 1e-5, None,
                                                   ALU.add, ALU.bypass), [("dve", d4)], "dve")
        P.op("dve", lambda e: e.reciprocal(sl(4), sl(4)))
        P.op("dve", lambda e: e.tensor_tensor(sl(4), sl(1), sl(4), ALU.mult))
        d6 = P.op("dve", lambda e: e.tensor_scalar(sl(4), sl(4), 1e-6, None,
                                                   ALU.add, ALU.bypass), inc="dve")
        a5 = P.op("act", lambda e: e.activation(sl(4), sl(4), AF.Ln), [("dve", d6)], "act")
        a6 = P.op("act", lambda e: e.activation(sl(4), sl(4), AF.Exp, scale=-0.5),
                  inc="act")
        d7 = P.op("dve", lambda e: e.tensor_tensor(sl(5), sl(3), sl(4), ALU.mult),
                  [("act", a6)], "dve")

    def sb(i):
        return statb[0:1, i * tile:(i + 1) * tile]

    reps = [("m", sl(0), 0, pk_m, d1, 6), ("r", sl(3), 2, pk_r, a4, 7)]
    if have_s:
        reps.append(("s", sl(5), 4, pk_s, d7, 2))
    out = {}
    for name, row, bi, pk, wv, tmpi in reps:
        wsem = "dve" if name in ("m", "s") else "act"
        ah = P.op("act", lambda e, r=row, o=sb(bi): e.activation(o, r, AF.Copy),
                  [(wsem, wv)], "act")
        dl = P.op("dve", lambda e, r=row, h=sb(bi), o=sl(tmpi): e.tensor_tensor(o, r, h, ALU.subtract),
                  [("act", ah)], "dve")
        al = P.op("act", lambda e, o=sb(bi + 1), r=sl(tmpi): e.activation(o, r, AF.Copy),
                  [("dve", dl)], "act")
        m1 = P.op("pe", lambda e, pk=pk, h=sb(bi): e.matmul(
            pk, ones1b[:], h, start=True, stop=False),
            [("act", ah), bank_rel[name]] + list(rep_extra), None)
        m2 = P.op("pe", lambda e, pk=pk, l=sb(bi + 1): e.matmul(
            pk, ones1b[:], l, start=False, stop=True),
            [("act", al)], "pe")
        out[name] = m2
    out["dtail"] = ("dve", P.cnt["dve"])
    return out



def emit_phase_A(P, sh, tile=256):
    ntiles = T // tile
    qpt = tile // Q
    upto = 'full'; max_tiles = None; scan_heads = NH; no_ops = ()
    xr = sh["xr"]; sid = sh["sid"]; sidb = sh["sidb"]
    onesb = sh["onesb"]; ones1b = sh["ones1b"]

    wA = P.dram("A_wA", [D, PROJ_PAD], F32, kind="ExternalInput")
    convT = P.dram("A_convT", [DI, 4], F32, kind="ExternalInput")
    cbT = P.dram("A_cbT", [128, 16], F32, kind="ExternalInput")
    nscT = P.dram("A_nscT", [128, 8], F32, kind="ExternalInput")
    alog = P.dram("A_alog", [NH, 1], F32, kind="ExternalInput")
    dtbias = P.dram("A_dtbias", [NH, 1], F32, kind="ExternalInput")
    wo_in = P.dram("A_wo_in", [DI, D], F32, kind="ExternalInput")
    caus = P.dram("A_caus", [128, 128], F32, kind="ExternalInput")
    seli = P.dram("A_seli", [128, NH * 128], F32, kind="ExternalInput")
    wab = P.dram("A_wab_scr", [D, PROJ_PAD], BF16)
    wob = P.dram("A_wob_scr", [DI, D], BF16)
    dbg_hooks = set()

    xnb = P.sbuf("xnb", [128, KC * tile], BF16)
    wbuf = [P.sbuf(f"wbuf{i}", [128, KC * 256], BF16) for i in range(3)]
    wobuf = [P.sbuf(f"wobuf{i}", [128, 16 * 128], BF16) for i in range(3)]
    sgate = P.sbuf("sgate", [128, qpt * DI], BF16)
    sbm = P.sbuf("sbm", [128, 16 * (tile + 3)], BF16)
    hbuf = P.sbuf("hbuf", [128, 16 * tile], BF16)
    hT = [P.sbuf(f"hT{i}", [128, DI], BF16) for i in range(2)]
    ybuf = [P.sbuf(f"ybuf{i}", [128, DI], BF16) for i in range(2)]
    ygT = P.sbuf("ygT", [128, 16 * tile], BF16)
    state = P.sbuf("state", [64, NH * HD], F32)
    stateb = P.sbuf("stateb", [128, NH * HD], BF16)
    Bq = P.sbuf("Bq", [128, tile], BF16)
    Cq = P.sbuf("Cq", [128, tile], BF16)
    dtraw = P.sbuf("dtraw", [32, tile], F32)
    lnb = [P.sbuf(f"lnb{i}", [128, tile], F32) for i in range(2)]
    t1b = [P.sbuf(f"t1b{i}", [128, tile], F32) for i in range(2)]
    cvt = [P.sbuf(f"cvt{i}", [128, tile], BF16) for i in range(2)]
    xqb = [P.sbuf(f"xqb{i}", [128, tile], BF16) for i in range(4)]
    statr = P.sbuf("statr", [1, 8 * tile], F32)
    statb = P.sbuf("statb", [1, 6 * tile], BF16)
    dtv = P.sbuf("dtv", [32, 128], F32)
    Lbuf = P.sbuf("Lbuf", [32, 128], F32)
    edec = P.sbuf("edec", [32, 128], F32)
    dtrow = P.sbuf("dtrow", [32, 128], F32)
    eLend = P.sbuf("eLend", [32, 1], F32)
    Lhi = P.sbuf("Lhi", [128, 128], BF16)
    Llo = P.sbuf("Llo", [128, 128], BF16)
    Ltmp = P.sbuf("Ltmp", [32, 128], F32)
    tev = P.sbuf("tev", [128, 128], F32)
    negLT = P.sbuf("negLT", [128, 32], F32)
    expLT = P.sbuf("expLT", [128, 32], F32)
    Gs = P.sbuf("Gs", [128, 128], BF16)
    Gm = [P.sbuf(f"Gm{i}", [128, 128], BF16) for i in range(2)]
    Dc = [P.sbuf(f"Dc{i}", [128, 128], F32) for i in range(4)]
    Eb = [P.sbuf(f"Eb{i}", [128, 128], F32) for i in range(4)]
    Mb = [P.sbuf(f"Mb{i}", [128, 128], BF16) for i in range(4)]
    XdTb = [P.sbuf(f"XdTb{i}", [128, 64], BF16) for i in range(4)]
    XdT2b = [P.sbuf(f"XdT2b{i}", [128, 64], BF16) for i in range(4)]
    BTs = P.sbuf("BTs", [128, 128], BF16)
    scaus = P.sbuf("scaus", [128, 128], BF16)
    sselb = P.sbuf("sselb", [128, NH * 128], BF16)
    scw = P.sbuf("scw", [128, 16 * 4], F32)
    scb = P.sbuf("scb", [128, 16], F32)
    snsc = P.sbuf("snsc", [128, 8], F32)
    salog = P.sbuf("salog", [32, 1], F32)
    sdtb = P.sbuf("sdtb", [32, 1], F32)
    negeA = P.sbuf("negeA", [32, 1], F32)
    ones32 = P.sbuf("ones32", [32, 128], F32)
    zero32 = P.sbuf("zero32", [32, 1], F32)

    pk0 = P.psum("pk0", [128, 512], F32)
    pk1 = P.psum("pk1", [128, 512], F32)
    pk2 = P.psum("pk2", [128, 512], F32)
    pk3 = P.psum("pk3", [128, 512], F32)
    pk4 = P.psum("pk4", [128, 512], F32)
    pk5 = P.psum("pk5", [128, 1024], BF16)
    pk6 = P.psum("pk6", [128, 1024], BF16)
    pk7 = P.psum("pk7", [128, 512], F32)

    P.dma("sp", scw[:].rearrange("p (c k) -> p c k", c=16),
          convT.rearrange("(c p) k -> p c k", p=128), sem="initA")
    P.dma("sp", scb[:], cbT, sem="initA")
    P.dma("sp", snsc[:], nscT, sem="initA")
    P.dma("sp", salog[:], alog, sem="initA")
    P.dma("sp", sdtb[:], dtbias, sem="initA")
    w_init = P.cnt["initA"]
    w_wab = P.dma("pool", wab, wA, max_dma_last_dim=8176, sem="cast_a")
    w_wob = P.dma("pool", wob, wo_in, max_dma_last_dim=8176, sem="cast_b")
    w_caus = P.dma("pool", scaus[:], caus, sem="cast_c")
    w_sel = P.dma("pool", sselb[:], seli, sem="cast_d")

    P.op("dve", lambda e: e.memset(ones32[:], 1.0))
    P.op("dve", lambda e: e.memset(zero32[:], 0.0))
    P.op("dve", lambda e: e.memset(state[:], 0.0))
    P.op("dve", lambda e: e.memset(BTs[:], 0.0))
    P.op("dve", lambda e: e.memset(Bq[:], 0.0))
    P.op("dve", lambda e: e.memset(Cq[:], 0.0))
    P.op("dve", lambda e: e.memset(Lhi[:], 0.0))
    P.op("dve", lambda e: e.memset(Llo[:], 0.0))
    d_init = P.op("dve", lambda e: e.memset(stateb[:], 0.0), inc="dve")
    p_halo = P.op("pool", lambda e: e.memset(sbm[:], 0.0), inc="pool")
    a_eA = P.op("act", lambda e: e.activation(negeA[:], salog[:], AF.Exp),
                [("initA", w_init)], "act")
    d_eA = P.op("dve", lambda e: e.tensor_scalar(negeA[:], negeA[:], -1.0, None,
                                                 ALU.mult, ALU.bypass),
                [("act", a_eA)], "dve")

    fence_w = [("initA", w_init), ("dve", d_init), ("pool", p_halo),
               ("cast_c", w_caus), ("cast_d", w_sel), ("act", a_eA),
               ("dve", d_eA)]
    P.op("pe", lambda e: e.nop(), fence_w)
    P.op("act", lambda e: e.nop(), fence_w)
    P.op("dve", lambda e: e.engine_nop(), fence_w)
    P.op("pool", lambda e: e.engine_nop(), fence_w)

    rb_xq = RB(4)
    rb_t1 = RB(2)
    rb_ln = RB(2)
    rb_cvt = RB(2)
    rb_w = RB(3)
    rb_wo = RB(3)
    yrel = [("act", 0)] * 4
    yrel2 = [("dve", 0)] * 4
    srel = [("dve", 0)] * 4
    erel = [("dve", 0)] * 2
    trel = ("act", 0)
    bf16rel = [("act", 0)] * 2
    bf16turn = [0]
    Grel = ("act", 0)
    hTrel = [("pool", 0)] * 2
    ybrel = [("act", 0)] * 2
    gmrel = ("pool", 0)
    gsrel = ("pool", 0)
    tev_dve_rel = 0
    tev_pool_rel = 0
    aE_prev = [("act", 0)] * 4
    pm_prev = [("pool", 0)] * 4
    y2_prev = [("pe", 0)] * 4
    xd2_prev = [("dve", 0)] * 4
    s_prev = [("pe", 0)] * 4
    bank_rel = {"sum": ("dve", 0), "sq": ("dve", 0), "m": ("dve", 0),
                "r": ("dve", 0), "s": ("dve", 0)}
    sbm_cons = [("dve", 0), ("pool", 0)]
    prev_state_b = [("dve", d_init)] * NH
    last_wout_dve = 0
    statefin = []

    for i in range(ntiles if max_tiles is None else max_tiles):
        t0 = i * tile

        repv = emit_ln_stats(P, xr, t0, tile, xqb, rb_xq, statr, statb, onesb,
                             ones1b, pk0, pk1, pk2[:, 0:tile], pk3[:, 0:tile],
                             bank_rel, pk_s=pk2[:, 256:256 + tile],
                             extra_waits=[("initA", w_init),
                                          ("dve", last_wout_dve)],
                             rep_extra=[trel, Grel, erel[0], erel[1]])

        p_x_last = 0
        for c in range(KC):
            xc = xr[:, c * T + t0: c * T + t0 + tile]
            kt, wt = rb_t1.acquire()
            d_t1 = P.op("dve", lambda e, x=xc, kt=kt: e.tensor_tensor(
                t1b[kt][:], x, pk2[:, 0:tile], ALU.subtract),
                wt + [("pe", repv["m"]), ("pe", repv["s"]),
                      ("dve", last_wout_dve), repv["dtail"]], "dve")
            kl, wl = rb_ln.acquire()
            d_ln = P.op("dve", lambda e, kl=kl, kt=kt: e.tensor_tensor(
                lnb[kl][:], t1b[kt][:], pk3[:, 0:tile], ALU.mult),
                wl + [("pe", repv["r"])], "dve")
            p_x = P.op("pool", lambda e, x=xc, kl=kl: e.tensor_tensor(
                x, x, lnb[kl][:], ALU.add), [("dve", d_ln)], "pool")
            rb_ln.release(kl, "pool", p_x)
            d_xn = P.op("dve", lambda e, kt=kt, c=c: e.scalar_tensor_tensor(
                xnb[:, c * tile:(c + 1) * tile], t1b[kt][:], snsc[:, c:c + 1],
                pk2[:, 256:256 + tile], ALU.mult, ALU.mult),
                [("pe", repv["s"])], "dve")
            rb_t1.release(kt, "dve", d_xn)
            p_x_last = p_x
        d_xn_last = P.cnt["dve"]
        bank_rel["m"] = ("dve", d_xn_last)
        bank_rel["r"] = ("dve", d_xn_last)
        bank_rel["s"] = ("dve", d_xn_last)


        if upto == 'ln':
            continue
        if i > 0:
            for c in range(16):
                off = c * (tile + 3)
                p_halo = P.op("pool", lambda e, off=off: e.tensor_copy(
                    sbm[:, off:off + 3], sbm[:, off + tile:off + tile + 3]),
                    sbm_cons if c == 0 else [], "pool")

        ssm_evac = {}
        dt_evac = B_evac = C_evac = None
        for j in range(8, 18):
            kw, ww = rb_w.acquire()
            w_sl = P.dma("sp", wbuf[kw][:].rearrange("p (k f) -> p k f", k=KC),
                         wA_slab_ap(wab, j), ww + [("cast_a", w_wab)],
                         sem="wd%d" % kw)
            mmv = None
            for sub in range(2):
                fc = j * 2 + sub
                if fc >= 35:
                    continue
                pk, bk = (pk0, "sum") if (fc % 2 == 0) else (pk1, "sq")
                for k in range(KC):
                    mmv = P.op("pe", lambda e, kw=kw, k=k, sub=sub, pk=pk: e.matmul(
                        pk[:, 0:tile],
                        wbuf[kw][:, k * 256 + sub * 128: k * 256 + sub * 128 + 128],
                        xnb[:, k * tile:(k + 1) * tile],
                        start=(k == 0), stop=(k == KC - 1)),
                        [("wd%d" % kw, w_sl), ("dve", d_xn_last)] +
                        ([bank_rel[bk]] if k == 0 else []),
                        "pe" if k == KC - 1 else None)
                if fc <= 31:
                    c = fc - 16
                    off = c * (tile + 3)
                    av = P.op("act", lambda e, pk=pk, off=off: e.activation(
                        sbm[:, off + 3:off + 3 + tile], pk[:, 0:tile], AF.Copy),
                        [("pe", mmv), ("pool", p_halo)], "act")
                    ssm_evac[c] = av
                elif fc == 32:
                    av = dt_evac = P.op("act", lambda e, pk=pk: e.activation(
                        dtraw[:, 0:tile], pk[0:32, 0:tile], AF.Copy),
                        [("pe", mmv)], "act")
                elif fc == 33:
                    av = B_evac = P.op("act", lambda e, pk=pk: e.activation(
                        Bq[0:64, 0:tile], pk[0:64, 0:tile], AF.Copy),
                        [("pe", mmv)], "act")
                else:
                    av = C_evac = P.op("act", lambda e, pk=pk: e.activation(
                        Cq[0:64, 0:tile], pk[0:64, 0:tile], AF.Copy),
                        [("pe", mmv)], "act")
                bank_rel[bk] = ("act", av)
            rb_w.release(kw, "pe", mmv)

        gate_evac = {}
        gbank = {0: (pk0, "sum"), 1: (pk1, "sq")}
        for ns in range(4):
            mm_tc = {0: None, 1: None}
            for jj in range(2):
                j = ns * 2 + jj
                kw, ww = rb_w.acquire()
                w_sl = P.dma("sp", wbuf[kw][:].rearrange("p (k f) -> p k f", k=KC),
                             wA_slab_ap(wab, j), ww + [("cast_a", w_wab)],
                             sem="wd%d" % kw)
                for tc in range(qpt):
                    pk, bk = gbank[tc]
                    for k in range(KC):
                        mm_tc[tc] = P.op("pe", lambda e, kw=kw, k=k, jj=jj, tc=tc, pk=pk: e.matmul(
                            pk[:, jj * 256:(jj + 1) * 256],
                            xnb[:, k * tile + tc * 128: k * tile + tc * 128 + 128],
                            wbuf[kw][:, k * 256:(k + 1) * 256],
                            start=(k == 0), stop=(k == KC - 1)),
                            [("wd%d" % kw, w_sl), ("dve", d_xn_last)] +
                            ([bank_rel[bk]] if (k == 0 and jj == 0) else []),
                            "pe" if k == KC - 1 else None)
                rb_w.release(kw, "pe", mm_tc[qpt - 1])
            for tc in range(qpt):
                pk, bk = gbank[tc]
                av = P.op("act", lambda e, pk=pk, tc=tc, ns=ns: e.activation(
                    sgate[:, tc * DI + ns * 512: tc * DI + (ns + 1) * 512],
                    pk[:, 0:512], AF.Silu), [("pe", mm_tc[tc])], "act")
                gate_evac[(tc, ns)] = av
                bank_rel[bk] = ("act", av)

        if upto == 'proj':
            continue
        h_evac = {}
        for c in range(16):
            eng = "dve"
            kc_, wc = rb_cvt.acquire()
            off = c * (tile + 3)
            waits = wc + [("act", ssm_evac[c]), ("pool", p_halo),
                          ("initA", w_init)]
            P.op(eng, lambda e, kc_=kc_, c=c, off=off: e.tensor_scalar(
                cvt[kc_][:], sbm[:, off:off + tile], scw[:, c * 4 + 0:c * 4 + 1],
                scb[:, c:c + 1], ALU.mult, ALU.add), waits)
            civ = None
            for k in (1, 2, 3):
                civ = P.op(eng, lambda e, kc_=kc_, c=c, k=k, off=off: e.scalar_tensor_tensor(
                    cvt[kc_][:], sbm[:, off + k:off + k + tile],
                    scw[:, c * 4 + k:c * 4 + k + 1], cvt[kc_][:],
                    ALU.mult, ALU.add), inc=(eng if k == 3 else None))
            av = P.op("act", lambda e, kc_=kc_, c=c: e.activation(
                hbuf[:, c * tile:(c + 1) * tile], cvt[kc_][:], AF.Silu),
                [(eng, civ)], "act")
            h_evac[c] = av
            rb_cvt.release(kc_, "act", av)
        h_all = P.cnt["act"]


        if upto == 'conv':
            continue
        yrel[2] = yrel2[2] = srel[2] = bank_rel["sum"]
        yrel[3] = yrel2[3] = srel[3] = bank_rel["sq"]
        for q in range(qpt):
            g = i * qpt + q
            dtq = dtraw[:, q * Q:(q + 1) * Q]
            a_abs = P.op("act", lambda e, s=dtq: e.activation(
                Ltmp[:], s, AF.Abs, bias=sdtb[:, 0:1]),
                [("act", dt_evac), ("initA", w_init), ("dve", P.cnt["dve"])], "act")
            a_e = P.op("act", lambda e: e.activation(Ltmp[:], Ltmp[:], AF.Exp,
                                                     scale=-1.0), inc="act")
            a_l = P.op("act", lambda e: e.activation(Ltmp[:], Ltmp[:], AF.Ln,
                                                     bias=1.0), inc="act")
            d_r = P.op("dve", lambda e, s=dtq: e.tensor_scalar(
                dtv[:], s, sdtb[:, 0:1], 0.0, ALU.add, ALU.max),
                [("act", dt_evac), ("pe", P.cnt["pe"])], "dve")
            d_dt = P.op("dve", lambda e: e.tensor_tensor(
                dtv[:], dtv[:], Ltmp[:], ALU.add),
                [("act", a_l)], "dve")
            d_a = P.op("dve", lambda e: e.tensor_scalar(
                Ltmp[:], dtv[:], negeA[:, 0:1], None, ALU.mult, ALU.bypass),
                [("dve", d_eA)], "dve")
            d_L = P.op("dve", lambda e: e.tensor_tensor_scan(
                Lbuf[:], ones32[:], Ltmp[:], zero32[:, 0:1],
                ALU.mult, ALU.add), [("dve", d_a), ("pe", P.cnt["pe"])], "dve")
            a_ed = P.op("act", lambda e: e.activation(
                edec[:], Lbuf[:], AF.Exp, scale=-1.0,
                bias=Lbuf[:, 127:128]), [("dve", d_L)], "act")
            a_dt0 = P.op("act", lambda e: e.activation(
                eLend[:], Lbuf[:, 127:128], AF.Exp), [("dve", d_L)], "act")
            d_dtr = P.op("dve", lambda e: e.tensor_scalar(
                dtrow[:], ones32[:], eLend[:, 0:1], None, ALU.mult, ALU.bypass),
                [("act", a_dt0), ("pe", P.cnt["pe"])], "dve")
            a_hi = P.op("act", lambda e: e.activation(Lhi[0:32, :], Lbuf[:], AF.Copy),
                        [("dve", d_L), ("pe", P.cnt["pe"])], "act")
            d_lo = P.op("dve", lambda e: e.tensor_tensor(
                Ltmp[:], Lbuf[:], Lhi[0:32, :], ALU.subtract), [("act", a_hi)], "dve")
            a_lo = P.op("act", lambda e: e.activation(Llo[0:32, :], Ltmp[:], AF.Copy),
                        [("dve", d_lo)], "act")
            id32 = sid[0:32, 0:32]
            pe_t1 = P.op("pe", lambda e: e.transpose(pk3[:, 256:288], dtv[:], id32),
                         [("dve", d_dt), ("dve", d_a), trel, Grel, erel[1],
                          ("initA", w_init)], None)
            pe_t1 = P.op("pe", lambda e: e.transpose(pk3[:, 288:320], Lbuf[:], id32),
                         [("dve", d_L), ("act", a_hi), ("dve", d_lo)], None)
            pe_t1 = P.op("pe", lambda e: e.transpose(pk3[:, 320:352], edec[:], id32),
                         [("act", a_ed)], None)
            pe_t1 = P.op("pe", lambda e: e.transpose(pk3[:, 352:384], dtrow[:], id32),
                         [("dve", d_dtr)], "pe")
            a_tev = P.op("act", lambda e: e.activation(tev[:], pk3[:, 256:384], AF.Copy),
                         [("pe", pe_t1), ("dve", tev_dve_rel), ("pool", tev_pool_rel)],
                         "act")
            trel = ("act", a_tev)
            a_dtotT = a_tev
            d_nL = P.op("dve", lambda e: e.tensor_scalar(
                negLT[:], tev[:, 32:64], -1.0, None, ALU.mult, ALU.bypass),
                [("act", a_tev)], "dve")
            a_eL = P.op("act", lambda e: e.activation(expLT[:], tev[:, 32:64], AF.Exp),
                        [("act", a_tev)], "act")
            bt = bf16turn[0] % 2
            bf16turn[0] += 1
            pkb = pk5 if bt == 0 else pk6
            pe_t3 = P.op("pe", lambda e, q=q, pkb=pkb: e.transpose(
                pkb[:, 0:64], Bq[0:64, q * Q:(q + 1) * Q], sidb[0:64, 0:64]),
                [("act", B_evac), bf16rel[bt]], "pe")
            a_BT = P.op("act", lambda e, pkb=pkb: e.activation(
                BTs[:, 0:64], pkb[:, 0:64], AF.Copy), [("pe", pe_t3)], "act")
            bf16rel[bt] = ("act", a_BT)
            pe_G = P.op("pe", lambda e, q=q: e.matmul(
                pk3[:, 384:512], Bq[:, q * Q:(q + 1) * Q], Cq[:, q * Q:(q + 1) * Q],
                start=True, stop=True),
                [("act", B_evac), ("act", C_evac), Grel, ("act", a_tev),
                 erel[1]], "pe")
            a_G = P.op("act", lambda e: e.activation(Gs[:], pk3[:, 384:512], AF.Copy),
                       [("pe", pe_G), gsrel], "act")
            Grel = ("act", a_G)
            kh = g % 2
            hT_ev = 0
            for c in range(16):
                bt = bf16turn[0] % 2
                bf16turn[0] += 1
                pkb = pk5 if bt == 0 else pk6
                pe_th = P.op("pe", lambda e, c=c, q=q, pkb=pkb: e.transpose(
                    pkb[:, 128:256],
                    hbuf[:, c * tile + q * Q: c * tile + q * Q + 128], sidb[:]),
                    [("act", h_evac[c]), bf16rel[bt]] +
                    ([hTrel[kh]] if c == 0 else []), "pe")
                av = P.op("act", lambda e, c=c, kh=kh, pkb=pkb: e.activation(
                    hT[kh][:, c * 128:(c + 1) * 128],
                    pkb[:, 128:256], AF.Copy),
                    [("pe", pe_th)], "act")
                bf16rel[bt] = ("act", av)
                hT_ev = av
            if scan_heads == 0:
                continue
            p_gmc = P.op("pool", lambda e: e.tensor_tensor(
                Gm[0][:], Gs[:], scaus[:], ALU.mult),
                [("act", a_G), ("cast_c", w_caus), gmrel], "pool")
            gsrel = ("pool", p_gmc)
            d_yg_last = 0
            G4 = 4
            for hg in range(0, scan_heads, G4):
                hs = list(range(hg, min(hg + G4, scan_heads)))
                st = {h: {} for h in hs}
                for h in hs:
                    ks = h % 4
                    pke = pk2 if h % 2 == 0 else pk3
                    P.op("pe", lambda e, h=h, pke=pke: e.matmul(
                        pke[:, 0:128],
                        sselb[:, h * 128:(h + 1) * 128], Lhi[:],
                        start=True, stop=False),
                        [("act", a_hi), ("cast_d", w_sel), erel[h % 2],
                         ("dve", d_xn_last)] +
                        ([("act", a_tev), ("act", a_G)] if h % 2 == 1 else []),
                        None)
                    pe_rep = P.op("pe", lambda e, h=h, pke=pke: e.matmul(
                        pke[:, 0:128],
                        sselb[:, h * 128:(h + 1) * 128], Llo[:],
                        start=False, stop=True), [("act", a_lo)], "pe")
                    d_dc = P.op("dve", lambda e, ks=ks, h=h, pke=pke: e.tensor_scalar(
                        Dc[ks][:], pke[:, 0:128],
                        negLT[:, h:h + 1], 0.0, ALU.add, ALU.min),
                        [("pe", pe_rep), ("dve", d_nL), aE_prev[ks]], "dve")
                    erel[h % 2] = ("dve", d_dc)
                    a_E = P.op("act", lambda e, ks=ks: e.activation(
                        Eb[ks][:], Dc[ks][:], AF.Exp),
                        [("dve", d_dc), pm_prev[ks]], "act")
                    aE_prev[ks] = ("act", a_E)
                for h in hs:
                    ks = h % 4
                    p_m = P.op("pool", lambda e, ks=ks: e.tensor_tensor(
                        Mb[ks][:], Eb[ks][:], Gm[0][:], ALU.mult),
                        [aE_prev[ks], ("pool", p_gmc), y2_prev[ks]], "pool")
                    pm_prev[ks] = ("pool", p_m)
                    p_xd = P.op("dve", lambda e, ks=ks, h=h, kh=kh: e.tensor_scalar(
                        XdTb[ks][:], hT[kh][:, h * 64:(h + 1) * 64],
                        tev[:, h:h + 1], None, ALU.mult, ALU.bypass),
                        [("act", hT_ev), ("act", a_tev), y2_prev[ks],
                         xd2_prev[ks]], "dve")
                    p_xd2 = P.op("dve", lambda e, ks=ks, h=h: e.tensor_scalar(
                        XdT2b[ks][:], XdTb[ks][:], tev[:, 64 + h:64 + h + 1],
                        None, ALU.mult, ALU.bypass),
                        [("dve", p_xd), s_prev[ks]], "dve")
                    xd2_prev[ks] = ("dve", p_xd2)
                    st[h].update(p_m=p_m, p_xd=p_xd, p_xd2=p_xd2)
                for h in hs:
                    ks = h % 4
                    par = h % 4
                    pkp = (pk4, pk7, pk0, pk1)[par]
                    pe_yi = P.op("pe", lambda e, h=h, q=q, pkp=pkp: e.matmul(
                        pkp[:, 0:64], Cq[:, q * Q:(q + 1) * Q],
                        stateb[:, h * 64:(h + 1) * 64], start=True, stop=True),
                        [("act", C_evac), prev_state_b[h], yrel[par],
                         yrel2[par], srel[par]], "pe")
                    pe_y2 = P.op("pe", lambda e, ks=ks, pkp=pkp: e.matmul(
                        pkp[:, 64:128], Mb[ks][:],
                        XdTb[ks][:], start=True, stop=True),
                        [("pool", st[h]["p_m"]), ("dve", st[h]["p_xd"])], "pe")
                    y2_prev[ks] = ("pe", pe_y2)
                    pe_s = P.op("pe", lambda e, ks=ks, pkp=pkp: e.matmul(
                        pkp[:, 128:192], BTs[:],
                        XdT2b[ks][:], start=True, stop=True),
                        [("dve", st[h]["p_xd2"]), ("act", a_BT)], "pe")
                    st[h]["pe_s"] = pe_s
                for h in hs:
                    par = h % 4
                    pkp = (pk4, pk7, pk0, pk1)[par]
                    pe_s = st[h]["pe_s"]
                    a_yi = P.op("act", lambda e, h=h, kh=kh, pkp=pkp: e.activation(
                        ybuf[kh][:, h * 64:(h + 1) * 64],
                        pkp[:, 0:64], AF.Copy,
                        scale=expLT[:, h:h + 1]),
                        [("pe", pe_s), ("act", a_eL)] +
                        ([ybrel[kh]] if h == 0 else []), "act")
                    yrel[par] = ("act", a_yi)
                    d_yg = P.op("dve", lambda e, h=h, kh=kh, pkp=pkp: e.tensor_tensor(
                        ybuf[kh][:, h * 64:(h + 1) * 64],
                        ybuf[kh][:, h * 64:(h + 1) * 64],
                        pkp[:, 64:128], ALU.add),
                        [("pe", pe_s), ("act", a_yi)], "dve")
                    yrel2[par] = ("dve", d_yg)
                    d_yg_last = d_yg
                    d_st = P.op("dve", lambda e, h=h, pkp=pkp: e.scalar_tensor_tensor(
                        state[:, h * 64:(h + 1) * 64],
                        state[:, h * 64:(h + 1) * 64],
                        tev[0:64, 96 + h:97 + h],
                        pkp[0:64, 128:192],
                        ALU.mult, ALU.add), [("pe", pe_s), ("act", a_dtotT)], "dve")
                    srel[par] = ("dve", d_st)
                    p_stb = P.op("pool", lambda e, h=h: e.tensor_copy(
                        stateb[0:64, h * 64:(h + 1) * 64],
                        state[:, h * 64:(h + 1) * 64]),
                        [("dve", d_st)], "pool")
                    prev_state_b[h] = ("pool", p_stb)
            hTrel[kh] = ("dve", P.cnt["dve"])
            gmrel = ("pool", P.cnt["pool"])
            tev_dve_rel = P.cnt["dve"]
            tev_pool_rel = P.cnt["pool"]

            d_gate = P.op("dve", lambda e, kh=kh, q=q: e.tensor_tensor(
                ybuf[kh][:], ybuf[kh][:], sgate[:, q * DI:(q + 1) * DI], ALU.mult),
                [("dve", d_yg_last)] +
                [("act", gate_evac[(q, ns)]) for ns in range(4)], "dve")
            d_yg_last = d_gate

            ygT_ev = 0
            for c in range(16):
                bt = bf16turn[0] % 2
                bf16turn[0] += 1
                pkb = pk5 if bt == 0 else pk6
                pe_ty = P.op("pe", lambda e, c=c, kh=kh, pkb=pkb: e.transpose(
                    pkb[:, 128:256],
                    ybuf[kh][:, c * 128:(c + 1) * 128], sidb[:]),
                    [("dve", d_yg_last), bf16rel[bt]], "pe")
                av = P.op("act", lambda e, c=c, q=q, pkb=pkb: e.activation(
                    ygT[:, c * tile + q * Q: c * tile + q * Q + 128],
                    pkb[:, 128:256], AF.Copy),
                    [("pe", pe_ty)], "act")
                bf16rel[bt] = ("act", av)
                ygT_ev = av
            ybrel[kh] = ("act", ygT_ev)

        sbm_cons = [("dve", P.cnt["dve"]), ("pool", P.cnt["pool"])]
        if upto == 'scan':
            continue
        bank_rel["sum"] = srel[2]
        bank_rel["sq"] = srel[3]

        for dm in range(KC):
            kw, ww = rb_wo.acquire()
            w_sl = P.dma("sp", wobuf[kw][:].rearrange("p (k d) -> p k d", k=16),
                         wo_slab_ap(wob, dm), ww + [("cast_b", w_wob)],
                         sem="wod%d" % kw)
            pk, bk = (pk0, "sum") if (dm % 2 == 0) else (pk1, "sq")
            mmv = None
            for fc in range(16):
                mmv = P.op("pe", lambda e, kw=kw, fc=fc, pk=pk: e.matmul(
                    pk[:, 0:tile], wobuf[kw][:, fc * 128:(fc + 1) * 128],
                    ygT[:, fc * tile:(fc + 1) * tile],
                    start=(fc == 0), stop=(fc == 15)),
                    [("wod%d" % kw, w_sl), ("act", ygT_ev)] +
                    ([bank_rel[bk]] if fc == 0 else []),
                    "pe" if fc == 15 else None)
            rb_wo.release(kw, "pe", mmv)
            xc = xr[:, dm * T + t0: dm * T + t0 + tile]
            dv = P.op("dve", lambda e, x=xc, pk=pk: e.tensor_tensor(
                x, x, pk[:, 0:tile], ALU.add),
                [("pe", mmv), ("pool", p_x_last)], "dve")
            bank_rel[bk] = ("dve", dv)
            last_wout_dve = dv

    return




def emit_phase_B(P, sh, tile=256):
    ntiles = T // tile
    FF = 4096
    NFC = FF // 128
    xr = sh["xr"]; sid = sh["sid"]; sidb = sh["sidb"]
    onesb = sh["onesb"]; ones1b = sh["ones1b"]

    W1 = P.dram("B_W1", [FF, D], F32, kind="ExternalInput")
    W2 = P.dram("B_W2", [D, FF], F32, kind="ExternalInput")
    b1T = P.dram("B_b1T", [128, NFC], F32, kind="ExternalInput")
    b2T = P.dram("B_b2T", [128, KC], F32, kind="ExternalInput")

    w2ts = P.dram("B_w2t_scr", [NFC, 128, 1024], BF16)
    w1t = P.sbuf("w1t", [128, KC * FF], BF16)
    w2sl = [P.sbuf(f"w2sl{i}", [128, 8 * 1024], BF16) for i in range(2)]
    w2st = [P.sbuf(f"w2st{i}", [128, 1024], BF16) for i in range(2)]
    xn2 = P.sbuf("xn2", [128, KC * tile], BF16)
    ws = [P.sbuf(f"ws{i}", [128, 1024], F32) for i in range(2)]
    wtt = [P.sbuf(f"wtt{i}", [128, 1024], BF16) for i in range(2)]
    tmpb = P.sbuf("tmpb", [128, 1024], BF16)
    hball = P.sbuf("hball", [128, NFC * tile], BF16)
    xqb = [P.sbuf(f"xqb{i}", [128, tile], BF16) for i in range(2)]
    statr = P.sbuf("statr", [1, 5 * tile], F32)
    statb = P.sbuf("statb", [1, 4 * tile], BF16)
    acc1 = P.sbuf("acc1", [128, 2], F32)
    accb = P.sbuf("accb", [128, 4], BF16)
    thr = P.sbuf("thr", [128, 2], F32)
    nthr = P.sbuf("nthr", [128, 2], F32)
    rowt = P.sbuf("rowt", [1, 8], F32)
    rowb = P.sbuf("rowb", [1, 8], BF16)
    sb1 = P.sbuf("sb1", [128, NFC], F32)
    sb2 = P.sbuf("sb2", [128, KC], F32)
    red = P.sbuf("red", [128, 1], F32)

    pk0 = P.psum("pk0", [128, 512], F32)
    pk1 = P.psum("pk1", [128, 512], F32)
    pk2 = P.psum("pk2", [128, 512], F32)
    pk3 = P.psum("pk3", [128, 512], F32)
    pk4 = P.psum("pk4", [128, 512], F32)
    pk7 = P.psum("pk7", [128, 512], F32)
    pk5 = P.psum("pk5", [128, 1024], BF16)
    pk6 = P.psum("pk6", [128, 1024], BF16)

    P.dma("sp", sb1[:], b1T, sem="initB")
    P.dma("sp", sb2[:], b2T, sem="initB")
    w_init = P.cnt["initB"]
    P.op("dve", lambda e: e.memset(acc1[:], 0.0), [("initB", w_init)])
    d_init = P.op("dve", lambda e: e.memset(red[:], 0.0), inc="dve")
    fence_w = [("initB", w_init), ("dve", d_init)]
    P.op("pe", lambda e: e.nop(), fence_w)
    P.op("act", lambda e: e.nop(), fence_w)
    P.op("pool", lambda e: e.engine_nop(), fence_w)

    def w1_slab_ap(fc):
        return W1[fc * 128:(fc + 1) * 128, :]

    def w2_slab_ap(fc):
        r = W2.rearrange("(k p) f -> p k f", p=128)
        return r[:, :, fc * 128:(fc + 1) * 128]

    rb_ws = RB(2)
    d_red = 0
    for wi in (0, 1):
        for c in range(NFC if wi == 0 else NFC):
            kw, ww = rb_ws.acquire()
            ap = w1_slab_ap(c) if wi == 0 else w2_slab_ap(c)
            w_sl = P.dma("sp", ws[kw][:], ap, ww, sem="wd%d" % kw)
            P.op("dve", lambda e, kw=kw: e.reduce_sum(
                red[:], ws[kw][:], axis=AX.X, apply_absolute_value=True),
                [("wd%d" % kw, w_sl), ("dve", d_init)], "dve")
            d_red = P.op("dve", lambda e, wi=wi: e.tensor_tensor(
                acc1[:, wi:wi + 1], acc1[:, wi:wi + 1], red[:], ALU.add), inc="dve")
            rb_ws.release(kw, "dve", d_red)
    a_h = P.op("act", lambda e: e.activation(accb[:, 0:2], acc1[:, 0:2], AF.Copy),
               [("dve", d_red)], "act")
    d_l = P.op("dve", lambda e: e.tensor_tensor(thr[:, 0:2], acc1[:, 0:2],
                                                accb[:, 0:2], ALU.subtract),
               [("act", a_h)], "dve")
    a_l = P.op("act", lambda e: e.activation(accb[:, 2:4], thr[:, 0:2], AF.Copy),
               [("dve", d_l)], "act")
    m1 = P.op("pe", lambda e: e.matmul(pk0[0:1, 0:2], onesb[:], accb[:, 0:2],
                                       start=True, stop=False), [("act", a_h)], None)
    m2 = P.op("pe", lambda e: e.matmul(pk0[0:1, 0:2], onesb[:], accb[:, 2:4],
                                       start=False, stop=True), [("act", a_l)], "pe")
    d_t = P.op("dve", lambda e: e.tensor_scalar(rowt[0:1, 0:2], pk0[0:1, 0:2],
                                                0.5 / (FF * D), None, ALU.mult,
                                                ALU.bypass), [("pe", m2)], "dve")
    d_t2 = P.op("dve", lambda e: e.tensor_scalar(rowt[0:1, 2:4], rowt[0:1, 0:2], -1.0,
                                                 None, ALU.mult, ALU.bypass), inc="dve")
    a_th = P.op("act", lambda e: e.activation(rowb[0:1, 0:4], rowt[0:1, 0:4],
                                              AF.Copy), [("dve", d_t2)], "act")
    d_tl = P.op("dve", lambda e: e.tensor_tensor(rowt[0:1, 4:8], rowt[0:1, 0:4],
                                                 rowb[0:1, 0:4], ALU.subtract),
                [("act", a_th)], "dve")
    a_tl = P.op("act", lambda e: e.activation(rowb[0:1, 4:8], rowt[0:1, 4:8],
                                              AF.Copy), [("dve", d_tl)], "act")
    m3 = P.op("pe", lambda e: e.matmul(pk1[:, 0:4], ones1b[:], rowb[0:1, 0:4],
                                       start=True, stop=False), [("act", a_th)], None)
    m4 = P.op("pe", lambda e: e.matmul(pk1[:, 0:4], ones1b[:], rowb[0:1, 4:8],
                                       start=False, stop=True), [("act", a_tl)], "pe")
    a_thr = P.op("act", lambda e: e.activation(thr[:, 0:2], pk1[:, 0:2], AF.Copy),
                 [("pe", m4)], "act")
    a_nthr = P.op("act", lambda e: e.activation(nthr[:, 0:2], pk1[:, 2:4], AF.Copy),
                  [("pe", m4)], "act")

    rb_wt = RB(2)
    bf16rel = [("act", 0)] * 2
    bfturn = [0]
    w2st_rel = [("act", 0), ("act", 0)]
    for wi in (0, 1):
        for c in range(NFC if wi == 0 else NFC):
            kw, ww = rb_ws.acquire()
            ap = w1_slab_ap(c) if wi == 0 else w2_slab_ap(c)
            w_sl = P.dma("sp", ws[kw][:], ap, ww, sem="wd%d" % kw)
            kt, wt_ = rb_wt.acquire()
            P.op("dve", lambda e, kw=kw, wi=wi: e.tensor_scalar(
                tmpb[:], ws[kw][:], nthr[:, wi:wi + 1], None, ALU.is_lt, ALU.bypass),
                [("wd%d" % kw, w_sl), ("act", a_nthr), ("act", a_thr)] + wt_, "dve")
            d_tn = P.op("dve", lambda e, kw=kw, kt=kt, wi=wi: e.scalar_tensor_tensor(
                wtt[kt][:], ws[kw][:], thr[:, wi:wi + 1], tmpb[:],
                ALU.is_gt, ALU.subtract), inc="dve")
            rb_ws.release(kw, "dve", d_tn)
            lastpe = None
            for k in range(8):
                bt = bfturn[0] % 2
                bfturn[0] += 1
                pkb = pk5 if bt == 0 else pk6
                pe_t = P.op("pe", lambda e, kt=kt, k=k, pkb=pkb: e.transpose(
                    pkb[:, 0:128], wtt[kt][:, k * 128:(k + 1) * 128], sidb[:]),
                    [("dve", d_tn), bf16rel[bt]], "pe")
                lastpe = pe_t
                if wi == 0:
                    fn = lambda e, c=c, k=k, pkb=pkb: e.activation(
                        w1t[:, k * FF + c * 128: k * FF + (c + 1) * 128],
                        pkb[:, 0:128], AF.Copy)
                else:
                    fn = lambda e, c=c, k=k, pkb=pkb: e.activation(
                        w2st[c % 2][:, k * 128:(k + 1) * 128],
                        pkb[:, 0:128], AF.Copy)
                av = P.op("act", fn, [("pe", pe_t)] +
                          ([w2st_rel[c % 2]] if (wi == 1 and k == 0) else []),
                          "act")
                bf16rel[bt] = ("act", av)
            if wi == 1:
                sm = "d_w2s%d" % (c % 2)
                w2st_rel[c % 2] = (sm, P.dma(
                    "sp", w2ts[c], w2st[c % 2][:],
                    [("act", av)], sem=sm))
            rb_wt.release(kt, "pe", lastpe)
    wt_done = P.cnt["act"]

    rb_xq = RB(2)
    rb_w2 = RB(2)
    bank_rel = {"sum": ("dve", 0), "sq": ("dve", 0), "m": ("dve", 0),
                "r": ("dve", 0)}
    orel = [("dve", 0)] * 2
    hrel = [("act", 0)] * 2
    h_mm_prev = 0

    for i in range(ntiles):
        t0 = i * tile
        repv = emit_ln_stats_x(P, xr, tile, xqb, rb_xq, statr, statb, onesb,
                               ones1b, pk0, pk1, pk2[:, 0:tile], pk3[:, 0:tile],
                               bank_rel, extra_waits=[orel[0], orel[1]],
                               xoff=t0, xstride=T)
        d_xn_last = 0
        for c in range(KC):
            xc = xr[:, c * T + t0: c * T + t0 + tile]
            P.op("dve", lambda e, x=xc, c=c: e.tensor_tensor(
                xn2[:, c * tile:(c + 1) * tile], x, pk2[:, 0:tile], ALU.subtract),
                [("pe", repv["m"]), ("pe", h_mm_prev), repv["dtail"]], "dve")
            d_xn_last = P.op("dve", lambda e, c=c: e.tensor_tensor(
                xn2[:, c * tile:(c + 1) * tile], xn2[:, c * tile:(c + 1) * tile],
                pk3[:, 0:tile], ALU.mult),
                [("pe", repv["r"])], "dve")
        bank_rel["m"] = ("dve", d_xn_last)
        bank_rel["r"] = ("dve", d_xn_last)

        a_h_last = 0
        for fc in range(NFC):
            par = fc % 2
            pkh = pk4 if par == 0 else pk7
            mmh = None
            for k in range(KC):
                mmh = P.op("pe", lambda e, k=k, fc=fc, pkh=pkh: e.matmul(
                    pkh[:, 0:tile],
                    w1t[:, k * FF + fc * 128:k * FF + (fc + 1) * 128],
                    xn2[:, k * tile:(k + 1) * tile],
                    start=(k == 0), stop=(k == KC - 1)),
                    [("act", wt_done), ("dve", d_xn_last)] +
                    ([hrel[par]] if k == 0 else []),
                    "pe" if k == KC - 1 else None)
            a_h2 = P.op("act", lambda e, fc=fc, pkh=pkh: e.activation(
                hball[:, fc * tile:(fc + 1) * tile], pkh[:, 0:tile], AF.Gelu,
                bias=sb1[:, fc:fc + 1]),
                [("pe", mmh)] + ([orel[0], orel[1]] if fc < 2 else []), "act")
            hrel[par] = ("act", a_h2)
            a_h_last = a_h2
        for fcg in range(4):
            kw2, ww2 = rb_w2.acquire()
            w_sl2 = P.dma("sp", w2sl[kw2][:].rearrange("p (g f) -> p g f", g=8),
                          w2ts[fcg * 8:(fcg + 1) * 8].rearrange("g p f -> p g f"),
                          list(ww2) + [("d_w2s0", P.cnt.get("d_w2s0", 0)),
                                       ("d_w2s1", P.cnt.get("d_w2s1", 0))],
                          sem="w2d%d" % kw2)
            for dm in range(KC):
                pko = pk0 if dm % 2 == 0 else pk1
                mmo = None
                for fci in range(8):
                    fc = fcg * 8 + fci
                    mmo = P.op("pe", lambda e, fci=fci, fc=fc, dm=dm, pko=pko,
                               kw2=kw2: e.matmul(
                        pko[:, 0:tile],
                        w2sl[kw2][:, fci * 1024 + dm * 128: fci * 1024 + (dm + 1) * 128],
                        hball[:, fc * tile:(fc + 1) * tile],
                        start=(fci == 0), stop=(fci == 7)),
                        [("act", a_h_last), ("w2d%d" % kw2, w_sl2)] +
                        ([orel[dm % 2]] if fci == 0 else []),
                        "pe" if fci == 7 else None)
                if fcg == 0:
                    dv = P.op("dve", lambda e, dm=dm, pko=pko, t0=t0: e.scalar_tensor_tensor(
                        xr[:, dm * T + t0: dm * T + t0 + tile],
                        pko[:, 0:tile], sb2[:, dm:dm + 1],
                        xr[:, dm * T + t0: dm * T + t0 + tile], ALU.add, ALU.add),
                        [("pe", mmo)], "dve")
                else:
                    dv = P.op("dve", lambda e, dm=dm, pko=pko, t0=t0: e.tensor_tensor(
                        xr[:, dm * T + t0: dm * T + t0 + tile],
                        xr[:, dm * T + t0: dm * T + t0 + tile],
                        pko[:, 0:tile], ALU.add),
                        [("pe", mmo)], "dve")
                orel[dm % 2] = ("dve", dv)
            rb_w2.release(kw2, "pe", mmo)
        h_mm_prev = P.cnt["pe"]

    return


def emit_ln_stats_x(P, xbuf, tile, xqb, rb_xq, statr, statb, onesb, ones1b,
                    pk_sum, pk_sq, pk_m, pk_r, bank_rel, extra_waits=(),
                    rep_extra=(), xoff=0, xstride=None):
    if xstride is None:
        xstride = tile
    mmS = mmQ = None
    for c in range(KC):
        src = xbuf[:, c * xstride + xoff: c * xstride + xoff + tile]
        kb, w = rb_xq.acquire()
        a1 = P.op("act", lambda e, s=src, d=xqb[kb]: e.activation(d[:, 0:tile], s, AF.Copy),
                  list(w) + list(extra_waits), "act")
        mmS = P.op("pe", lambda e, d=xqb[kb], c=c: e.matmul(
            pk_sum[0:1, 0:tile], onesb[:], d[:, 0:tile],
            start=(c == 0), stop=(c == KC - 1)),
            [("act", a1)] + ([bank_rel["sum"]] if c == 0 else []), "pe")
        kb2, w2 = rb_xq.acquire()
        a2 = P.op("act", lambda e, s=src, d=xqb[kb2]: e.activation(d[:, 0:tile], s, AF.Square),
                  list(w2) + list(extra_waits), "act")
        mmQ = P.op("pe", lambda e, d=xqb[kb2], c=c: e.matmul(
            pk_sq[0:1, 0:tile], onesb[:], d[:, 0:tile],
            start=(c == 0), stop=(c == KC - 1)),
            [("act", a2)] + ([bank_rel["sq"]] if c == 0 else []), "pe")
        rb_xq.release(kb, "pe", mmS)
        rb_xq.release(kb2, "pe", mmQ)

    def sl(i):
        return statr[0:1, i * tile:(i + 1) * tile]

    inv = 1.0 / D
    d1 = P.op("dve", lambda e: e.tensor_scalar(sl(0), pk_sum[0:1, 0:tile], inv, None,
                                               ALU.mult, ALU.bypass), [("pe", mmS)], "dve")
    d2 = P.op("dve", lambda e: e.tensor_scalar(sl(1), pk_sq[0:1, 0:tile], inv, None,
                                               ALU.mult, ALU.bypass), [("pe", mmQ)], "dve")
    bank_rel["sum"] = ("dve", d1)
    bank_rel["sq"] = ("dve", d2)
    P.op("dve", lambda e: e.tensor_tensor(sl(2), sl(0), sl(0), ALU.mult))
    d4 = P.op("dve", lambda e: e.tensor_tensor(sl(1), sl(1), sl(2), ALU.subtract),
              inc="dve")
    d4b = P.op("dve", lambda e: e.tensor_scalar(sl(2), sl(1), 1e-5, None,
                                                ALU.add, ALU.bypass), [("dve", d4)], "dve")
    a3 = P.op("act", lambda e: e.activation(sl(2), sl(2), AF.Ln), [("dve", d4b)], "act")
    a4 = P.op("act", lambda e: e.activation(sl(3), sl(2), AF.Exp, scale=-0.5),
              [("act", a3)], "act")
    reps = [("m", sl(0), 0, pk_m, ("dve", d1), 4), ("r", sl(3), 2, pk_r, ("act", a4), 4)]
    out = {}
    prev_al = None
    for name, row, bi, pk, wv, tmpi in reps:
        def sb(i):
            return statb[0:1, i * tile:(i + 1) * tile]
        ah = P.op("act", lambda e, r=row, o=sb(bi): e.activation(o, r, AF.Copy),
                  [wv], "act")
        dl = P.op("dve", lambda e, r=row, h=sb(bi), o=sl(tmpi): e.tensor_tensor(
            o, r, h, ALU.subtract),
            [("act", ah)] + ([("act", prev_al)] if prev_al else []), "dve")
        al = P.op("act", lambda e, o=sb(bi + 1), r=sl(tmpi): e.activation(o, r, AF.Copy),
                  [("dve", dl)], "act")
        prev_al = al
        m1 = P.op("pe", lambda e, pk=pk, h=sb(bi): e.matmul(
            pk, ones1b[:], h, start=True, stop=False),
            [("act", ah), bank_rel[name]] + list(rep_extra), None)
        m2 = P.op("pe", lambda e, pk=pk, l=sb(bi + 1): e.matmul(
            pk, ones1b[:], l, start=False, stop=True),
            [("act", al)], "pe")
        out[name] = m2
    out["dtail"] = ("dve", P.cnt["dve"])
    return out


def wA_slab_ap(wab, j):
    r = wab.rearrange("(k p) f -> p k f", p=128)
    return r[:, :, j * 256:(j + 1) * 256]


def wo_slab_ap(wob, dm):
    r = wob.rearrange("(k p) d -> p k d", p=128)
    return r[:, :, dm * 128:(dm + 1) * 128]



def emit_phase_C(P, sh, tile=256):
    ntiles = T // tile
    NHD = 16
    TC = tile // 128
    xr = sh["xr"]; sid = sh["sid"]; sidb = sh["sidb"]
    onesb = sh["onesb"]; ones1b = sh["ones1b"]

    Wq = P.dram("C_Wq", [D, D], F32, kind="ExternalInput")
    Wk = P.dram("C_Wk", [D, D], F32, kind="ExternalInput")
    Wv = P.dram("C_Wv", [D, D], F32, kind="ExternalInput")
    Wg = P.dram("C_Wg", [D, D], F32, kind="ExternalInput")
    Wo = P.dram("C_Wo", [D, D], F32, kind="ExternalInput")
    bg = P.dram("C_bg", [1, D], F32, kind="ExternalInput")

    xn3 = P.sbuf("xn3", [128, KC * T], BF16)
    WS = [P.sbuf(f"WS{i}", [128, KC * D], BF16) for i in range(3)]
    ktm = P.sbuf("ktm", [128, TC * D], BF16)
    vA = P.sbuf("vA", [128, TC * NHD * 65], BF16)
    qb = P.sbuf("qb", [128, KC * tile], BF16)
    gb = P.sbuf("gb", [128, TC * D], BF16)
    o2 = P.sbuf("o2", [128, TC * D], BF16)
    o2T = P.sbuf("o2T", [128, KC * tile], BF16)
    KVA = P.sbuf("KVA", [128, NHD * 65], BF16)
    bgrep = P.sbuf("bgrep", [128, D], F32)
    bgror = P.sbuf("bgror", [1, D], F32)
    bgrow = P.sbuf("bgrow", [1, 2 * D], BF16)
    et1 = P.sbuf("et1", [128, 512], F32)
    et2 = P.sbuf("et2", [128, 512], F32)
    zrec = P.sbuf("zrec", [128, 2], F32)
    xqb = [P.sbuf(f"xqb{i}", [128, tile], BF16) for i in range(4)]
    statr = P.sbuf("statr", [1, 5 * tile], F32)
    statb = P.sbuf("statb", [1, 4 * tile], BF16)

    pk0 = P.psum("pk0", [128, 512], F32)
    pk1 = P.psum("pk1", [128, 512], F32)
    pk2 = P.psum("pk2", [128, 512], F32)
    pk3 = P.psum("pk3", [128, 512], F32)
    pk4 = P.psum("pk4", [128, 512], F32)
    pk7 = P.psum("pk7", [128, 512], F32)
    pk5 = P.psum("pk5", [128, 1024], BF16)
    pk6 = P.psum("pk6", [128, 1024], BF16)

    P.dma("sp", bgror[:], bg, sem="initC")
    w_init = P.cnt["initC"]

    def wrearr(w):
        return w.rearrange("(k p) f -> p k f", p=128)

    w_k = P.dma("pool", WS[0][:].rearrange("p (k f) -> p k f", k=KC),
                wrearr(Wk), sem="cast_k")
    w_v = P.dma("pool", WS[1][:].rearrange("p (k f) -> p k f", k=KC),
                wrearr(Wv), sem="cast_v")
    w_o = P.dma("pool", WS[2][:].rearrange("p (k f) -> p k f", k=KC),
                wrearr(Wo), sem="cast_o")
    d_init = P.op("dve", lambda e: e.memset(vA[:], 1.0), inc="dve")
    fence_w = [("initC", w_init), ("dve", d_init)]
    P.op("pe", lambda e: e.nop(), fence_w)
    P.op("act", lambda e: e.nop(), fence_w)
    P.op("pool", lambda e: e.engine_nop(), fence_w)

    a_bh = P.op("act", lambda e: e.activation(bgrow[0:1, 0:D], bgror[:], AF.Copy),
                [("initC", w_init)], "act")
    d_bl = P.op("dve", lambda e: e.tensor_tensor(bgrep[0:1, 0:D], bgror[:],
                                                 bgrow[0:1, 0:D], ALU.subtract),
                [("act", a_bh)], "dve")
    a_bl = P.op("act", lambda e: e.activation(bgrow[0:1, D:2 * D], bgrep[0:1, 0:D],
                                              AF.Copy), [("dve", d_bl)], "act")
    a_bgrep = 0
    for half in range(2):
        pkh = pk0 if half == 0 else pk1
        P.op("pe", lambda e, pkh=pkh, half=half: e.matmul(
            pkh[:, 0:512], ones1b[:], bgrow[0:1, half * 512:half * 512 + 512],
            start=True, stop=False), [("act", a_bh), ("dve", d_bl)], None)
        m_bg = P.op("pe", lambda e, pkh=pkh, half=half: e.matmul(
            pkh[:, 0:512], ones1b[:], bgrow[0:1, D + half * 512:D + half * 512 + 512],
            start=False, stop=True), [("act", a_bl)], "pe")
        a_bgrep = P.op("act", lambda e, pkh=pkh, half=half: e.activation(
            bgrep[:, half * 512:(half + 1) * 512], pkh[:, 0:512], AF.Copy),
            [("pe", m_bg)], "act")

    rb_xq = RB(4)
    bank_rel = {"sum": ("act", a_bgrep), "sq": ("act", a_bgrep),
                "m": ("dve", 0), "r": ("dve", 0)}
    d_xn_last = 0
    for i in range(ntiles):
        t0 = i * tile
        repv = emit_ln_stats_x(P, xr, tile, xqb, rb_xq, statr, statb, onesb,
                               ones1b, pk0, pk1, pk2[:, 0:tile], pk3[:, 0:tile],
                               bank_rel, xoff=t0, xstride=T)
        for c in range(KC):
            xc = xr[:, c * T + t0: c * T + t0 + tile]
            P.op("dve", lambda e, x=xc, c=c, t0=t0: e.tensor_tensor(
                xn3[:, c * T + t0: c * T + t0 + tile], x, pk2[:, 0:tile],
                ALU.subtract), [("pe", repv["m"]), repv["dtail"]], "dve")
            d_xn_last = P.op("dve", lambda e, c=c, t0=t0: e.tensor_tensor(
                xn3[:, c * T + t0: c * T + t0 + tile],
                xn3[:, c * T + t0: c * T + t0 + tile], pk3[:, 0:tile], ALU.mult),
                [("pe", repv["r"])], "dve")
        bank_rel["m"] = ("dve", d_xn_last)
        bank_rel["r"] = ("dve", d_xn_last)

    prel = [("act", 0), ("act", 0)]
    kv_first = [True] * 4
    pproj = [0]
    kv_mm_last = None
    ktm_rel = ("pe", 0)
    vA_rel = ("pe", 0)
    for i in range(ntiles):
        t0 = i * tile
        k_ev = v_ev = None
        for wi, wbuf, wdone in ((0, WS[0], w_k), (1, WS[1], w_v)):
            for half in range(2):
                for tc in range(TC):
                    par = pproj[0] % 2
                    pproj[0] += 1
                    pkh = pk4 if par == 0 else pk7
                    mmv = None
                    for k in range(KC):
                        mmv = P.op("pe", lambda e, k=k, half=half, tc=tc, t0=t0,
                                   pkh=pkh, wbuf=wbuf: e.matmul(
                            pkh[:, 0:512],
                            xn3[:, k * T + t0 + tc * 128: k * T + t0 + tc * 128 + 128],
                            wbuf[:, k * D + half * 512: k * D + half * 512 + 512],
                            start=(k == 0), stop=(k == KC - 1)),
                            [("cast_k" if wi == 0 else "cast_v", wdone),
                             ("dve", d_xn_last)] + ([prel[par]] if k == 0 else []) +
                            ([ktm_rel] if (wi == 0 and half == 0 and tc == 0) else []) +
                            ([vA_rel] if (wi == 1 and half == 0 and tc == 0) else []),
                            "pe" if k == KC - 1 else None)
                    if wi == 0:
                        d_e1 = P.op("dve", lambda e, pkh=pkh: e.tensor_scalar(
                            et1[:], pkh[:, 0:512], 0.0, None, ALU.min, ALU.bypass),
                            [("pe", mmv)], "dve")
                        a_e = P.op("act", lambda e: e.activation(et2[:], et1[:], AF.Exp),
                                   [("dve", d_e1)], "act")
                        k_ev = P.op("dve", lambda e, pkh=pkh, half=half, tc=tc: e.scalar_tensor_tensor(
                            ktm[:, tc * D + half * 512: tc * D + (half + 1) * 512],
                            pkh[:, 0:512], 0.0, et2[:], ALU.max, ALU.add),
                            [("act", a_e)], "dve")
                        prel[par] = ("dve", k_ev)
                    else:
                        dst = vA[:, tc * NHD * 65:(tc + 1) * NHD * 65].rearrange(
                            "p (h c) -> p h c", c=65)[:, half * 8:(half + 1) * 8, 0:64]
                        v_ev = P.op("act", lambda e, pkh=pkh, dst=dst: e.activation(
                            dst, pkh[:, 0:512].rearrange("p (h c) -> p h c", c=64),
                            AF.Copy), [("pe", mmv)], "act")
                        prel[par] = ("act", v_ev)
        for h in range(NHD):
            bank = (pk0, pk1, pk2, pk3)[h // 4]
            reg = (h % 4) * 65
            for tc in range(TC):
                first = kv_first[h // 4]
                kv_first[h // 4] = False
                last = (i == ntiles - 1) and (tc == TC - 1)
                kv_mm_last = P.op("pe", lambda e, h=h, tc=tc, bank=bank, reg=reg,
                                  first=first, last=last: e.matmul(
                    bank[0:64, reg:reg + 65],
                    ktm[:, tc * D + h * 64: tc * D + h * 64 + 64],
                    vA[:, tc * NHD * 65 + h * 65: tc * NHD * 65 + (h + 1) * 65],
                    start=first, stop=last, skip_group_check=True),
                    [("dve", k_ev), ("act", v_ev)] +
                    ([bank_rel["m"], bank_rel["r"], bank_rel["sum"], bank_rel["sq"]]
                     if first else []),
                    "pe")
        ktm_rel = ("pe", kv_mm_last)
        vA_rel = ("pe", kv_mm_last)

    a_kv = 0
    for b in range(4):
        bank = (pk0, pk1, pk2, pk3)[b]
        for rw in (0, 64):
            a_kv = P.op("act", lambda e, b=b, bank=bank, rw=rw: e.activation(
                KVA[rw:rw + 64, b * 260:(b + 1) * 260], bank[0:64, 0:260], AF.Copy),
                [("pe", kv_mm_last)], "act")

    w_q = P.dma("pool", WS[0][:].rearrange("p (k f) -> p k f", k=KC),
                wrearr(Wq), [ktm_rel, ("dve", P.cnt["dve"])], sem="cast_k")
    w_g = P.dma("pool", WS[1][:].rearrange("p (k f) -> p k f", k=KC),
                wrearr(Wg), [vA_rel, ("act", P.cnt["act"])], sem="cast_v")
    orel = [("dve", 0), ("dve", 0)]
    grel = ("act", 0)
    hrel = [("dve", 0), ("dve", 0)]
    qrel = [("dve", 0), ("dve", 0)]
    bfrel = [("act", 0), ("act", 0)]
    bft = [0]
    for i in range(ntiles):
        t0 = i * tile
        q_ev = None
        for fc in range(KC):
            pkq = pk2 if fc % 2 == 0 else pk3
            mmq = None
            for k in range(KC):
                mmq = P.op("pe", lambda e, k=k, fc=fc, t0=t0, pkq=pkq: e.matmul(
                    pkq[:, 0:tile],
                    WS[0][:, k * D + fc * 128: k * D + (fc + 1) * 128],
                    xn3[:, k * T + t0: k * T + t0 + tile],
                    start=(k == 0), stop=(k == KC - 1)),
                    [("cast_k", w_q), ("dve", d_xn_last)] +
                    ([qrel[fc % 2], ("act", a_kv)] if k == 0 else []),
                    "pe" if k == KC - 1 else None)
            d_e1 = P.op("dve", lambda e, pkq=pkq: e.tensor_scalar(
                et1[:, 0:tile], pkq[:, 0:tile], 0.0, None, ALU.min, ALU.bypass),
                [("pe", mmq)], "dve")
            a_e = P.op("act", lambda e: e.activation(et2[:, 0:tile], et1[:, 0:tile],
                                                     AF.Exp), [("dve", d_e1)], "act")
            q_ev = P.op("dve", lambda e, pkq=pkq, fc=fc: e.scalar_tensor_tensor(
                qb[:, fc * tile:(fc + 1) * tile], pkq[:, 0:tile], 0.0, et2[:, 0:tile],
                ALU.max, ALU.add), [("act", a_e)], "dve")
            qrel[fc % 2] = ("dve", q_ev)
        g_ev = None
        for half in range(2):
            for tc in range(TC):
                mmg = None
                for k in range(KC):
                    mmg = P.op("pe", lambda e, k=k, half=half, tc=tc, t0=t0: e.matmul(
                        pk4[:, 0:512],
                        xn3[:, k * T + t0 + tc * 128: k * T + t0 + tc * 128 + 128],
                        WS[1][:, k * D + half * 512: k * D + half * 512 + 512],
                        start=(k == 0), stop=(k == KC - 1)),
                        [("cast_v", w_g), ("dve", d_xn_last)] +
                        ([grel, hrel[0]] if k == 0 else []),
                        "pe" if k == KC - 1 else None)
                d_g1 = P.op("dve", lambda e, half=half: e.tensor_tensor(
                    et1[:], pk4[:, 0:512], bgrep[:, half * 512:(half + 1) * 512],
                    ALU.add), [("pe", mmg), ("act", a_bgrep)], "dve")
                g_ev = P.op("act", lambda e, half=half, tc=tc: e.activation(
                    gb[:, tc * D + half * 512: tc * D + (half + 1) * 512], et1[:],
                    AF.Sigmoid), [("dve", d_g1)], "act")
                grel = ("act", g_ev)
        d_o_last = 0
        o2_pe_rel = P.cnt["pe"] if i == 0 else o2_pe_rel
        for h in range(NHD):
            par = h % 2
            pkh = pk4 if par == 0 else pk7
            fc2 = h // 2
            r0 = (h % 2) * 64
            for tc in range(TC):
                mmo = P.op("pe", lambda e, h=h, tc=tc, pkh=pkh, fc2=fc2, r0=r0,
                           t0=t0: e.matmul(
                    pkh[:, 0:65],
                    qb[r0:r0 + 64, fc2 * tile + tc * 128: fc2 * tile + tc * 128 + 128],
                    KVA[r0:r0 + 64, h * 65:(h + 1) * 65],
                    start=True, stop=True),
                    [("dve", q_ev), ("act", a_kv), hrel[par],
                     (grel if (h == 0 and tc == 0) else ("act", 0))], "pe")
                d_z = P.op("dve", lambda e, pkh=pkh: e.tensor_scalar(
                    zrec[:, 0:1], pkh[:, 64:65], 8.0, 8e-6, ALU.mult, ALU.max),
                    [("pe", mmo)], "dve")
                d_zr = P.op("dve", lambda e: e.reciprocal(zrec[:, 1:2], zrec[:, 0:1]),
                            [("dve", d_z)], "dve")
                d_o = P.op("dve", lambda e, pkh=pkh, h=h, tc=tc: e.scalar_tensor_tensor(
                    o2[:, tc * D + h * 64: tc * D + (h + 1) * 64],
                    pkh[:, 0:64], zrec[:, 1:2],
                    gb[:, tc * D + h * 64: tc * D + (h + 1) * 64],
                    ALU.mult, ALU.mult),
                    [("dve", d_zr), ("act", g_ev), ("pe", o2_pe_rel)], "dve")
                hrel[par] = ("dve", d_o)
                d_o_last = d_o
        to_ev = 0
        for tc in range(TC):
            for fc2 in range(KC):
                bt = bft[0] % 2
                bft[0] += 1
                pkb = pk5 if bt == 0 else pk6
                pe_t = P.op("pe", lambda e, tc=tc, fc2=fc2, pkb=pkb: e.transpose(
                    pkb[:, 0:128], o2[:, tc * D + fc2 * 128: tc * D + (fc2 + 1) * 128],
                    sidb[:]), [("dve", d_o_last), bfrel[bt]], "pe")
                to_ev = P.op("act", lambda e, tc=tc, fc2=fc2, pkb=pkb: e.activation(
                    o2T[:, fc2 * tile + tc * 128: fc2 * tile + tc * 128 + 128],
                    pkb[:, 0:128], AF.Copy), [("pe", pe_t)], "act")
                bfrel[bt] = ("act", to_ev)
        o2_pe_rel = 0
        for dm in range(KC):
            pko = pk0 if dm % 2 == 0 else pk1
            mmw = None
            for fc2 in range(KC):
                mmw = P.op("pe", lambda e, dm=dm, fc2=fc2, pko=pko: e.matmul(
                    pko[:, 0:tile],
                    WS[2][:, fc2 * D + dm * 128: fc2 * D + (dm + 1) * 128],
                    o2T[:, fc2 * tile:(fc2 + 1) * tile],
                    start=(fc2 == 0), stop=(fc2 == KC - 1)),
                    [("cast_o", w_o), ("act", to_ev)] +
                    ([orel[dm % 2], ("act", a_kv)] if fc2 == 0 else []),
                    "pe" if fc2 == KC - 1 else None)
            dv = P.op("dve", lambda e, dm=dm, pko=pko, t0=t0: e.tensor_tensor(
                xr[:, dm * T + t0: dm * T + t0 + tile],
                xr[:, dm * T + t0: dm * T + t0 + tile],
                pko[:, 0:tile], ALU.add), [("pe", mmw)], "dve")
            orel[dm % 2] = ("dve", dv)
            o2_pe_rel = mmw

    return



def emit_phase_D(P, sh, tile=512):
    ntiles = T // tile
    HID = 3072
    NFC = HID // 128
    xr = sh["xr"]; sid = sh["sid"]; sidb = sh["sidb"]
    onesb = sh["onesb"]; ones1b = sh["ones1b"]

    wg = P.dram("D_wg", [HID, D], I32, kind="ExternalInput")
    wu = P.dram("D_wu", [HID, D], I32, kind="ExternalInput")
    wd = P.dram("D_wd", [D, HID], I32, kind="ExternalInput")

    wgts = P.dram("D_wgt_scr", [NFC, 128, 1024], BF16)
    wuts = P.dram("D_wut_scr", [NFC, 128, 1024], BF16)
    wdt = P.sbuf("wdt", [128, NFC * D], BF16)
    wst = [P.sbuf(f"wst{i}", [128, 1024], BF16) for i in range(2)]
    wgl = [P.sbuf(f"wgl{i}", [128, 1024], BF16) for i in range(3)]
    wul = [P.sbuf(f"wul{i}", [128, 1024], BF16) for i in range(3)]
    xn4 = P.sbuf("xn4", [128, KC * tile], BF16)
    wsl = [P.sbuf(f"wsl{i}", [128, KC * 128], BF16) for i in range(3)]
    gball = P.sbuf("gball", [128, tile], BF16)
    uball = P.sbuf("uball", [128, tile], BF16)
    hball = P.sbuf("hball", [128, NFC * tile], BF16)
    xqb = [P.sbuf(f"xqb{i}", [128, tile], BF16) for i in range(4)]
    statr = P.sbuf("statr", [1, 5 * tile], F32)
    statb = P.sbuf("statb", [1, 4 * tile], BF16)

    pk0 = P.psum("pk0", [128, 512], F32)
    pk1 = P.psum("pk1", [128, 512], F32)
    pk2 = P.psum("pk2", [128, 512], F32)
    pk3 = P.psum("pk3", [128, 512], F32)
    pk4 = P.psum("pk4", [128, 512], F32)
    pk7 = P.psum("pk7", [128, 512], F32)
    pk5 = P.psum("pk5", [128, 1024], BF16)
    pk6 = P.psum("pk6", [128, 1024], BF16)

    rb_w = RB(3)
    bfrel = [("act", 0), ("act", 0)]
    bft = [0]
    wst_rel = [("act", 0), ("act", 0)]
    for wi, wsrc in ((0, wg), (1, wu), (2, wd)):
        for fc in range(NFC):
            kw, ww = rb_w.acquire()
            if wi < 2:
                ap = wsrc[fc * 128:(fc + 1) * 128, :].rearrange("p (k f) -> p k f", k=KC)
            else:
                ap = wsrc.rearrange("(k p) f -> p k f", p=128)[:, :, fc * 128:(fc + 1) * 128]
            w_sl = P.dma("pool", wsl[kw][:].rearrange("p (k f) -> p k f", k=KC), ap,
                         ww, sem="dwc%d" % kw)
            lastpe = None
            ev_last = {"act": ("act", 0), "dve": ("dve", 0)}
            for k in range(KC):
                bt = bft[0] % 2
                bft[0] += 1
                pkb = pk5 if bt == 0 else pk6
                pe_t = P.op("pe", lambda e, kw=kw, k=k, pkb=pkb: e.transpose(
                    pkb[:, 0:128], wsl[kw][:, k * 128:(k + 1) * 128], sidb[:]),
                    [("dwc%d" % kw, w_sl), bfrel[bt]], "pe")
                lastpe = pe_t
                eng = "act" if k % 2 == 0 else "dve"
                if wi < 2:
                    dst = wst[fc % 2][:, k * 128:(k + 1) * 128]
                else:
                    dst = wdt[:, fc * D + k * 128: fc * D + (k + 1) * 128]
                if eng == "act":
                    fn = lambda e, dst=dst, pkb=pkb: e.activation(
                        dst, pkb[:, 0:128], AF.Copy)
                else:
                    fn = lambda e, dst=dst, pkb=pkb: e.tensor_copy(
                        dst, pkb[:, 0:128])
                av_t = (eng, P.op(eng, fn, [("pe", pe_t)] +
                        ([wst_rel[fc % 2]] if (wi < 2 and k <= 1) else []),
                        eng))
                bfrel[bt] = av_t
                ev_last[eng] = av_t
            if wi < 2:
                scr = wgts if wi == 0 else wuts
                sm = "d_wst%d" % (fc % 2)
                wst_rel[fc % 2] = (sm, P.dma(
                    "sp", scr[fc], wst[fc % 2][:],
                    [ev_last["act"], ev_last["dve"]], sem=sm))
            rb_w.release(kw, "pe", lastpe)
    wt_done_l = [("act", P.cnt["act"]), ("dve", P.cnt["dve"])]

    rb_xq = RB(4)
    rb_wg = RB(3)
    rb_wu = RB(3)
    wst_done = [("d_wst0", P.cnt.get("d_wst0", 0)),
                ("d_wst1", P.cnt.get("d_wst1", 0))]
    bank_rel = {"sum": ("dve", 0), "sq": ("dve", 0), "m": ("dve", 0),
                "r": ("dve", 0)}
    orel = [("dve", 0)] * 2
    grel = [("act", 0)] * 2
    urel = [("act", 0)] * 2
    h_mm_prev = 0
    s2 = 1.0 / (127.0 * 127.0)

    for i in range(ntiles):
        t0 = i * tile
        repv = emit_ln_stats_x(P, xr, tile, xqb, rb_xq, statr, statb, onesb,
                               ones1b, pk0, pk1, pk2[:, 0:tile], pk3[:, 0:tile],
                               bank_rel,
                               extra_waits=[orel[0], orel[1], urel[0], urel[1]],
                               xoff=t0, xstride=T)
        d_xn_last = 0
        for c in range(KC):
            xc = xr[:, c * T + t0: c * T + t0 + tile]
            P.op("dve", lambda e, x=xc, c=c: e.tensor_tensor(
                xn4[:, c * tile:(c + 1) * tile], x, pk2[:, 0:tile], ALU.subtract),
                [("pe", repv["m"]), ("pe", h_mm_prev), repv["dtail"]], "dve")
            d_xn_last = P.op("dve", lambda e, c=c: e.tensor_tensor(
                xn4[:, c * tile:(c + 1) * tile], xn4[:, c * tile:(c + 1) * tile],
                pk3[:, 0:tile], ALU.mult), [("pe", repv["r"])], "dve")
        bank_rel["m"] = ("dve", d_xn_last)
        bank_rel["r"] = ("dve", d_xn_last)

        d_h_last = 0
        for fc in range(NFC):
            pkg = pk4 if fc % 2 == 0 else pk7
            pku = pk2 if fc % 2 == 0 else pk3
            kwg, wwg = rb_wg.acquire()
            w_g = P.dma("sp", wgl[kwg][:], wgts[fc],
                        wwg + wst_done, sem="wgd%d" % kwg)
            kwu, wwu = rb_wu.acquire()
            w_u = P.dma("sp", wul[kwu][:], wuts[fc],
                        wwu + wst_done, sem="wud%d" % kwu)
            mmg = mmu = None
            for k in range(KC):
                mmg = P.op("pe", lambda e, k=k, fc=fc, pkg=pkg, kwg=kwg: e.matmul(
                    pkg[:, 0:tile], wgl[kwg][:, k * 128:(k + 1) * 128],
                    xn4[:, k * tile:(k + 1) * tile],
                    start=(k == 0), stop=(k == KC - 1)),
                    wt_done_l + [("dve", d_xn_last), ("wgd%d" % kwg, w_g)] +
                    ([grel[fc % 2]] if k == 0 else []),
                    "pe" if k == KC - 1 else None)
            rb_wg.release(kwg, "pe", mmg)
            for k in range(KC):
                mmu = P.op("pe", lambda e, k=k, fc=fc, pku=pku, kwu=kwu: e.matmul(
                    pku[:, 0:tile], wul[kwu][:, k * 128:(k + 1) * 128],
                    xn4[:, k * tile:(k + 1) * tile],
                    start=(k == 0), stop=(k == KC - 1)),
                    wt_done_l + [("dve", d_xn_last), ("wud%d" % kwu, w_u)] +
                    ([urel[fc % 2], bank_rel["m"], bank_rel["r"]] if k == 0 else []),
                    "pe" if k == KC - 1 else None)
            rb_wu.release(kwu, "pe", mmu)
            a_g = P.op("act", lambda e, pkg=pkg: e.activation(
                gball[:], pkg[:, 0:tile], AF.Silu, scale=1.0 / 127.0),
                [("pe", mmg), ("dve", d_h_last)], "act")
            grel[fc % 2] = ("act", a_g)
            a_u = P.op("act", lambda e, pku=pku: e.activation(
                uball[:], pku[:, 0:tile], AF.Copy),
                [("pe", mmu), ("dve", d_h_last)], "act")
            urel[fc % 2] = ("act", a_u)
            d_h_last = P.op("dve", lambda e, fc=fc: e.tensor_tensor(
                hball[:, fc * tile:(fc + 1) * tile], gball[:], uball[:], ALU.mult),
                [("act", a_g), ("act", a_u)], "dve")
        h_mm_prev = P.cnt["pe"]

        dv = None
        for dm in range(KC):
            pko = pk0 if dm % 2 == 0 else pk1
            mmo = None
            for fc in range(NFC):
                mmo = P.op("pe", lambda e, fc=fc, dm=dm, pko=pko: e.matmul(
                    pko[:, 0:tile], wdt[:, fc * D + dm * 128: fc * D + (dm + 1) * 128],
                    hball[:, fc * tile:(fc + 1) * tile],
                    start=(fc == 0), stop=(fc == NFC - 1)),
                    [("dve", d_h_last)] + ([orel[dm % 2]] if fc == 0 else []),
                    "pe" if fc == NFC - 1 else None)
            dv = P.op("dve", lambda e, dm=dm, pko=pko, t0=t0: e.scalar_tensor_tensor(
                xr[:, dm * T + t0: dm * T + t0 + tile], pko[:, 0:tile], s2,
                xr[:, dm * T + t0: dm * T + t0 + tile], ALU.mult, ALU.add),
                [("pe", mmo)], "dve")
            orel[dm % 2] = ("dve", dv)

    return



def build_fused(sim_safe=True, relax=("pe", "act", "dve", "pool"), phases="ABCD"):
    P = PB(sim_safe=sim_safe, relax=relax)
    sr = globals().get("_STRICT_RNG")
    if sr:
        P.strict_rng[sr[0]] = (sr[1], sr[2])
    x_in = P.dram("x_in", [D, T], F32, kind="ExternalInput")
    identf = P.dram("identf", [128, 128], F32, kind="ExternalInput")
    x_out = P.dram("x_out", [D, T], F32, kind="ExternalOutput")

    xr = P.sbuf("xr", [128, KC * T], F32)
    sid = P.sbuf("sid", [128, 128], F32)
    sidb = P.sbuf("sidb", [128, 128], BF16)
    onesb = P.sbuf("onesb", [128, 1], BF16)
    ones1b = P.sbuf("ones1b", [1, 128], BF16)

    P.dma("sp", xr[:].rearrange("p (c t) -> p c t", c=KC),
          x_in.rearrange("(c p) t -> p c t", p=128), sem="init0")
    P.dma("sp", sid[:], identf, sem="init0")
    w0 = P.cnt["init0"]
    P.op("dve", lambda e: e.tensor_copy(sidb[:], sid[:]), [("init0", w0)])
    P.op("dve", lambda e: e.memset(onesb[:], 1.0))
    d0 = P.op("dve", lambda e: e.memset(ones1b[:], 1.0), inc="dve")
    fw = [("init0", w0), ("dve", d0)]
    P.op("pe", lambda e: e.nop(), fw)
    P.op("act", lambda e: e.nop(), fw)
    P.op("dve", lambda e: e.engine_nop(), fw)
    P.op("pool", lambda e: e.engine_nop(), fw)
    P.op("sp", lambda e: e.nop(), fw)

    sh = dict(xr=xr, sid=sid, sidb=sidb, onesb=onesb, ones1b=ones1b)
    all_phases = (("pA_", emit_phase_A), ("pB_", emit_phase_B),
                  ("pC_", emit_phase_C), ("pD_", emit_phase_D))
    sel = phases
    for pfx, emit in all_phases:
        if pfx[1] not in sel:
            continue
        P.push_scope()
        P.name_prefix = pfx
        emit(P, sh)
        P.name_prefix = ""
        P.fence()
        P.pop_scope()

    fin = P.dma("sp", x_out.rearrange("(c p) t -> p c t", p=128),
                xr[:].rearrange("p (c t) -> p c t", c=KC), sem="fin")
    P.op("sp", lambda e: e.nop(), [("fin", fin)])
    return P


def _prep_inputs(inputs):
    f32 = np.float32
    w_in = np.asarray(inputs["L0_attn_w_in"], f32)
    pad = np.zeros((D, 96), f32)
    pad64 = np.zeros((D, 64), f32)
    pad128 = np.zeros((D, 128), f32)
    wA = np.ascontiguousarray(np.concatenate(
        [w_in[:, 0:4096], w_in[:, 4096:4128], pad,
         w_in[:, 4128:4192], pad64, w_in[:, 4192:4256], pad64, pad128], axis=1))
    assert wA.shape == (D, PROJ_PAD)
    s = np.arange(128)[:, None]
    t = np.arange(128)[None, :]
    seli = np.zeros((128, NH * 128), f32)
    for h in range(NH):
        seli[h, h * 128:(h + 1) * 128] = 1.0
    com = dict(
        A_wA=wA,
        A_convT=np.ascontiguousarray(np.asarray(inputs["L0_attn_conv_weight"], f32).T),
        A_cbT=np.ascontiguousarray(np.asarray(inputs["L0_attn_conv_bias"], f32).reshape(16, 128).T),
        A_nscT=np.ascontiguousarray(np.asarray(inputs["L0_attn_norm_scale"], f32).reshape(8, 128).T),
        A_alog=np.ascontiguousarray(np.asarray(inputs["L0_attn_A_log"], f32).reshape(NH, 1)),
        A_dtbias=np.ascontiguousarray(np.asarray(inputs["L0_attn_dt_bias"], f32).reshape(NH, 1)),
        A_wo_in=np.ascontiguousarray(np.asarray(inputs["L0_attn_w_out"], f32)),
        A_caus=(s <= t).astype(f32),
        A_seli=seli,
        B_W1=np.asarray(inputs["L0_ffn_W1"], f32),
        B_W2=np.asarray(inputs["L0_ffn_W2"], f32),
        B_b1T=np.ascontiguousarray(np.asarray(inputs["L0_ffn_b1"], f32).reshape(32, 128).T),
        B_b2T=np.ascontiguousarray(np.asarray(inputs["L0_ffn_b2"], f32).reshape(8, 128).T),
        C_Wq=np.asarray(inputs["L1_attn_Wq"], f32),
        C_Wk=np.asarray(inputs["L1_attn_Wk"], f32),
        C_Wv=np.asarray(inputs["L1_attn_Wv"], f32),
        C_Wg=np.asarray(inputs["L1_attn_Wg"], f32),
        C_Wo=np.asarray(inputs["L1_attn_Wo"], f32),
        C_bg=np.ascontiguousarray(np.asarray(inputs["L1_attn_bg"], f32).reshape(1, D)),
        D_wg=np.asarray(inputs["L1_ffn_w_gate"], np.int32),
        D_wu=np.asarray(inputs["L1_ffn_w_up"], np.int32),
        D_wd=np.asarray(inputs["L1_ffn_w_down"], np.int32),
        identf=np.eye(128, dtype=f32),
    )
    xs = np.asarray(inputs["L0_attn_x"], np.float32)
    maps = []
    for b in range(xs.shape[0]):
        m = dict(com)
        m["x_in"] = np.ascontiguousarray(xs[b].T)
        maps.append(m)
    return maps


def kernel(**inputs):
    maps = _prep_inputs(inputs)
    nc = build_fused().build()
    res = run_bass_kernel_spmd(nc, maps, core_ids=list(range(len(maps))))
    out = np.stack([np.ascontiguousarray(r["x_out"].T) for r in res.results])
    return out.astype(np.float32)
